# version 11
# speedup vs baseline: 1.3196x; 1.0039x over previous
import math
import contextlib
import numpy as np

import concourse.bass as bass
import concourse.mybir as mybir
from concourse.bass_utils import run_bass_kernel_spmd

F32 = mybir.dt.float32
BF16 = mybir.dt.bfloat16
I32 = mybir.dt.int32
AF = mybir.ActivationFunctionType
ALU = mybir.AluOpType

NCORE = 8
S = 4096
D = 1024
NCH = 8
TT = 512
NTILE = S // TT
H0, DH0 = 16, 64
FH, WIN, INNER = 8, 512, 512
HID = 2730
W1OUT = 5460
NJ = 22
CHJ = [128] * 21 + [42]
SCALE2S = 2.0 / 4096.0
NEG_BIG = -1.0e38

DEBUG_WANT = []
DEBUG_TAPS = []
BUILD_UNTIL = 99
FFN_SUBS = "AB12"
FFN_ASEC = "all"

ENGS = ("te", "ve", "se", "gp", "sy")
EATTR = {"te": "tensor", "ve": "vector", "se": "scalar", "gp": "gpsimd", "sy": "sync"}


class P:
    def __init__(self, nc, es):
        self.nc = nc
        self.q = {e: [] for e in ENGS}
        self.cnt = {}
        self.sem = {}
        self.last_wait = {}
        for k in ("te", "ve", "se", "gp", "d_sy", "d_gp", "d_se", "d_ve", "cc"):
            self.sem[k] = es.enter_context(nc.semaphore(k))
            self.cnt[k] = 0

    def op(self, eng, meth, *a, inc=True, **kw):
        if inc:
            self.cnt[eng] += 1
        def f(E, meth=meth, a=a, kw=kw, inc=inc, sem=self.sem[eng]):
            r = getattr(E, meth)(*a, **kw)
            if inc:
                r.then_inc(sem, 1)
        self.q[eng].append(f)
        return self.cnt[eng]

    def dma(self, qeng, out, in_, **kw):
        key = "d_" + qeng
        self.cnt[key] += 16
        def f(E, out=out, in_=in_, kw=kw, sem=self.sem[key]):
            E.dma_start(out=out, in_=in_, **kw).then_inc(sem, 16)
        self.q[qeng].append(f)
        return self.cnt[key]

    def ccop(self, *a, **kw):
        self.cnt["cc"] += 1
        def f(E, a=a, kw=kw, sem=self.sem["cc"]):
            E.collective_compute(*a, **kw).then_inc(sem, 1)
        self.q["gp"].append(f)
        return self.cnt["cc"]

    def wait(self, eng, key, val):
        if val <= 0:
            return
        if val <= self.last_wait.get((eng, key), -1):
            return
        self.last_wait[(eng, key)] = val
        def f(E, sem=self.sem[key], val=val):
            E.wait_ge(sem, val)
        self.q[eng].append(f)

    def vsync(self, eng="ve"):
        self.wait(eng, eng, self.cnt[eng])

    def wait_compute(self, eng):
        for k in ("te", "ve", "se", "gp"):
            self.wait(eng, k, self.cnt[k])

    def wait_all(self, eng):
        for k in self.sem:
            if k != "cc":
                self.wait(eng, k, self.cnt[k])

    def barrier(self):
        for e in ENGS:
            self.wait_all(e)

    def finalize(self, blk):
        for e in ENGS:
            fns = self.q[e]
            def runner(E, fns=fns):
                for f in fns:
                    f(E)
            getattr(blk, EATTR[e])(runner)


def _host_prep(I):
    f = np.float32
    hp = {}
    hp["l0_wkv"] = np.ascontiguousarray(
        np.concatenate([np.asarray(I["L0_attn_Wk"]), np.asarray(I["L0_attn_Wv"])], axis=2)
        .transpose(1, 0, 2).reshape(DH0, H0 * 128)).astype(f)
    hp["l0_wq"] = np.ascontiguousarray(
        np.asarray(I["L0_attn_Wq"]).transpose(1, 0, 2).reshape(DH0, H0 * DH0)).astype(f)
    hp["l0_wo"] = np.ascontiguousarray(I["L0_attn_Wo"]).astype(f)
    hp["l0_bwT"] = np.ascontiguousarray(np.asarray(I["L0_ffn_base_w"]).T).astype(f)
    hp["l0_bb"] = np.ascontiguousarray(np.asarray(I["L0_ffn_base_b"]).reshape(NCH, 128).T).astype(f)
    hp["l0_rwT"] = np.ascontiguousarray(np.asarray(I["L0_ffn_router_w"]).T).astype(f)
    hp["l0_rb"] = np.ascontiguousarray(np.asarray(I["L0_ffn_router_b"]).reshape(8, 1)).astype(f)
    hp["l0_lA"] = np.ascontiguousarray(I["L0_ffn_lora_A"]).astype(f)
    hp["l0_lB"] = np.ascontiguousarray(np.asarray(I["L0_ffn_lora_B"]).reshape(64, D)).astype(f)
    for nm, k in [("l1_wkT", "L1_attn_Wk"), ("l1_wvT", "L1_attn_Wv"),
                  ("l1_wqT", "L1_attn_Wq"), ("l1_woutT", "L1_attn_Wout"),
                  ("l1_w0T", "L1_attn_W0"), ("l1_w1T", "L1_attn_W1")]:
        hp[nm] = np.ascontiguousarray(np.asarray(I[k]).T).astype(f)
    hp["l1_w1n"] = np.ascontiguousarray(I["L1_attn_W1"]).astype(f)
    hp["l1_wg"] = np.ascontiguousarray(
        np.concatenate([I["L1_attn_Wg_decay"], I["L1_attn_Wg_lr"]], axis=0).T).astype(f)
    hp["l1_bg"] = np.ascontiguousarray(
        np.concatenate([I["L1_attn_bg_decay"], I["L1_attn_bg_lr"]]).reshape(2, 1)).astype(f)
    hp["f_wqkv"] = np.ascontiguousarray(np.asarray(I["L1_ffn_wqkv"]).reshape(2 * D, 3 * INNER)).astype(f)
    hp["f_wo"] = np.ascontiguousarray(np.asarray(I["L1_ffn_wo"]).reshape(2 * INNER, D)).astype(f)
    hp["f_w1"] = np.ascontiguousarray(np.asarray(I["L1_ffn_w1"]).reshape(2 * D, W1OUT)).astype(f)
    hp["f_w2"] = np.ascontiguousarray(np.asarray(I["L1_ffn_w2"]).reshape(2 * HID, D)).astype(f)
    hp["f_qs"] = np.ascontiguousarray(np.tile(np.asarray(I["L1_ffn_q_scale"]), (1, 2)).T).astype(f)
    hp["f_ks"] = np.ascontiguousarray(np.tile(np.asarray(I["L1_ffn_k_scale"]), (1, 2)).T).astype(f)
    hp["f_lng"] = np.ascontiguousarray(np.concatenate(
        [np.asarray(I["L1_ffn_ln_g"])[l].reshape(NCH, 128).T for l in range(2)], axis=1)).astype(f)
    hp["f_lnb"] = np.ascontiguousarray(np.concatenate(
        [np.asarray(I["L1_ffn_ln_b"])[l].reshape(NCH, 128).T for l in range(2)], axis=1)).astype(f)
    b1 = np.zeros((128, 2 * 44), dtype=f)
    for l in range(2):
        ap_ = np.asarray(I["L1_ffn_b1"])[l][:HID]
        gp_ = np.asarray(I["L1_ffn_b1"])[l][HID:]
        for j in range(NJ):
            m = CHJ[j]
            b1[:m, l * 44 + j] = ap_[j * 128:j * 128 + m]
            b1[:m, l * 44 + 22 + j] = gp_[j * 128:j * 128 + m]
    hp["f_b1"] = np.ascontiguousarray(b1)
    hp["f_b2"] = np.ascontiguousarray(np.concatenate(
        [np.asarray(I["L1_ffn_b2"])[l].reshape(NCH, 128).T for l in range(2)], axis=1)).astype(f)
    hp["c_ident"] = np.eye(128, dtype=f)
    R = np.zeros((64, 64), dtype=f)
    R[0:32, 32:64] = -np.eye(32)
    R[32:64, 0:32] = np.eye(32)
    hp["c_rt"] = np.ascontiguousarray(np.kron(np.eye(2, dtype=f), R.T))
    h2 = np.zeros((128, 2), dtype=f)
    h2[0:64, 0] = 1.0
    h2[64:128, 1] = 1.0
    hp["c_h2"] = h2
    hp["c_b2t"] = np.ascontiguousarray(h2.T)
    g8 = np.zeros((8, 64), dtype=f)
    t8 = np.zeros((8, 64), dtype=f)
    for m in range(64):
        g8[m // 8, m] = 1.0
        t8[m % 8, m] = 1.0
    hp["c_g8"] = g8
    hp["c_t8"] = t8
    iq = np.arange(128)[:, None]
    jj = np.arange(128)[None, :]
    hp["c_tlo"] = np.where(jj < iq, NEG_BIG, 0.0).astype(f)
    hp["c_thi"] = np.where(jj > iq, NEG_BIG, 0.0).astype(f)
    hp["c_j32"] = np.ascontiguousarray((np.arange(128) % 32).reshape(128, 1)).astype(f)
    return hp


def build_nc(debug_want=(), build_until=None):
    global DEBUG_TAPS
    DEBUG_TAPS = []
    BU = build_until if build_until is not None else BUILD_UNTIL
    nc = bass.Bass("TRN2", target_bir_lowering=False)
    es = contextlib.ExitStack()
    hb = {}

    def din(name, shape, dt=F32):
        hb[name] = nc.dram_tensor(name, list(shape), dt, kind="ExternalInput")

    def dscr(name, shape, dt=F32):
        hb[name] = nc.dram_tensor(name, list(shape), dt)

    din("x_fm", (D, S))
    for nm, shp in [("l0_wkv", (64, 2048)), ("l0_wq", (64, 1024)), ("l0_wo", (D, D)),
                    ("l0_bwT", (D, D)), ("l0_bb", (128, 8)), ("l0_rwT", (D, 8)),
                    ("l0_rb", (8, 1)), ("l0_lA", (D, 8)), ("l0_lB", (64, D)),
                    ("l1_wkT", (D, D)), ("l1_wvT", (D, D)), ("l1_wqT", (D, D)),
                    ("l1_woutT", (D, D)), ("l1_w0T", (D, D)), ("l1_w1T", (D, D)),
                    ("l1_w1n", (D, D)), ("l1_wg", (D, 2)), ("l1_bg", (2, 1)),
                    ("f_wqkv", (2 * D, 3 * INNER)), ("f_wo", (2 * INNER, D)),
                    ("f_w1", (2 * D, W1OUT)), ("f_w2", (2 * HID, D)),
                    ("f_qs", (128, 2)), ("f_ks", (128, 2)),
                    ("f_lng", (128, 16)), ("f_lnb", (128, 16)),
                    ("f_b1", (128, 88)), ("f_b2", (128, 16)),
                    ("c_ident", (128, 128)), ("c_rt", (128, 128)),
                    ("c_h2", (128, 2)), ("c_b2t", (2, 128)),
                    ("c_g8", (8, 64)), ("c_t8", (8, 64)),
                    ("c_tlo", (128, 128)), ("c_thi", (128, 128)),
                    ("c_j32", (128, 1))]:
        din(nm, shp)
    out_fm = nc.dram_tensor("out_fm", [D, S], F32, kind="ExternalOutput")

    dscr("x1h", (D, S))
    dscr("xn1h", (D, S), BF16)
    dscr("kfmh", (D, S), BF16)
    dscr("vfmh", (D, S), BF16)
    dscr("a1tmh", (S, D), BF16)
    dscr("d2tmh", (S, D), BF16)
    dscr("d1tmh", (S, D), BF16)
    dscr("ktmh", (S, D), BF16)
    dscr("tabC", (128, S))
    dscr("tabS", (128, S))
    dscr("tabCi", (128, S))
    dscr("tabSi", (128, S))
    dscr("yAh", (D, S))
    dscr("yBh", (D, S))
    dscr("y2h", (D, S))
    dscr("acth", (NJ * 128, S), BF16)
    dscr("cc_in", (2050, D))
    dscr("cc_out", (2050, D))

    taps = {}
    def tap(name, rows, cols, dt=F32):
        if name in taps:
            return taps[name]
        t = nc.dram_tensor("tap_" + name, [rows, cols], dt, kind="ExternalOutput")
        taps[name] = t
        DEBUG_TAPS.append("tap_" + name)
        return t

    p = P(nc, es)

    def sbuf(name, part, free, dt):
        return es.enter_context(nc.sbuf_tensor(name, [part, free], dt))

    ones_b = sbuf("ones_b", 128, 1, BF16)
    ones_f = sbuf("ones_f", 128, 1, F32)
    ones1f = sbuf("ones1f", 1, 128, F32)
    ident = sbuf("ident", 128, 128, F32)
    identb = sbuf("identb", 128, 128, BF16)
    cpool = sbuf("cpool", 128, 8, F32)
    C_EPSLN, C_EPSL0, C_IVF, C_LNS, C_LNB, C_MEAN = 0, 1, 2, 3, 4, 5
    gacc = sbuf("gacc", 2, 1, F32)

    p.op("gp", "memset", ones_b[:, :], 1.0)
    p.op("gp", "memset", ones_f[:, :], 1.0)
    p.op("gp", "memset", ones1f[:, :], 1.0)
    p.op("gp", "memset", cpool[:, C_EPSLN:C_EPSLN + 1], 1e-5)
    p.op("gp", "memset", cpool[:, C_EPSL0:C_EPSL0 + 1], 1e-6)
    p.op("gp", "memset", cpool[:, C_IVF:C_IVF + 1], float(-math.log(10000.0) / 32.0))
    p.op("gp", "memset", cpool[:, C_LNS:C_LNS + 1], float(2.0 / 89.6))
    p.op("gp", "memset", cpool[:, C_LNB:C_LNB + 1], float(25.6 / 89.6))
    p.op("gp", "memset", cpool[:, C_MEAN:C_MEAN + 1], float(1.0 / 32768.0))
    p.op("gp", "memset", gacc[:, :], 0.0)
    p.dma("sy", ident[:, :], hb["c_ident"][:, :])
    p.dma("gp", identb[:, :], hb["c_ident"][:, :])
    p.barrier()

    def load_fm_tile(qeng, dst, dstcols, hbmT, t0, width):
        for c in range(NCH):
            p.dma(qeng, dst[:, dstcols(c):dstcols(c) + width],
                  hb[hbmT][c * 128:(c + 1) * 128, t0:t0 + width])
        return p.cnt["d_" + qeng]

    def store_fm_tile(qeng, hbmT, src, srccols, t0, width):
        for c in range(NCH):
            p.dma(qeng, hb[hbmT][c * 128:(c + 1) * 128, t0:t0 + width],
                  src[:, srccols(c):srccols(c) + width])
        return p.cnt["d_" + qeng]

    def load_w(qeng, dst, hbm_name, K, row0=0, nrows=D):
        for c in range(nrows // 128):
            p.dma(qeng, dst[:, c * K:(c + 1) * K],
                  hb[hbm_name][row0 + c * 128:row0 + (c + 1) * 128, 0:K])
        return p.cnt["d_" + qeng]

    def emit_ln(src_c, dst_c, W, sqb, xbb, tmpf, srow, psS, psQ, psA, psB,
                lg_col=None, lb_col=None):
        for c in range(NCH):
            p.op("se", "activation", sqb[:, c * W:(c + 1) * W], src_c(c), AF.Square)
            p.op("se", "activation", xbb[:, c * W:(c + 1) * W], src_c(c), AF.Copy)
        p.wait("te", "se", p.cnt["se"])
        p.wait("te", "ve", p.cnt["ve"])
        for c in range(NCH):
            p.op("te", "matmul", psS[0:1, 0:W], ones_b[:, :], xbb[:, c * W:(c + 1) * W],
                 start=(c == 0), stop=(c == NCH - 1), inc=(c == NCH - 1))
        for c in range(NCH):
            p.op("te", "matmul", psQ[0:1, 0:W], ones_b[:, :], sqb[:, c * W:(c + 1) * W],
                 start=(c == 0), stop=(c == NCH - 1), inc=(c == NCH - 1))
        p.wait("ve", "te", p.cnt["te"])
        m_, q2 = srow[0:1, 0:W], srow[0:1, W:2 * W]
        a_, b_ = srow[0:1, 2 * W:3 * W], srow[0:1, 3 * W:4 * W]
        p.op("ve", "tensor_scalar", m_, psS[0:1, 0:W], 1.0 / D, None, ALU.mult)
        p.op("ve", "tensor_scalar", q2, psQ[0:1, 0:W], 1.0 / D, None, ALU.mult)
        p.op("ve", "tensor_tensor", a_, m_, m_, ALU.mult)
        p.op("ve", "tensor_tensor", q2, q2, a_, ALU.subtract)
        p.wait("se", "ve", p.cnt["ve"])
        p.op("se", "activation", a_, q2, AF.Sqrt, bias=cpool[0:1, C_EPSLN:C_EPSLN + 1])
        p.wait("ve", "se", p.cnt["se"])
        p.op("ve", "reciprocal", a_, a_)
        p.op("ve", "scalar_tensor_tensor", b_, m_, -1.0, a_, ALU.mult, ALU.mult)
        p.wait("te", "ve", p.cnt["ve"])
        p.op("te", "matmul", psA[:, 0:W], ones1f[0:1, :], a_, start=True, stop=True)
        p.op("te", "matmul", psB[:, 0:W], ones1f[0:1, :], b_, start=True, stop=True)
        p.wait("ve", "te", p.cnt["te"])
        for c in range(NCH):
            p.op("ve", "tensor_tensor", tmpf[:, 0:W], src_c(c), psA[:, 0:W], ALU.mult)
            if lg_col is None:
                p.op("ve", "tensor_tensor", dst_c(c), tmpf[:, 0:W], psB[:, 0:W], ALU.add)
            else:
                p.op("ve", "tensor_tensor", tmpf[:, 0:W], tmpf[:, 0:W], psB[:, 0:W], ALU.add)
                p.op("ve", "tensor_scalar", dst_c(c), tmpf[:, 0:W],
                     lg_col[:, c:c + 1], lb_col[:, c:c + 1], ALU.mult, ALU.add)
        return p.cnt["ve"]

    if BU >= 1:
      with contextlib.ExitStack() as ph:
        tA = ph.enter_context(nc.sbuf_tensor("tA", [128, S], F32))
        tB = ph.enter_context(nc.sbuf_tensor("tB", [128, S], F32))
        tC = ph.enter_context(nc.sbuf_tensor("tC", [128, S], F32))
        tD = ph.enter_context(nc.sbuf_tensor("tD", [128, S], F32))
        tI = ph.enter_context(nc.sbuf_tensor("tI", [128, S], I32))
        col = ph.enter_context(nc.sbuf_tensor("colv", [128, 4], F32))
        inv2pi = float(1.0 / (2.0 * math.pi))
        twopi = float(2.0 * math.pi)
        p.op("gp", "iota", tD[:, :], [[1, S]], channel_multiplier=0,
             allow_small_or_imprecise_dtypes=True)
        ldj = p.dma("sy", col[:, 0:1], hb["c_j32"][:, :])
        p.wait("se", "d_sy", ldj)
        p.wait("se", "gp", p.cnt["gp"])
        p.wait("ve", "gp", p.cnt["gp"])
        p.wait("ve", "d_sy", ldj)
        p.op("se", "activation", col[:, 1:2], col[:, 0:1], AF.Exp,
             scale=cpool[:, C_IVF:C_IVF + 1])
        p.op("se", "activation", col[:, 2:3], col[:, 0:1], AF.Ln,
             scale=cpool[:, C_LNS:C_LNS + 1], bias=cpool[:, C_LNB:C_LNB + 1])
        p.wait("ve", "se", p.cnt["se"])
        p.op("ve", "tensor_scalar", col[:, 2:3], col[:, 2:3], 1.0 / 256.0, None, ALU.mult)
        p.op("ve", "tensor_scalar", tA[:, :], tD[:, :], 2048.0, None, ALU.subtract)
        p.op("ve", "tensor_scalar", tA[:, :], tA[:, :], col[:, 2:3], None, ALU.mult)
        p.wait("se", "ve", p.cnt["ve"])
        p.op("se", "activation", tA[:, :], tA[:, :], AF.Exp)
        p.op("se", "activation", tD[:, :], tD[:, :], AF.Copy, scale=col[:, 1:2])
        p.wait("ve", "se", p.cnt["se"])

        def reduced_sin(dst, shift):
            if shift:
                p.op("ve", "tensor_scalar", tB[:, :], tD[:, :], float(shift), inv2pi,
                     ALU.add, ALU.mult)
            else:
                p.op("ve", "tensor_scalar", tB[:, :], tD[:, :], inv2pi, None, ALU.mult)
            p.op("ve", "tensor_copy", tI[:, :], tB[:, :])
            p.op("ve", "tensor_copy", tB[:, :], tI[:, :])
            p.op("ve", "scalar_tensor_tensor", tB[:, :], tB[:, :], -twopi, tD[:, :],
                 ALU.mult, ALU.add)
            if shift:
                p.op("ve", "tensor_scalar", tB[:, :], tB[:, :], float(shift), None,
                     ALU.add)
            p.op("ve", "tensor_scalar", tB[:, :], tB[:, :], float(-np.pi), float(np.pi),
                 ALU.max, ALU.min)
            p.wait("se", "ve", p.cnt["ve"])
            p.op("se", "activation", dst, tB[:, :], AF.Sin)
            p.wait("ve", "se", p.cnt["se"])

        reduced_sin(tC[:, :], 0.0)
        p.op("ve", "tensor_tensor", tB[:, :], tC[:, :], tA[:, :], ALU.mult)
        p.wait("gp", "ve", p.cnt["ve"])
        st = p.dma("gp", hb["tabS"][:, :], tB[:, :])
        p.wait("ve", "d_gp", st)
        p.op("ve", "reciprocal", tB[:, :], tA[:, :])
        p.op("ve", "tensor_tensor", tC[:, :], tC[:, :], tB[:, :], ALU.mult)
        p.wait("gp", "ve", p.cnt["ve"])
        st = p.dma("gp", hb["tabSi"][:, :], tC[:, :])
        p.wait("ve", "d_gp", st)
        reduced_sin(tC[:, :], math.pi / 2.0)
        p.op("ve", "tensor_tensor", tB[:, :], tC[:, :], tA[:, :], ALU.mult)
        p.wait("gp", "ve", p.cnt["ve"])
        st = p.dma("gp", hb["tabC"][:, :], tB[:, :])
        p.wait("ve", "d_gp", st)
        p.op("ve", "reciprocal", tB[:, :], tA[:, :])
        p.op("ve", "tensor_tensor", tC[:, :], tC[:, :], tB[:, :], ALU.mult)
        p.wait("gp", "ve", p.cnt["ve"])
        p.dma("gp", hb["tabCi"][:, :], tC[:, :])
        p.barrier()

    if BU >= 2:
      with contextlib.ExitStack() as ph0:
        xn0 = ph0.enter_context(nc.sbuf_tensor("xn0", [128, NCH * S], BF16))
        kv_sb = ph0.enter_context(nc.sbuf_tensor("kv_sb", [64, H0 * 64], BF16))
        z_col = ph0.enter_context(nc.sbuf_tensor("z_col", [64, H0], BF16))

        with contextlib.ExitStack() as ph:
            xt = [ph.enter_context(nc.sbuf_tensor(f"p0x{i}", [128, NCH * TT], F32))
                  for i in range(2)]
            sqb = ph.enter_context(nc.sbuf_tensor("p0sq", [128, NCH * TT], BF16))
            xbb = ph.enter_context(nc.sbuf_tensor("p0xb", [128, NCH * TT], BF16))
            tmpf = ph.enter_context(nc.sbuf_tensor("p0tmp", [128, TT], F32))
            srow = ph.enter_context(nc.sbuf_tensor("p0srow", [1, 4 * TT], F32))
            psS = ph.enter_context(nc.psum_tensor("p0psS", [1, TT], F32))
            psQ = ph.enter_context(nc.psum_tensor("p0psQ", [1, TT], F32))
            psA = ph.enter_context(nc.psum_tensor("p0psA", [128, TT], F32))
            psB = ph.enter_context(nc.psum_tensor("p0psB", [128, TT], F32))
            lds = [load_fm_tile("sy", xt[0], lambda c: c * TT, "x_fm", 0, TT)]
            for t in range(NTILE):
                if t + 1 < NTILE:
                    p.wait_compute("sy")
                    lds.append(load_fm_tile("sy", xt[(t + 1) % 2], lambda c: c * TT,
                                            "x_fm", (t + 1) * TT, TT))
                x = xt[t % 2]
                p.wait("se", "d_sy", lds[t])
                p.wait("ve", "d_sy", lds[t])
                emit_ln(lambda c, x=x: x[:, c * TT:(c + 1) * TT],
                        lambda c, t=t: xn0[:, c * S + t * TT:c * S + t * TT + TT],
                        TT, sqb, xbb, tmpf, srow, psS, psQ, psA, psB)
            if "xn0" in debug_want:
                tt = tap("xn0", D, S, BF16)
                p.wait_compute("gp")
                for c in range(NCH):
                    p.dma("gp", tt[c * 128:(c + 1) * 128, :], xn0[:, c * S:(c + 1) * S])
            p.barrier()

        if BU >= 3:
          with contextlib.ExitStack() as ph:
            wkv = ph.enter_context(nc.sbuf_tensor("p1wkv", [128, H0 * 128], BF16))
            kf_fm = ph.enter_context(nc.sbuf_tensor("p1kf", [64, 2 * TT], BF16))
            v_fm = ph.enter_context(nc.sbuf_tensor("p1vf", [64, 2 * TT], BF16))
            kfp_tm = ph.enter_context(nc.sbuf_tensor("p1kfp", [128, 4 * 128], BF16))
            vp_tm = ph.enter_context(nc.sbuf_tensor("p1vp", [128, 4 * 128], BF16))
            tmf = ph.enter_context(nc.sbuf_tensor("p1tmf", [64, 2 * TT], F32))
            tme = ph.enter_context(nc.sbuf_tensor("p1tme", [64, 2 * TT], F32))
            zpart = ph.enter_context(nc.sbuf_tensor("p1zp", [64, 2], F32))
            zcf = ph.enter_context(nc.sbuf_tensor("p1zcf", [64, H0], F32))
            kv_acc = ph.enter_context(nc.sbuf_tensor("p1kvacc", [128, 1024], F32))
            kvb_st = ph.enter_context(nc.sbuf_tensor("p1kvst", [128, 1024], BF16))
            Pa = ph.enter_context(nc.psum_tensor("p1Pa", [64, TT], F32))
            Pb = ph.enter_context(nc.psum_tensor("p1Pb", [64, TT], F32))
            Pc = ph.enter_context(nc.psum_tensor("p1Pc", [64, TT], F32))
            Pd = ph.enter_context(nc.psum_tensor("p1Pd", [64, TT], F32))
            PT0 = ph.enter_context(nc.psum_tensor("p1PT0", [128, 128], BF16))
            PT1 = ph.enter_context(nc.psum_tensor("p1PT1", [128, 128], BF16))
            KVP = [ph.enter_context(nc.psum_tensor(f"p1KVP{i}", [128, 128], F32))
                   for i in range(2)]
            p.dma("gp", wkv[0:64, :], hb["l0_wkv"][:, :])
            ld = p.dma("gp", wkv[64:128, :], hb["l0_wkv"][:, :])
            p.wait("te", "d_gp", ld)
            p.op("gp", "memset", kv_acc[:, :], 0.0)
            p.op("gp", "memset", zcf[:, :], 0.0)
            p.wait("ve", "gp", p.cnt["gp"])
            prs = [(t, hp_) for t in range(NTILE) for hp_ in range(8)]
            phi_ve = {}

            def p1_prod(j):
                t, hp_ = prs[j]
                tc0 = t * TT
                if j == 0:
                    p.wait("te", "ve", p.cnt["ve"])
                    p.wait("te", "se", p.cnt["se"])
                else:
                    p.wait("te", "ve", phi_ve[j - 1])
                for (PP, h, kvoff) in ((Pa, 2 * hp_, 0), (Pb, 2 * hp_, 64),
                                       (Pc, 2 * hp_ + 1, 0), (Pd, 2 * hp_ + 1, 64)):
                    b = (h % 2) * 64
                    p.op("te", "matmul", PP[:, :],
                         wkv[b:b + 64, h * 128 + kvoff:h * 128 + kvoff + 64],
                         xn0[b:b + 64, (h // 2) * S + tc0:(h // 2) * S + tc0 + TT],
                         start=True, stop=True, inc=True)
                return p.cnt["te"]

            prod_te = {0: p1_prod(0)}
            for j in range(len(prs)):
                t, hp_ = prs[j]
                h0, h1 = 2 * hp_, 2 * hp_ + 1
                p.wait("ve", "te", prod_te[j])
                p.op("ve", "tensor_scalar", tmf[:, 0:TT], Pa[:, :], 0.0, None, ALU.min)
                p.op("ve", "tensor_scalar", tmf[:, TT:2 * TT], Pc[:, :], 0.0, None,
                     ALU.min)
                p.op("ve", "tensor_copy", v_fm[:, 0:TT], Pb[:, :])
                p.op("ve", "tensor_copy", v_fm[:, TT:2 * TT], Pd[:, :])
                p.wait("se", "ve", p.cnt["ve"])
                p.op("se", "activation", tme[:, :], tmf[:, :], AF.Exp)
                p.wait("ve", "se", p.cnt["se"])
                p.op("ve", "scalar_tensor_tensor", kf_fm[:, 0:TT], Pa[:, :], 0.0,
                     tme[:, 0:TT], ALU.max, ALU.add)
                p.op("ve", "scalar_tensor_tensor", kf_fm[:, TT:2 * TT], Pc[:, :],
                     0.0, tme[:, TT:2 * TT], ALU.max, ALU.add)
                phi_ve[j] = p.cnt["ve"]
                if j + 1 < len(prs):
                    prod_te[j + 1] = p1_prod(j + 1)
                p.op("ve", "reduce_sum", zpart[:, 0:1], kf_fm[:, 0:TT],
                     axis=mybir.AxisListType.X)
                p.op("ve", "reduce_sum", zpart[:, 1:2], kf_fm[:, TT:2 * TT],
                     axis=mybir.AxisListType.X)
                p.vsync()
                p.op("ve", "tensor_tensor", zcf[:, h0:h0 + 1], zcf[:, h0:h0 + 1],
                     zpart[:, 0:1], ALU.add)
                p.op("ve", "tensor_tensor", zcf[:, h1:h1 + 1], zcf[:, h1:h1 + 1],
                     zpart[:, 1:2], ALU.add)
                for (src, dstt) in ((kf_fm, kfp_tm), (v_fm, vp_tm)):
                    for g in range(2):
                        for bb in range(4):
                            k_ = g * 4 + bb
                            PT = PT0 if k_ % 2 == 0 else PT1
                            eng = "ve" if k_ % 2 == 0 else "se"
                            p.wait("te", "ve", p.cnt["ve"])
                            p.wait("te", eng, p.cnt[eng])
                            p.op("te", "transpose", PT[:, 0:64],
                                 src[:, g * TT + bb * 128:g * TT + (bb + 1) * 128],
                                 identb[0:64, 0:64], inc=True)
                            p.wait(eng, "te", p.cnt["te"])
                            dst_ = dstt[:, bb * 128 + g * 64:bb * 128 + g * 64 + 64]
                            if eng == "ve":
                                p.op("ve", "tensor_copy", dst_, PT[:, 0:64])
                            else:
                                p.op("se", "activation", dst_, PT[:, 0:64], AF.Copy)
                p.wait("te", "ve", p.cnt["ve"])
                p.wait("te", "se", p.cnt["se"])
                KV = KVP[j % 2]
                for bb in range(4):
                    p.op("te", "matmul", KV[:, :],
                         kfp_tm[:, bb * 128:(bb + 1) * 128],
                         vp_tm[:, bb * 128:(bb + 1) * 128],
                         start=(bb == 0), stop=(bb == 3), inc=(bb == 3))
                p.wait("ve", "te", p.cnt["te"])
                p.op("ve", "tensor_tensor", kv_acc[:, hp_ * 128:(hp_ + 1) * 128],
                     kv_acc[:, hp_ * 128:(hp_ + 1) * 128], KV[:, :], ALU.add)
            p.op("ve", "tensor_copy", kvb_st[:, :], kv_acc[:, :])
            p.op("ve", "tensor_copy", z_col[:, :], zcf[:, :])
            p.wait("gp", "ve", p.cnt["ve"])
            for hp_ in range(8):
                p.dma("gp", kv_sb[0:64, (2 * hp_) * 64:(2 * hp_ + 1) * 64],
                      kvb_st[0:64, hp_ * 128:hp_ * 128 + 64])
                p.dma("gp", kv_sb[0:64, (2 * hp_ + 1) * 64:(2 * hp_ + 2) * 64],
                      kvb_st[64:128, hp_ * 128 + 64:hp_ * 128 + 128])
            p.wait("te", "d_gp", p.cnt["d_gp"])
            p.wait("ve", "d_gp", p.cnt["d_gp"])
            if "kv" in debug_want:
                p.wait_compute("gp")
                p.dma("gp", tap("kv", 64, H0 * 64, BF16)[:, :], kv_sb[:, :])
                p.dma("gp", tap("z", 64, H0, BF16)[:, :], z_col[:, :])
            p.barrier()

        if BU >= 4:
          with contextlib.ExitStack() as ph:
            wq = ph.enter_context(nc.sbuf_tensor("p2wq", [128, 1024], BF16))
            wo = ph.enter_context(nc.sbuf_tensor("p2wo", [128, NCH * D], BF16))
            bwT = ph.enter_context(nc.sbuf_tensor("p2bw", [128, NCH * D], BF16))
            rwT = ph.enter_context(nc.sbuf_tensor("p2rw", [128, NCH * 8], BF16))
            rb = ph.enter_context(nc.sbuf_tensor("p2rb", [8, 1], F32))
            lA = ph.enter_context(nc.sbuf_tensor("p2lA", [128, NCH * 8], BF16))
            lB = ph.enter_context(nc.sbuf_tensor("p2lB", [64, D], BF16))
            bb = ph.enter_context(nc.sbuf_tensor("p2bb", [128, 8], F32))
            g8 = ph.enter_context(nc.sbuf_tensor("p2g8", [8, 64], BF16))
            t8 = ph.enter_context(nc.sbuf_tensor("p2t8", [8, 64], BF16))
            qf = ph.enter_context(nc.sbuf_tensor("p2qf", [64, 2 * TT], BF16))
            rden = ph.enter_context(nc.sbuf_tensor("p2rden", [1, TT], F32))
            ctx = ph.enter_context(nc.sbuf_tensor("p2ctx", [128, NCH * TT], BF16))
            s_sb = ph.enter_context(nc.sbuf_tensor("p2s", [128, NCH * TT], F32))
            g_sb = ph.enter_context(nc.sbuf_tensor("p2g", [128, NCH * TT], F32))
            lgt = ph.enter_context(nc.sbuf_tensor("p2lgt", [8, TT], F32))
            gt_tm = ph.enter_context(nc.sbuf_tensor("p2gtm", [128, 4 * 8], BF16))
            gate_fm = ph.enter_context(nc.sbuf_tensor("p2gfm", [8, TT], BF16))
            mid_sb = ph.enter_context(nc.sbuf_tensor("p2mid", [8, TT], BF16))
            gmid = ph.enter_context(nc.sbuf_tensor("p2gmid", [64, TT], BF16))
            xt = ph.enter_context(nc.sbuf_tensor("p2x", [128, NCH * TT], F32))
            x1t = xt
            xn1t = ph.enter_context(nc.sbuf_tensor("p2xn1", [128, NCH * TT], BF16))
            sqb = ph.enter_context(nc.sbuf_tensor("p2sq", [128, NCH * TT], BF16))
            xbb = ph.enter_context(nc.sbuf_tensor("p2xb", [128, NCH * TT], BF16))
            tmpf = ph.enter_context(nc.sbuf_tensor("p2tmp", [128, TT], F32))
            tmpf2 = ph.enter_context(nc.sbuf_tensor("p2tmp2", [64, 2 * TT], F32))
            srow = ph.enter_context(nc.sbuf_tensor("p2srow", [1, 4 * TT], F32))
            rowm = ph.enter_context(nc.sbuf_tensor("p2rowm", [128, 1], F32))
            PA = ph.enter_context(nc.psum_tensor("p2PA", [128, 1024], F32))
            PB = ph.enter_context(nc.psum_tensor("p2PB", [128, 512], F32))
            PC = ph.enter_context(nc.psum_tensor("p2PC", [128, 512], F32))
            PD = ph.enter_context(nc.psum_tensor("p2PD", [128, 512], F32))
            PE = ph.enter_context(nc.psum_tensor("p2PE", [1, 512], F32))
            PF = ph.enter_context(nc.psum_tensor("p2PF", [1, 512], F32))
            PG = ph.enter_context(nc.psum_tensor("p2PG", [8, 512], BF16))
            p.dma("gp", wq[0:64, :], hb["l0_wq"][:, :])
            p.dma("gp", wq[64:128, :], hb["l0_wq"][:, :])
            load_w("gp", wo, "l0_wo", D)
            load_w("gp", bwT, "l0_bwT", D)
            load_w("gp", rwT, "l0_rwT", 8)
            p.dma("sy", rb[:, :], hb["l0_rb"][:, :])
            load_w("gp", lA, "l0_lA", 8)
            p.dma("gp", lB[:, :], hb["l0_lB"][:, :])
            p.dma("sy", bb[:, :], hb["l0_bb"][:, :])
            p.dma("gp", g8[:, :], hb["c_g8"][:, :])
            ld = p.dma("gp", t8[:, :], hb["c_t8"][:, :])
            p.wait("te", "d_gp", ld)
            p.wait("ve", "d_gp", ld)
            p.wait("te", "d_sy", p.cnt["d_sy"])
            p.wait("ve", "d_sy", p.cnt["d_sy"])
            p.op("ve", "tensor_scalar", lB[:, :], lB[:, :], 2.0, None, ALU.mult)
            st_x1 = 0
            consS = {0: 0, 1: 0}
            for t in range(NTILE):
                tc0 = t * TT
                p.wait_compute("sy")
                p.wait("sy", "d_gp", st_x1)
                ldx = load_fm_tile("sy", xt, lambda c: c * TT, "x_fm", tc0, TT)
                for h in range(H0):
                    b = (h % 2) * 64
                    g = h % 2
                    if g == 0:
                        p.wait("te", "ve", p.cnt["ve"])
                        p.wait("te", "se", p.cnt["se"])
                        for g2 in range(2):
                            h2_ = h + g2
                            b2 = (h2_ % 2) * 64
                            p.op("te", "matmul", PA[0:64, g2 * TT:(g2 + 1) * TT],
                                 wq[b2:b2 + 64, h2_ * 64:(h2_ + 1) * 64],
                                 xn0[b2:b2 + 64, (h2_ // 2) * S + tc0:(h2_ // 2) * S + tc0 + TT],
                                 start=True, stop=True, inc=(g2 == 1))
                        p.wait("ve", "te", p.cnt["te"])
                        p.op("ve", "tensor_scalar", tmpf2[:, :], PA[0:64, 0:2 * TT],
                             0.0, None, ALU.min)
                        p.wait("se", "ve", p.cnt["ve"])
                        p.op("se", "activation", tmpf2[:, :], tmpf2[:, :], AF.Exp)
                        p.wait("ve", "se", p.cnt["se"])
                        p.op("ve", "scalar_tensor_tensor", qf[:, :],
                             PA[0:64, 0:2 * TT], 0.0, tmpf2[:, :], ALU.max, ALU.add)
                    p.wait("te", "ve", p.cnt["ve"])
                    p.op("te", "matmul", PC[0:64, 0:TT], kv_sb[:, h * 64:(h + 1) * 64],
                         qf[:, g * TT:(g + 1) * TT], start=True, stop=True)
                    p.op("te", "matmul", PE[0:1, 0:TT], z_col[:, h:h + 1],
                         qf[:, g * TT:(g + 1) * TT], start=True, stop=True, inc=True)
                    p.wait("ve", "te", p.cnt["te"])
                    p.op("ve", "tensor_scalar", rden[0:1, 0:TT], PE[0:1, 0:TT],
                         1e-6, None, ALU.add)
                    p.op("ve", "reciprocal", rden[0:1, 0:TT], rden[0:1, 0:TT])
                    p.wait("te", "ve", p.cnt["ve"])
                    p.op("te", "matmul", PD[0:64, 0:TT], ones1f[0:1, 0:64],
                         rden[0:1, 0:TT], start=True, stop=True, inc=True)
                    p.wait("ve", "te", p.cnt["te"])
                    p.op("ve", "tensor_copy", tmpf2[:, 0:TT], PC[0:64, 0:TT])
                    p.op("ve", "tensor_tensor",
                         ctx[b:b + 64, (h // 2) * TT:(h // 2) * TT + TT],
                         tmpf2[:, 0:TT], PD[0:64, 0:TT], ALU.mult)
                ctx_done = p.cnt["ve"]
                for oc in range(NCH):
                    par = oc % 2
                    PP = PB if par == 0 else PA
                    p.wait("te", "ve", ctx_done)
                    p.wait("te", "se", consS[par])
                    for ec in range(NCH):
                        p.op("te", "matmul", PP[:, 0:TT],
                             wo[:, ec * D + oc * 128:ec * D + oc * 128 + 128],
                             ctx[:, ec * TT:(ec + 1) * TT],
                             start=(ec == 0), stop=(ec == NCH - 1), inc=(ec == NCH - 1))
                    p.wait("se", "te", p.cnt["te"])
                    p.op("se", "activation", s_sb[:, oc * TT:(oc + 1) * TT], PP[:, 0:TT],
                         AF.Copy)
                    consS[par] = p.cnt["se"]
                p.wait("te", "se", p.cnt["se"])
                p.wait("te", "ve", p.cnt["ve"])
                for dc in range(NCH):
                    p.op("te", "matmul", PC[0:8, 0:TT], rwT[:, dc * 8:(dc + 1) * 8],
                         xn0[:, dc * S + tc0:dc * S + tc0 + TT],
                         start=(dc == 0), stop=(dc == NCH - 1), inc=(dc == NCH - 1))
                p.wait("ve", "te", p.cnt["te"])
                p.op("ve", "tensor_scalar", lgt[:, :], PC[0:8, 0:TT], rb[:, 0:1], None,
                     ALU.add)
                for j in range(4):
                    p.wait("te", "ve", p.cnt["ve"])
                    p.wait("te", "se", p.cnt["se"])
                    p.op("te", "transpose", PD[:, 0:8],
                         lgt[0:8, j * 128:(j + 1) * 128], ident[0:8, 0:8], inc=True)
                    p.wait("ve", "te", p.cnt["te"])
                    p.op("ve", "reduce_max", rowm[:, 0:1], PD[:, 0:8],
                         axis=mybir.AxisListType.X, negate=True)
                    p.wait("se", "ve", p.cnt["ve"])
                    p.op("se", "activation", tmpf[:, 0:8], PD[:, 0:8],
                         AF.Exp, bias=rowm[:, 0:1])
                    p.wait("ve", "se", p.cnt["se"])
                    p.op("ve", "reduce_sum", rowm[:, 0:1], tmpf[:, 0:8],
                         axis=mybir.AxisListType.X)
                    p.vsync()
                    p.op("ve", "reciprocal", rowm[:, 0:1], rowm[:, 0:1])
                    p.vsync()
                    p.op("ve", "tensor_scalar", gt_tm[:, j * 8:(j + 1) * 8],
                         tmpf[:, 0:8], rowm[:, 0:1], None, ALU.mult)
                for j in range(4):
                    p.wait("te", "ve", p.cnt["ve"])
                    p.op("te", "transpose", PG[0:8, 0:128],
                         gt_tm[:, j * 8:(j + 1) * 8], identb[:, 0:128], inc=True)
                    p.wait("ve", "te", p.cnt["te"])
                    p.op("ve", "tensor_copy", gate_fm[:, j * 128:(j + 1) * 128],
                         PG[0:8, 0:128])
                p.wait("te", "ve", p.cnt["ve"])
                for dc in range(NCH):
                    p.op("te", "matmul", PD[0:8, 0:TT], lA[:, dc * 8:(dc + 1) * 8],
                         xn0[:, dc * S + tc0:dc * S + tc0 + TT],
                         start=(dc == 0), stop=(dc == NCH - 1), inc=(dc == NCH - 1))
                p.wait("ve", "te", p.cnt["te"])
                p.op("ve", "tensor_copy", mid_sb[:, :], PD[0:8, 0:TT])
                p.wait("te", "ve", p.cnt["ve"])
                p.op("te", "matmul", PA[0:64, 0:TT], g8[:, :], gate_fm[:, :],
                     start=True, stop=True)
                p.op("te", "matmul", PA[0:64, TT:2 * TT], t8[:, :], mid_sb[:, :],
                     start=True, stop=True, inc=True)
                p.wait("ve", "te", p.cnt["te"])
                p.op("ve", "tensor_copy", tmpf2[:, 0:TT], PA[0:64, 0:TT])
                p.op("ve", "tensor_tensor", gmid[:, :], tmpf2[:, 0:TT],
                     PA[0:64, TT:2 * TT], ALU.mult)
                gmid_done = p.cnt["ve"]
                for oc in range(NCH):
                    par = oc % 2
                    PP = PB if par == 0 else PA
                    p.wait("te", "ve", gmid_done)
                    p.wait("te", "se", consS[par])
                    for dc in range(NCH):
                        p.op("te", "matmul", PP[:, 0:TT],
                             bwT[:, dc * D + oc * 128:dc * D + oc * 128 + 128],
                             xn0[:, dc * S + tc0:dc * S + tc0 + TT],
                             start=(dc == 0), stop=False)
                    p.op("te", "matmul", PP[:, 0:TT], lB[:, oc * 128:(oc + 1) * 128],
                         gmid[:, :], start=False, stop=True, inc=True)
                    p.wait("se", "te", p.cnt["te"])
                    p.op("se", "activation", g_sb[:, oc * TT:(oc + 1) * TT], PP[:, 0:TT],
                         AF.Silu, bias=bb[:, oc:oc + 1])
                    consS[par] = p.cnt["se"]
                p.wait("ve", "d_sy", ldx)
                p.wait("ve", "se", p.cnt["se"])
                p.wait("ve", "d_gp", st_x1)
                for c in range(NCH):
                    p.op("ve", "tensor_tensor", tmpf[:, 0:TT], s_sb[:, c * TT:(c + 1) * TT],
                         g_sb[:, c * TT:(c + 1) * TT], ALU.mult)
                    p.op("ve", "tensor_tensor", x1t[:, c * TT:(c + 1) * TT],
                         tmpf[:, 0:TT], xt[:, c * TT:(c + 1) * TT], ALU.add)
                emit_ln(lambda c: x1t[:, c * TT:(c + 1) * TT],
                        lambda c: xn1t[:, c * TT:(c + 1) * TT],
                        TT, sqb, xbb, tmpf, srow, PE, PF, PC, PD)
                p.wait("gp", "ve", p.cnt["ve"])
                store_fm_tile("gp", "x1h", x1t, lambda c: c * TT, tc0, TT)
                st_x1 = store_fm_tile("gp", "xn1h", xn1t, lambda c: c * TT, tc0, TT)
                if "gate_l0" in debug_want:
                    p.wait_compute("gp")
                    p.dma("gp", tap("gate_l0", 8, S, BF16)[:, tc0:tc0 + TT], gate_fm[:, :])
                    p.dma("gp", tap("mid_l0", 8, S, BF16)[:, tc0:tc0 + TT], mid_sb[:, :])
                    p.dma("gp", tap("gmid_l0", 64, S, BF16)[:, tc0:tc0 + TT], gmid[:, :])
                    p.dma("gp", tap("lgt_l0", 8, S)[:, tc0:tc0 + TT], lgt[:, :])
                if "s0" in debug_want:
                    for c in range(NCH):
                        p.dma("gp", tap("s0", D, S)[c * 128:(c + 1) * 128, tc0:tc0 + TT],
                              s_sb[:, c * TT:(c + 1) * TT])
                if "g0" in debug_want:
                    for c in range(NCH):
                        p.dma("gp", tap("g0", D, S)[c * 128:(c + 1) * 128, tc0:tc0 + TT],
                              g_sb[:, c * TT:(c + 1) * TT])
            p.barrier()

    if BU >= 5:
      with contextlib.ExitStack() as ph:
        wk = ph.enter_context(nc.sbuf_tensor("p3wk", [128, NCH * D], BF16))
        wv = ph.enter_context(nc.sbuf_tensor("p3wv", [128, NCH * D], BF16))
        wg = ph.enter_context(nc.sbuf_tensor("p3wg", [128, NCH * 2], BF16))
        xnt = [ph.enter_context(nc.sbuf_tensor(f"p3x{i}", [128, NCH * TT], BF16))
               for i in range(2)]
        kt = ph.enter_context(nc.sbuf_tensor("p3k", [128, NCH * TT], BF16))
        vt = ph.enter_context(nc.sbuf_tensor("p3v", [128, NCH * TT], BF16))
        grow = ph.enter_context(nc.sbuf_tensor("p3grow", [2, 1], F32))
        P3A = ph.enter_context(nc.psum_tensor("p3A", [128, 512], F32))
        P3B = ph.enter_context(nc.psum_tensor("p3B", [128, 512], F32))
        P3C = ph.enter_context(nc.psum_tensor("p3C", [2, 512], F32))
        load_w("gp", wk, "l1_wkT", D)
        load_w("gp", wv, "l1_wvT", D)
        ld = load_w("gp", wg, "l1_wg", 2)
        p.wait("te", "d_gp", ld)
        lds = [load_fm_tile("sy", xnt[0], lambda c: c * TT, "xn1h", 0, TT)]
        st_done = 0
        cons3 = {0: 0, 1: 0}
        for t in range(NTILE):
            if t + 1 < NTILE:
                p.wait_compute("sy")
                lds.append(load_fm_tile("sy", xnt[(t + 1) % 2], lambda c: c * TT,
                                        "xn1h", (t + 1) * TT, TT))
            xn = xnt[t % 2]
            p.wait("te", "d_sy", lds[t])
            p.wait("se", "d_gp", st_done)
            for wsb, dst in ((wk, kt), (wv, vt)):
                for oc in range(NCH):
                    par = oc % 2
                    PP = P3A if par == 0 else P3B
                    p.wait("te", "se", cons3[par])
                    for dc in range(NCH):
                        p.op("te", "matmul", PP[:, 0:TT],
                             wsb[:, dc * D + oc * 128:dc * D + oc * 128 + 128],
                             xn[:, dc * TT:(dc + 1) * TT],
                             start=(dc == 0), stop=(dc == NCH - 1), inc=(dc == NCH - 1))
                    p.wait("se", "te", p.cnt["te"])
                    p.op("se", "activation", dst[:, oc * TT:(oc + 1) * TT], PP[:, 0:TT],
                         AF.Copy)
                    cons3[par] = p.cnt["se"]
            p.wait("te", "ve", p.cnt["ve"])
            for dc in range(NCH):
                p.op("te", "matmul", P3C[0:2, 0:TT], wg[:, dc * 2:(dc + 1) * 2],
                     xn[:, dc * TT:(dc + 1) * TT],
                     start=(dc == 0), stop=(dc == NCH - 1), inc=(dc == NCH - 1))
            p.wait("ve", "te", p.cnt["te"])
            p.op("ve", "reduce_sum", grow[:, 0:1], P3C[0:2, 0:TT],
                 axis=mybir.AxisListType.X)
            p.vsync()
            p.op("ve", "tensor_tensor", gacc[:, :], gacc[:, :], grow[:, :], ALU.add)
            p.wait("gp", "se", p.cnt["se"])
            store_fm_tile("gp", "kfmh", kt, lambda c: c * TT, t * TT, TT)
            st_done = store_fm_tile("gp", "vfmh", vt, lambda c: c * TT, t * TT, TT)
        p.barrier()

    if BU >= 6:
      with contextlib.ExitStack() as ph:
        w0T = ph.enter_context(nc.sbuf_tensor("p4w0", [128, NCH * D], BF16))
        w1T = ph.enter_context(nc.sbuf_tensor("p4w1", [128, NCH * D], BF16))
        w1n = ph.enter_context(nc.sbuf_tensor("p4w1n", [128, NCH * D], BF16))
        ktf = [ph.enter_context(nc.sbuf_tensor(f"p4k{i}", [128, NCH * TT], BF16))
               for i in range(2)]
        vtf = [ph.enter_context(nc.sbuf_tensor(f"p4v{i}", [128, NCH * TT], BF16))
               for i in range(2)]
        a1t = ph.enter_context(nc.sbuf_tensor("p4a1", [128, NCH * TT], BF16))
        d2t = ph.enter_context(nc.sbuf_tensor("p4d2", [128, NCH * TT], BF16))
        d1t = ph.enter_context(nc.sbuf_tensor("p4d1", [128, NCH * TT], BF16))
        sdt = ph.enter_context(nc.sbuf_tensor("p4sd", [128, NCH * TT], F32))
        sgt = ph.enter_context(nc.sbuf_tensor("p4sg", [128, 2 * TT], F32))
        tmpa = ph.enter_context(nc.sbuf_tensor("p4tmpa", [128, 2 * TT], F32))
        tmtile = ph.enter_context(nc.sbuf_tensor("p4tm", [128, 4 * D], BF16))
        P4A = ph.enter_context(nc.psum_tensor("p4A", [128, 512], F32))
        P4B = ph.enter_context(nc.psum_tensor("p4B", [128, 512], F32))
        P4C = ph.enter_context(nc.psum_tensor("p4C", [128, 512], F32))
        P4T = ph.enter_context(nc.psum_tensor("p4T", [128, 128], BF16))
        P4T2 = ph.enter_context(nc.psum_tensor("p4T2", [128, 128], BF16))
        P4A2 = ph.enter_context(nc.psum_tensor("p4A2", [128, 512], F32))
        P4B2 = ph.enter_context(nc.psum_tensor("p4B2", [128, 512], F32))
        P4C2 = ph.enter_context(nc.psum_tensor("p4C2", [128, 512], F32))
        load_w("gp", w0T, "l1_w0T", D)
        load_w("gp", w1T, "l1_w1T", D)
        ld = load_w("gp", w1n, "l1_w1n", D)
        p.wait("te", "d_gp", ld)
        st_tm = [0]

        def transpose_out(src_b, hbm_name, t):
            for oc in range(NCH):
                p.wait("ve", "d_gp", st_tm[0])
                p.wait("se", "d_gp", st_tm[0])
                for s_ in range(4):
                    PT = P4T if s_ % 2 == 0 else P4T2
                    eng = "ve" if s_ % 2 == 0 else "se"
                    p.wait("te", eng, p.cnt[eng])
                    p.op("te", "transpose", PT[:, :],
                         src_b[:, oc * TT + s_ * 128:oc * TT + (s_ + 1) * 128],
                         identb[:, :], inc=True)
                    p.wait(eng, "te", p.cnt["te"])
                    dst_ = tmtile[:, s_ * D + oc * 128:s_ * D + oc * 128 + 128]
                    if eng == "ve":
                        p.op("ve", "tensor_copy", dst_, PT[:, :])
                    else:
                        p.op("se", "activation", dst_, PT[:, :], AF.Copy)
            p.wait("gp", "ve", p.cnt["ve"])
            p.wait("gp", "se", p.cnt["se"])
            for s_ in range(4):
                p.dma("gp", hb[hbm_name][t * TT + s_ * 128:t * TT + (s_ + 1) * 128, :],
                      tmtile[:, s_ * D:(s_ + 1) * D])
            st_tm[0] = p.cnt["d_gp"]

        lds = [(load_fm_tile("sy", ktf[0], lambda c: c * TT, "kfmh", 0, TT),
                load_fm_tile("sy", vtf[0], lambda c: c * TT, "vfmh", 0, TT))]
        consA = {0: (0, 0), 1: (0, 0)}
        consB = {0: 0, 1: 0}
        consC = {0: 0, 1: 0}
        for t in range(NTILE):
            if t + 1 < NTILE:
                p.wait_compute("sy")
                lds.append((load_fm_tile("sy", ktf[(t + 1) % 2], lambda c: c * TT,
                                         "kfmh", (t + 1) * TT, TT),
                            load_fm_tile("sy", vtf[(t + 1) % 2], lambda c: c * TT,
                                         "vfmh", (t + 1) * TT, TT)))
            kt, vt = ktf[t % 2], vtf[t % 2]
            p.wait("te", "d_sy", lds[t][1])
            for oc in range(NCH):
                par = oc % 2
                PP = P4A if par == 0 else P4A2
                sg_ = sgt[:, 0:TT] if par == 0 else sgt[:, TT:2 * TT]
                tm_ = tmpa[:, 0:TT] if par == 0 else tmpa[:, TT:2 * TT]
                p.wait("te", "se", consA[par][0])
                p.wait("te", "ve", consA[par][1])
                for dc in range(NCH):
                    p.op("te", "matmul", PP[:, 0:TT],
                         w0T[:, dc * D + oc * 128:dc * D + oc * 128 + 128],
                         kt[:, dc * TT:(dc + 1) * TT],
                         start=(dc == 0), stop=(dc == NCH - 1), inc=(dc == NCH - 1))
                p.wait("se", "te", p.cnt["te"])
                p.op("se", "activation", a1t[:, oc * TT:(oc + 1) * TT], PP[:, 0:TT],
                     AF.Silu)
                p.op("se", "activation", sg_, PP[:, 0:TT], AF.Sigmoid)
                p.wait("ve", "se", p.cnt["se"])
                p.op("ve", "tensor_tensor", tm_, sg_,
                     a1t[:, oc * TT:(oc + 1) * TT], ALU.mult)
                p.op("ve", "tensor_tensor", sdt[:, oc * TT:(oc + 1) * TT], sg_,
                     a1t[:, oc * TT:(oc + 1) * TT], ALU.add)
                p.op("ve", "tensor_tensor", sdt[:, oc * TT:(oc + 1) * TT],
                     sdt[:, oc * TT:(oc + 1) * TT], tm_, ALU.subtract)
                consA[par] = (p.cnt["se"], p.cnt["ve"])
            p.wait("te", "se", p.cnt["se"])
            for oc in range(NCH):
                par = oc % 2
                PP = P4B if par == 0 else P4B2
                tm_ = tmpa[:, 0:TT] if par == 0 else tmpa[:, TT:2 * TT]
                p.wait("te", "ve", consB[par])
                for ec in range(NCH):
                    p.op("te", "matmul", PP[:, 0:TT],
                         w1T[:, ec * D + oc * 128:ec * D + oc * 128 + 128],
                         a1t[:, ec * TT:(ec + 1) * TT],
                         start=(ec == 0), stop=(ec == NCH - 1), inc=(ec == NCH - 1))
                p.wait("ve", "te", p.cnt["te"])
                p.op("ve", "tensor_tensor", tm_, PP[:, 0:TT],
                     vt[:, oc * TT:(oc + 1) * TT], ALU.subtract)
                p.op("ve", "tensor_scalar", d2t[:, oc * TT:(oc + 1) * TT], tm_,
                     -5.0, 5.0, ALU.max, ALU.min)
                consB[par] = p.cnt["ve"]
            p.wait("te", "ve", p.cnt["ve"])
            for oc in range(NCH):
                par = oc % 2
                PP = P4C if par == 0 else P4C2
                p.wait("te", "ve", consC[par])
                for dc in range(NCH):
                    p.op("te", "matmul", PP[:, 0:TT],
                         w1n[:, dc * D + oc * 128:dc * D + oc * 128 + 128],
                         d2t[:, dc * TT:(dc + 1) * TT],
                         start=(dc == 0), stop=(dc == NCH - 1), inc=(dc == NCH - 1))
                p.wait("ve", "te", p.cnt["te"])
                p.op("ve", "scalar_tensor_tensor", d1t[:, oc * TT:(oc + 1) * TT],
                     PP[:, 0:TT], SCALE2S, sdt[:, oc * TT:(oc + 1) * TT],
                     ALU.mult, ALU.mult)
                consC[par] = p.cnt["ve"]
            transpose_out(a1t, "a1tmh", t)
            transpose_out(d2t, "d2tmh", t)
            transpose_out(d1t, "d1tmh", t)
            transpose_out(kt, "ktmh", t)
        p.barrier()

    if BU >= 7:
      with contextlib.ExitStack() as ph:
        G0T = ph.enter_context(nc.sbuf_tensor("G0T", [128, NCH * D], F32))
        G1T = ph.enter_context(nc.sbuf_tensor("G1T", [128, NCH * D], F32))
        rhsb = [ph.enter_context(nc.sbuf_tensor(f"p5r{i}", [128, D], BF16))
                for i in range(2)]
        lhsb = [ph.enter_context(nc.sbuf_tensor(f"p5l{i}", [128, 512], BF16))
                for i in range(2)]
        GP = [ph.enter_context(nc.psum_tensor(f"p5G{j}", [128, D], F32))
              for j in range(4)]
        NB = S // 128
        for (lhs_name, rhs_name, dst) in (("a1tmh", "d2tmh", G1T),
                                          ("ktmh", "d1tmh", G0T)):
            for half in range(2):
                lds = [(p.dma("sy", lhsb[0][:, :],
                              hb[lhs_name][0:128, half * 512:(half + 1) * 512]),
                        p.dma("sy", rhsb[0][:, :], hb[rhs_name][0:128, :]))]
                for tb in range(NB):
                    if tb + 1 < NB:
                        p.wait_compute("sy")
                        lds.append((
                            p.dma("sy", lhsb[(tb + 1) % 2][:, :],
                                  hb[lhs_name][(tb + 1) * 128:(tb + 2) * 128,
                                               half * 512:(half + 1) * 512]),
                            p.dma("sy", rhsb[(tb + 1) % 2][:, :],
                                  hb[rhs_name][(tb + 1) * 128:(tb + 2) * 128, :])))
                    p.wait("te", "d_sy", lds[tb][1])
                    for j in range(4):
                        for g in range(2):
                            p.op("te", "matmul", GP[j][:, g * 512:(g + 1) * 512],
                                 lhsb[tb % 2][:, j * 128:(j + 1) * 128],
                                 rhsb[tb % 2][:, g * 512:(g + 1) * 512],
                                 start=(tb == 0), stop=(tb == NB - 1),
                                 inc=(j == 3 and g == 1),
                                 skip_group_check=True)
                p.wait("ve", "te", p.cnt["te"])
                for j in range(4):
                    p.op("ve", "tensor_copy",
                         dst[:, (half * 4 + j) * D:(half * 4 + j + 1) * D], GP[j][:, :])
                p.wait("te", "ve", p.cnt["ve"])
        p.wait("gp", "ve", p.cnt["ve"])
        for c in range(NCH):
            p.dma("gp", hb["cc_in"][c * 128:(c + 1) * 128, :], G0T[:, c * D:(c + 1) * D])
            p.dma("gp", hb["cc_in"][1024 + c * 128:1024 + (c + 1) * 128, :],
                  G1T[:, c * D:(c + 1) * D])
        p.dma("gp", hb["cc_in"][2048:2050, 0:1], gacc[0:2, 0:1], allow_slow_non_contiguous=True)
        p.wait("gp", "d_gp", p.cnt["d_gp"])
        p.ccop("AllReduce", ALU.add, replica_groups=[list(range(NCORE))],
               ins=[hb["cc_in"].ap()], outs=[hb["cc_out"].ap()])
        p.barrier()

    def ffn_layer(l, in_name, out_name):
        sfx = str(l)
        subs = FFN_SUBS
        if "A" in subs:
          with contextlib.ExitStack() as ph:
            wqkv = ph.enter_context(nc.sbuf_tensor("pAwqkv" + sfx, [128, NCH * 3 * INNER], BF16))
            wo = ph.enter_context(nc.sbuf_tensor("pAwo" + sfx, [128, 4 * D], BF16))
            qsv = ph.enter_context(nc.sbuf_tensor("pAqs" + sfx, [128, 2], F32))
            ksv = ph.enter_context(nc.sbuf_tensor("pAks" + sfx, [128, 1], F32))
            h2b = ph.enter_context(nc.sbuf_tensor("pAh2" + sfx, [128, 2], BF16))
            b2t = ph.enter_context(nc.sbuf_tensor("pAb2t" + sfx, [2, 128], F32))
            rtp = ph.enter_context(nc.sbuf_tensor("pArt" + sfx, [128, 128], F32))
            tlo = ph.enter_context(nc.sbuf_tensor("pAtlo" + sfx, [128, 128], F32))
            thi = ph.enter_context(nc.sbuf_tensor("pAthi" + sfx, [128, 128], F32))
            ywm = ph.enter_context(nc.sbuf_tensor("pAywm" + sfx, [128, NCH * WIN], BF16))
            ywr = ph.enter_context(nc.sbuf_tensor("pAywr" + sfx, [128, NCH * WIN], F32))
            hbot = ph.enter_context(nc.sbuf_tensor("pAhbot" + sfx, [128, 4 * WIN], BF16))
            qf32 = ph.enter_context(nc.sbuf_tensor("pAqf" + sfx, [128, 4 * WIN], F32))
            kf32 = ph.enter_context(nc.sbuf_tensor("pAkf" + sfx, [128, 4 * WIN], F32))
            vbf = ph.enter_context(nc.sbuf_tensor("pAvbf" + sfx, [128, WIN], BF16))
            sqb = ph.enter_context(nc.sbuf_tensor("pAsq" + sfx, [128, WIN], BF16))
            nrow = ph.enter_context(nc.sbuf_tensor("pAnrow" + sfx, [2, WIN], F32))
            tm1 = ph.enter_context(nc.sbuf_tensor("pAtm1" + sfx, [128, WIN], F32))
            tm2 = ph.enter_context(nc.sbuf_tensor("pAtm2" + sfx, [128, WIN], F32))
            tabs = ph.enter_context(nc.sbuf_tensor("pAtabs" + sfx, [128, 4 * WIN], F32))
            qrot = ph.enter_context(nc.sbuf_tensor("pAqrot" + sfx, [128, 4 * WIN], BF16))
            krot = [ph.enter_context(nc.sbuf_tensor(f"pAkrot{i}" + sfx, [128, 4 * WIN], BF16))
                    for i in range(2)]
            vtm = [ph.enter_context(nc.sbuf_tensor(f"pAvtm{i}" + sfx, [128, 4 * WIN], BF16))
                   for i in range(2)]
            attn2_ = [ph.enter_context(nc.sbuf_tensor(f"pAattn{i}" + sfx, [128, 1024], BF16))
                      for i in range(2)]
            attnT2_ = [ph.enter_context(nc.sbuf_tensor(f"pAattnT{i}" + sfx, [128, 640], BF16))
                       for i in range(2)]
            rowm = ph.enter_context(nc.sbuf_tensor("pArowm" + sfx, [128, 4], F32))
            ofm = ph.enter_context(nc.sbuf_tensor("pAofm" + sfx, [128, 4 * WIN], BF16))
            yA = ph.enter_context(nc.sbuf_tensor("pAyA" + sfx, [128, NCH * WIN], F32))
            psA = ph.enter_context(nc.psum_tensor("pApsA" + sfx, [128, 512], F32))
            pscp2_ = [ph.enter_context(nc.psum_tensor(f"pAscp{i}" + sfx, [128, 1024], F32))
                      for i in range(2)]
            atpA = ph.enter_context(nc.psum_tensor("pAatpA" + sfx, [128, 128], BF16))
            atpB = ph.enter_context(nc.psum_tensor("pAatpB" + sfx, [128, 128], BF16))
            avp = ph.enter_context(nc.psum_tensor("pAavp" + sfx, [64, 128], F32))
            load_w("gp", wqkv, "f_wqkv", 3 * INNER, row0=l * D)
            load_w("gp", wo, "f_wo", D, row0=l * INNER, nrows=INNER)
            p.dma("sy", qsv[:, 0:1], hb["f_qs"][:, l:l + 1], allow_slow_non_contiguous=True)
            p.dma("sy", ksv[:, :], hb["f_ks"][:, l:l + 1], allow_slow_non_contiguous=True)
            p.dma("gp", h2b[:, :], hb["c_h2"][:, :])
            p.dma("sy", b2t[:, :], hb["c_b2t"][:, :])
            p.dma("sy", rtp[:, :], hb["c_rt"][:, :])
            p.dma("sy", tlo[:, :], hb["c_tlo"][:, :])
            ldw = p.dma("sy", thi[:, :], hb["c_thi"][:, :])
            p.wait("ve", "d_sy", ldw)
            p.wait("ve", "d_gp", p.cnt["d_gp"])
            p.wait("te", "d_gp", p.cnt["d_gp"])
            p.op("ve", "tensor_scalar", qsv[:, 1:2], qsv[:, 0:1], 8.0, None, ALU.mult)
            st_yA = 0
            for w in range(NCH):
                t0 = w * WIN
                p.wait_compute("sy")
                p.wait_compute("gp")
                load_fm_tile("gp", ywm, lambda c: c * WIN, in_name, t0, WIN)
                load_fm_tile("gp", ywr, lambda c: c * WIN, in_name, t0, WIN)
                if w == 0:
                    for c in range(4):
                        p.op("gp", "memset", hbot[:, c * WIN:c * WIN + 1], 0.0)
                    for c in range(4):
                        p.dma("gp", hbot[:, c * WIN + 1:(c + 1) * WIN],
                              hb[in_name][(4 + c) * 128:(5 + c) * 128, 0:WIN - 1])
                else:
                    for c in range(4):
                        p.dma("gp", hbot[:, c * WIN:(c + 1) * WIN],
                              hb[in_name][(4 + c) * 128:(5 + c) * 128, t0 - 1:t0 + WIN - 1])
                for ti, tn in enumerate(("tabC", "tabS", "tabCi", "tabSi")):
                    p.dma("sy", tabs[:, ti * WIN:(ti + 1) * WIN], hb[tn][:, t0:t0 + WIN])
                ld2 = p.cnt["d_sy"]
                ld2g = p.cnt["d_gp"]
                for oc in range(12):
                    p.wait("te", "d_sy", ld2)
                    p.wait("te", "d_gp", ld2g)
                    p.wait("te", "se", p.cnt["se"])
                    p.wait("te", "ve", p.cnt["ve"])
                    for dc in range(NCH):
                        rhs = (ywm[:, dc * WIN:(dc + 1) * WIN] if dc < 4 else
                               hbot[:, (dc - 4) * WIN:(dc - 3) * WIN])
                        p.op("te", "matmul", psA[:, 0:WIN],
                             wqkv[:, dc * 3 * INNER + oc * 128:dc * 3 * INNER + oc * 128 + 128],
                             rhs, start=(dc == 0), stop=(dc == NCH - 1),
                             inc=(dc == NCH - 1))
                    p.wait("se", "te", p.cnt["te"])
                    if oc < 4:
                        p.op("se", "activation", qf32[:, oc * WIN:(oc + 1) * WIN],
                             psA[:, 0:WIN], AF.Copy)
                    elif oc < 8:
                        p.op("se", "activation", kf32[:, (oc - 4) * WIN:(oc - 3) * WIN],
                             psA[:, 0:WIN], AF.Copy)
                    else:
                        p.op("se", "activation", vbf[:, :], psA[:, 0:WIN], AF.Copy)
                        for s_ in range(4):
                            PT = atpA if s_ % 2 == 0 else atpB
                            eng = "ve" if s_ % 2 == 0 else "se"
                            p.wait("te", "se", p.cnt["se"])
                            p.wait("te", eng, p.cnt[eng])
                            p.op("te", "transpose", PT[:, :],
                                 vbf[:, s_ * 128:(s_ + 1) * 128], identb[:, :],
                                 inc=True)
                            p.wait(eng, "te", p.cnt["te"])
                            dst_ = vtm[w % 2][:, s_ * WIN + (oc - 8) * 128:
                                              s_ * WIN + (oc - 8) * 128 + 128]
                            if eng == "ve":
                                p.op("ve", "tensor_copy", dst_, PT[:, :])
                            else:
                                p.op("se", "activation", dst_, PT[:, :], AF.Copy)
                for kq in (range(2) if FFN_ASEC in ("rot", "attn", "all") else ()):
                    src = qf32 if kq == 0 else kf32
                    dst = qrot if kq == 0 else krot[w % 2]
                    ct0 = 0 if kq == 0 else 2 * WIN
                    scl = qsv[:, 1:2] if kq == 0 else ksv[:, 0:1]
                    for c in range(4):
                        p.wait("se", "ve", p.cnt["ve"])
                        p.op("se", "activation", sqb[:, :], src[:, c * WIN:(c + 1) * WIN],
                             AF.Square)
                        p.wait("te", "se", p.cnt["se"])
                        p.op("te", "matmul", psA[0:2, 0:WIN], h2b[:, :], sqb[:, :],
                             start=True, stop=True, inc=True)
                        p.wait("se", "te", p.cnt["te"])
                        p.op("se", "activation", nrow[:, :], psA[0:2, 0:WIN], AF.Sqrt)
                        p.wait("ve", "se", p.cnt["se"])
                        p.op("ve", "tensor_scalar", nrow[:, :], nrow[:, :], 1e-12, None,
                             ALU.max)
                        p.op("ve", "reciprocal", nrow[:, :], nrow[:, :])
                        p.wait("te", "ve", p.cnt["ve"])
                        p.op("te", "matmul", psA[:, 0:WIN], b2t[:, :], nrow[:, :],
                             start=True, stop=True, inc=True)
                        p.wait("ve", "te", p.cnt["te"])
                        p.op("ve", "tensor_tensor", tm1[:, :],
                             src[:, c * WIN:(c + 1) * WIN], psA[:, 0:WIN], ALU.mult)
                        p.op("ve", "tensor_scalar", tm1[:, :], tm1[:, :], scl, None,
                             ALU.mult)
                        p.wait("te", "ve", p.cnt["ve"])
                        p.op("te", "matmul", psA[:, 0:WIN], rtp[:, :], tm1[:, :],
                             start=True, stop=True, inc=True)
                        p.wait("ve", "te", p.cnt["te"])
                        p.op("ve", "tensor_tensor", tm2[:, :], tm1[:, :],
                             tabs[:, ct0:ct0 + WIN], ALU.mult)
                        p.op("ve", "tensor_tensor", tm1[:, :], psA[:, 0:WIN],
                             tabs[:, ct0 + WIN:ct0 + 2 * WIN], ALU.mult)
                        p.op("ve", "tensor_tensor", dst[:, c * WIN:(c + 1) * WIN],
                             tm2[:, :], tm1[:, :], ALU.add)
                its = ([(h, qc) for h in range(FH) for qc in range(4)]
                       if FFN_ASEC in ("attn", "all") else [])
                s1_te = {}
                pscp_free = {0: (0, 0), 1: (0, 0)}
                avp_free = [0]

                def emit_S1(i):
                    h, qc = its[i]
                    hbase = (h % 2) * 64
                    hc = h // 2
                    lo = qc * 128
                    bhi = lo + 640 if w > 0 else lo + 128
                    par = i % 2
                    PSC = pscp2_[par]
                    p.wait("te", "ve", pscp_free[par][0])
                    p.wait("te", "se", pscp_free[par][1])
                    if i == 0:
                        p.wait("te", "ve", p.cnt["ve"])
                        p.wait("te", "se", p.cnt["se"])
                    if w > 0:
                        p.op("te", "matmul", PSC[:, lo:512],
                             qrot[hbase:hbase + 64, hc * WIN + lo:hc * WIN + lo + 128],
                             krot[(w + 1) % 2][hbase:hbase + 64, hc * WIN + lo:hc * WIN + 512],
                             start=True, stop=True)
                        p.op("te", "matmul", PSC[:, 512:bhi],
                             qrot[hbase:hbase + 64, hc * WIN + lo:hc * WIN + lo + 128],
                             krot[w % 2][hbase:hbase + 64, hc * WIN:hc * WIN + bhi - 512],
                             start=True, stop=True, inc=True)
                    else:
                        p.op("te", "matmul", PSC[:, 0:bhi],
                             qrot[hbase:hbase + 64, hc * WIN + lo:hc * WIN + lo + 128],
                             krot[0][hbase:hbase + 64, hc * WIN:hc * WIN + bhi],
                             start=True, stop=True, inc=True)
                    s1_te[i] = p.cnt["te"]

                def emit_S2(i):
                    h, qc = its[i]
                    hbase = (h % 2) * 64
                    hc = h // 2
                    lo = qc * 128
                    if w > 0:
                        blo, bhi = lo, lo + 640
                    else:
                        blo, bhi = 0, lo + 128
                    par = i % 2
                    PSC = pscp2_[par]
                    attn = attn2_[par]
                    attnT = attnT2_[par]
                    rm0 = rowm[:, 2 * par:2 * par + 1]
                    rm1 = rowm[:, 2 * par + 1:2 * par + 2]
                    p.wait("ve", "te", s1_te[i])
                    if w > 0:
                        p.op("ve", "tensor_tensor", PSC[:, lo:lo + 128],
                             PSC[:, lo:lo + 128], tlo[:, :], ALU.add)
                    p.op("ve", "tensor_tensor", PSC[:, bhi - 128:bhi],
                         PSC[:, bhi - 128:bhi], thi[:, :], ALU.add)
                    p.op("ve", "reduce_max", rm0, PSC[:, blo:bhi],
                         axis=mybir.AxisListType.X, negate=True)
                    ve_masks = p.cnt["ve"]
                    p.wait("se", "ve", ve_masks)
                    p.op("se", "activation", attn[:, blo:bhi], PSC[:, blo:bhi],
                         AF.Exp, bias=rm0, accum_out=rm1)
                    se_exp = p.cnt["se"]
                    pscp_free[par] = (ve_masks, se_exp)
                    p.wait("ve", "se", se_exp)
                    p.op("ve", "reciprocal", rm1, rm1)
                    p.vsync()
                    p.op("ve", "tensor_scalar", attn[:, blo:bhi], attn[:, blo:bhi],
                         rm1, None, ALU.mult)
                    nj = (bhi - blo) // 128
                    for jb in range(nj):
                        PT = atpA if jb % 2 == 0 else atpB
                        eng = "ve" if jb % 2 == 0 else "se"
                        p.wait("te", eng, p.cnt[eng])
                        p.op("te", "transpose", PT[:, :],
                             attn[:, blo + jb * 128:blo + (jb + 1) * 128],
                             identb[:, :], inc=True)
                        p.wait(eng, "te", p.cnt["te"])
                        if eng == "ve":
                            p.op("ve", "tensor_copy",
                                 attnT[:, jb * 128:(jb + 1) * 128], PT[:, :])
                        else:
                            p.op("se", "activation",
                                 attnT[:, jb * 128:(jb + 1) * 128], PT[:, :], AF.Copy)
                    p.wait("te", "ve", p.cnt["ve"])
                    p.wait("te", "se", p.cnt["se"])
                    for jb in range(nj):
                        jl = blo + jb * 128
                        if w > 0:
                            if jl < 512:
                                vsl = vtm[(w + 1) % 2][:, (jl // 128) * WIN + h * 64:
                                                       (jl // 128) * WIN + h * 64 + 64]
                            else:
                                vsl = vtm[w % 2][:, ((jl - 512) // 128) * WIN + h * 64:
                                                 ((jl - 512) // 128) * WIN + h * 64 + 64]
                        else:
                            vsl = vtm[0][:, (jl // 128) * WIN + h * 64:
                                         (jl // 128) * WIN + h * 64 + 64]
                        p.op("te", "matmul", avp[:, 0:128], vsl,
                             attnT[:, jb * 128:(jb + 1) * 128],
                             start=(jb == 0), stop=(jb == nj - 1), inc=(jb == nj - 1))
                    p.wait("se", "te", p.cnt["te"])
                    p.op("se", "activation",
                         ofm[hbase:hbase + 64, hc * WIN + lo:hc * WIN + lo + 128],
                         avp[:, 0:128], AF.Copy)
                    avp_free[0] = p.cnt["se"]

                if its:
                    emit_S1(0)
                    for i in range(len(its)):
                        if i + 1 < len(its):
                            emit_S1(i + 1)
                        emit_S2(i)
                ofm_done = p.cnt["ve"]
                for oc in (range(NCH) if FFN_ASEC == "all" else ()):
                    p.wait("te", "ve", p.cnt["ve"])
                    p.wait("te", "se", p.cnt["se"])
                    for kc in range(4):
                        p.op("te", "matmul", psA[:, 0:WIN],
                             wo[:, kc * D + oc * 128:kc * D + oc * 128 + 128],
                             ofm[:, kc * WIN:(kc + 1) * WIN],
                             start=(kc == 0), stop=(kc == 3), inc=(kc == 3))
                    p.wait("ve", "te", p.cnt["te"])
                    p.wait("ve", "d_sy", ld2)
                    p.wait("ve", "d_gp", ld2g)
                    p.op("ve", "tensor_tensor", yA[:, oc * WIN:(oc + 1) * WIN],
                         psA[:, 0:WIN], ywr[:, oc * WIN:(oc + 1) * WIN], ALU.add)
                if FFN_ASEC == "all":
                    p.wait("gp", "ve", p.cnt["ve"])
                    st_yA = store_fm_tile("gp", "yAh", yA, lambda c: c * WIN, t0, WIN)
            p.barrier()
        if "1" in subs:
          with contextlib.ExitStack() as ph:
            w1 = ph.enter_context(nc.sbuf_tensor("pB1w1" + sfx, [128, NCH * W1OUT], BF16))
            lg = ph.enter_context(nc.sbuf_tensor("pB1lg" + sfx, [128, NCH], F32))
            lb = ph.enter_context(nc.sbuf_tensor("pB1lb" + sfx, [128, NCH], F32))
            b1 = ph.enter_context(nc.sbuf_tensor("pB1b1" + sfx, [128, 44], F32))
            h2 = ph.enter_context(nc.sbuf_tensor("pB1h2" + sfx, [128, NCH * WIN], F32))
            h2n = ph.enter_context(nc.sbuf_tensor("pB1h2n" + sfx, [128, NCH * WIN], BF16))
            sqb = ph.enter_context(nc.sbuf_tensor("pB1sq" + sfx, [128, NCH * WIN], BF16))
            xbb = ph.enter_context(nc.sbuf_tensor("pB1xb" + sfx, [128, NCH * WIN], BF16))
            tmpf = ph.enter_context(nc.sbuf_tensor("pB1tmp" + sfx, [128, WIN], F32))
            gel = ph.enter_context(nc.sbuf_tensor("pB1gel" + sfx, [128, 2 * WIN], F32))
            srow = ph.enter_context(nc.sbuf_tensor("pB1srow" + sfx, [1, 4 * WIN], F32))
            act = ph.enter_context(nc.sbuf_tensor("pB1act" + sfx, [128, NJ * WIN], BF16))
            psS = ph.enter_context(nc.psum_tensor("pB1psS" + sfx, [1, WIN], F32))
            psQ = ph.enter_context(nc.psum_tensor("pB1psQ" + sfx, [1, WIN], F32))
            psA = ph.enter_context(nc.psum_tensor("pB1psA" + sfx, [128, WIN], F32))
            psB = ph.enter_context(nc.psum_tensor("pB1psB" + sfx, [128, WIN], F32))
            psa = [ph.enter_context(nc.psum_tensor(f"pB1psa{i}" + sfx, [128, WIN], F32))
                   for i in range(2)]
            psg = [ph.enter_context(nc.psum_tensor(f"pB1psg{i}" + sfx, [128, WIN], F32))
                   for i in range(2)]
            load_w("gp", w1, "f_w1", W1OUT, row0=l * D)
            p.wait("te", "d_gp", p.cnt["d_gp"])
            p.dma("sy", lg[:, :], hb["f_lng"][:, l * 8:(l + 1) * 8])
            p.dma("sy", lb[:, :], hb["f_lnb"][:, l * 8:(l + 1) * 8])
            p.dma("sy", b1[:, :], hb["f_b1"][:, l * 44:(l + 1) * 44])
            st_act = 0
            consW1 = {0: (0, 0), 1: (0, 0)}
            for w in range(NCH):
                t0 = w * WIN
                p.wait_compute("sy")
                for c in range(4):
                    p.dma("sy", h2[:, c * WIN:(c + 1) * WIN],
                          hb["yAh"][c * 128:(c + 1) * 128, t0:t0 + WIN])
                if w == 0:
                    for c in range(4):
                        p.op("gp", "memset", h2[:, (4 + c) * WIN:(4 + c) * WIN + 1], 0.0)
                    p.wait("sy", "gp", p.cnt["gp"])
                    for c in range(4):
                        p.dma("sy", h2[:, (4 + c) * WIN + 1:(5 + c) * WIN],
                              hb["yAh"][(4 + c) * 128:(5 + c) * 128, 0:WIN - 1])
                else:
                    for c in range(4):
                        p.dma("sy", h2[:, (4 + c) * WIN:(5 + c) * WIN],
                              hb["yAh"][(4 + c) * 128:(5 + c) * 128, t0 - 1:t0 + WIN - 1])
                ldh = p.cnt["d_sy"]
                p.wait("se", "d_sy", ldh)
                p.wait("ve", "d_sy", ldh)
                emit_ln(lambda c: h2[:, c * WIN:(c + 1) * WIN],
                        lambda c: h2n[:, c * WIN:(c + 1) * WIN],
                        WIN, sqb, xbb, tmpf, srow, psS, psQ, psA, psB,
                        lg_col=lg, lb_col=lb)
                p.wait("te", "ve", p.cnt["ve"])
                for j in range(NJ):
                    m = CHJ[j]
                    par = j % 2
                    PSA, PSG = psa[par], psg[par]
                    gl_ = gel[:, 0:WIN] if par == 0 else gel[:, WIN:2 * WIN]
                    p.wait("te", "ve", consW1[par][1])
                    p.wait("te", "se", consW1[par][0])
                    for dc in range(NCH):
                        p.op("te", "matmul", PSA[0:m, 0:WIN],
                             w1[:, dc * W1OUT + j * 128:dc * W1OUT + j * 128 + m],
                             h2n[:, dc * WIN:(dc + 1) * WIN],
                             start=(dc == 0), stop=(dc == NCH - 1), inc=(dc == NCH - 1))
                    for dc in range(NCH):
                        p.op("te", "matmul", PSG[0:m, 0:WIN],
                             w1[:, dc * W1OUT + HID + j * 128:dc * W1OUT + HID + j * 128 + m],
                             h2n[:, dc * WIN:(dc + 1) * WIN],
                             start=(dc == 0), stop=(dc == NCH - 1), inc=(dc == NCH - 1))
                    p.wait("se", "te", p.cnt["te"])
                    p.op("se", "activation", gl_[0:m, 0:WIN], PSG[0:m, 0:WIN], AF.Gelu,
                         bias=b1[0:m, 22 + j:23 + j])
                    p.wait("ve", "se", p.cnt["se"])
                    p.wait("ve", "d_gp", st_act)
                    p.op("ve", "scalar_tensor_tensor", act[0:m, j * WIN:(j + 1) * WIN],
                         PSA[0:m, 0:WIN], b1[0:m, j:j + 1], gl_[0:m, 0:WIN],
                         ALU.add, ALU.mult)
                    consW1[par] = (p.cnt["se"], p.cnt["ve"])
                p.wait("gp", "ve", p.cnt["ve"])
                for j in range(NJ):
                    p.dma("gp", hb["acth"][j * 128:j * 128 + CHJ[j], t0:t0 + WIN],
                          act[0:CHJ[j], j * WIN:(j + 1) * WIN])
                st_act = p.cnt["d_gp"]
            p.barrier()
        if "2" in subs:
          with contextlib.ExitStack() as ph:
            w2 = ph.enter_context(nc.sbuf_tensor("pB2w2" + sfx, [128, NJ * D], BF16))
            b2 = ph.enter_context(nc.sbuf_tensor("pB2b2" + sfx, [128, NCH], F32))
            actt = [ph.enter_context(nc.sbuf_tensor(f"pB2a{i}" + sfx, [128, NJ * WIN], BF16))
                    for i in range(2)]
            ywr = [ph.enter_context(nc.sbuf_tensor(f"pB2y{i}" + sfx, [128, NCH * WIN], F32))
                   for i in range(2)]
            yo = ph.enter_context(nc.sbuf_tensor("pB2yo" + sfx, [128, NCH * WIN], F32))
            ps2 = [ph.enter_context(nc.psum_tensor(f"pB2ps{i}" + sfx, [128, WIN], F32))
                   for i in range(2)]
            for j in range(NJ):
                p.dma("gp", w2[0:CHJ[j], j * D:j * D + D],
                      hb["f_w2"][l * HID + j * 128:l * HID + j * 128 + CHJ[j], 0:D])
            p.dma("sy", b2[:, :], hb["f_b2"][:, l * 8:(l + 1) * 8])
            p.wait("te", "d_gp", p.cnt["d_gp"])

            def ldact(w, buf):
                for j in range(NJ):
                    p.dma("sy", actt[buf][0:CHJ[j], j * WIN:(j + 1) * WIN],
                          hb["acth"][j * 128:j * 128 + CHJ[j], w * WIN:(w + 1) * WIN])
                for c in range(NCH):
                    p.dma("sy", ywr[buf][:, c * WIN:(c + 1) * WIN],
                          hb["yAh"][c * 128:(c + 1) * 128, w * WIN:(w + 1) * WIN])
                return p.cnt["d_sy"]

            lds = [ldact(0, 0)]
            st_yo = 0
            consW2 = {0: 0, 1: 0}
            for w in range(NCH):
                if w + 1 < NCH:
                    p.wait_compute("sy")
                    lds.append(ldact(w + 1, (w + 1) % 2))
                p.wait("te", "d_sy", lds[w])
                for oc in range(NCH):
                    par = oc % 2
                    PP = ps2[par]
                    p.wait("te", "ve", consW2[par])
                    for j in range(NJ):
                        p.op("te", "matmul", PP[:, 0:WIN],
                             w2[0:CHJ[j], j * D + oc * 128:j * D + oc * 128 + 128],
                             actt[w % 2][0:CHJ[j], j * WIN:(j + 1) * WIN],
                             start=(j == 0), stop=(j == NJ - 1), inc=(j == NJ - 1))
                    p.wait("ve", "te", p.cnt["te"])
                    p.wait("ve", "d_gp", st_yo)
                    p.op("ve", "tensor_scalar", yo[:, oc * WIN:(oc + 1) * WIN],
                         PP[:, 0:WIN], b2[:, oc:oc + 1], None, ALU.add)
                    p.op("ve", "tensor_tensor", yo[:, oc * WIN:(oc + 1) * WIN],
                         yo[:, oc * WIN:(oc + 1) * WIN],
                         ywr[w % 2][:, oc * WIN:(oc + 1) * WIN], ALU.add)
                    consW2[par] = p.cnt["ve"]
                p.wait("gp", "ve", p.cnt["ve"])
                st_yo = store_fm_tile("gp", out_name, yo, lambda c: c * WIN, w * WIN, WIN)
            p.barrier()

    if BU >= 8:
        ffn_layer(0, "xn1h", "yBh")
    if BU >= 9:
        ffn_layer(1, "yBh", "y2h")

    w0nT = sbuf("w0nT", 128, NCH * D, BF16)
    w1nT = sbuf("w1nT", 128, NCH * D, BF16)
    if BU >= 10:
      with contextlib.ExitStack() as ph:
        wtf = ph.enter_context(nc.sbuf_tensor("p9wt", [128, D], F32))
        gtf = ph.enter_context(nc.sbuf_tensor("p9gt", [128, D], F32))
        grow = ph.enter_context(nc.sbuf_tensor("p9grow", [2, 1], F32))
        grow2 = ph.enter_context(nc.sbuf_tensor("p9grow2", [1, 2], F32))
        bgs = ph.enter_context(nc.sbuf_tensor("p9bgs", [2, 1], F32))
        amul = ph.enter_context(nc.sbuf_tensor("p9amul", [128, 2], F32))
        ps9 = ph.enter_context(nc.psum_tensor("p9ps", [128, 2], F32))
        p.wait("gp", "cc", 1)
        p.wait("sy", "cc", 1)
        p.dma("gp", grow[:, :], hb["cc_out"][2048:2050, 0:1], allow_slow_non_contiguous=True)
        p.dma("gp", bgs[:, :], hb["l1_bg"][:, :])
        p.wait("se", "d_gp", p.cnt["d_gp"])
        p.op("se", "activation", grow[:, :], grow[:, :], AF.Copy,
             scale=cpool[0:2, C_MEAN:C_MEAN + 1])
        p.wait("ve", "se", p.cnt["se"])
        p.op("ve", "tensor_tensor", grow[:, :], grow[:, :], bgs[:, :], ALU.add)
        p.wait("se", "ve", p.cnt["ve"])
        p.op("se", "activation", grow[:, :], grow[:, :], AF.Sigmoid)
        p.wait("te", "se", p.cnt["se"])
        p.op("te", "transpose", ps9[0:1, 0:2], grow[0:2, 0:1], ident[0:2, 0:2], inc=True)
        p.wait("ve", "te", p.cnt["te"])
        p.op("ve", "tensor_copy", grow2[:, :], ps9[0:1, 0:2])
        p.wait("te", "ve", p.cnt["ve"])
        p.op("te", "matmul", ps9[:, :], ones1f[0:1, :], grow2[0:1, :],
             start=True, stop=True, inc=True)
        p.wait("ve", "te", p.cnt["te"])
        p.op("ve", "tensor_scalar", amul[:, 0:1], ps9[:, 0:1], -1.0, 1.0,
             ALU.mult, ALU.add)
        p.op("ve", "tensor_copy", amul[:, 1:2], ps9[:, 1:2])
        p.vsync()
        for (g_row0, wt_name, dst, sc) in ((0, "l1_w0T", w0nT, None),
                                           (1024, "l1_w1T", w1nT, SCALE2S)):
            for c in range(NCH):
                p.dma("sy", wtf[:, :], hb[wt_name][c * 128:(c + 1) * 128, :])
                ldg = p.dma("sy", gtf[:, :],
                            hb["cc_out"][g_row0 + c * 128:g_row0 + (c + 1) * 128, :])
                p.wait("ve", "d_sy", ldg)
                if sc is not None:
                    p.op("ve", "tensor_scalar", gtf[:, :], gtf[:, :], sc, None, ALU.mult)
                p.op("ve", "tensor_scalar", gtf[:, :], gtf[:, :], -1.0, 1.0,
                     ALU.max, ALU.min)
                p.op("ve", "tensor_scalar", gtf[:, :], gtf[:, :], amul[:, 1:2], None,
                     ALU.mult)
                p.op("ve", "tensor_scalar", wtf[:, :], wtf[:, :], amul[:, 0:1], None,
                     ALU.mult)
                p.op("ve", "tensor_tensor", dst[:, c * D:(c + 1) * D], wtf[:, :],
                     gtf[:, :], ALU.subtract)
                p.wait("sy", "ve", p.cnt["ve"])
        if "w0n" in debug_want:
            p.wait_compute("gp")
            for c in range(NCH):
                p.dma("gp", tap("w0n", D, D, BF16)[c * 128:(c + 1) * 128, :],
                      w0nT[:, c * D:(c + 1) * D])
                p.dma("gp", tap("w1n", D, D, BF16)[c * 128:(c + 1) * 128, :],
                      w1nT[:, c * D:(c + 1) * D])
            p.dma("gp", tap("amul", 128, 2)[:, :], amul[:, :])
        p.barrier()

    if BU >= 11:
      with contextlib.ExitStack() as ph:
        wqT = ph.enter_context(nc.sbuf_tensor("pTwq", [128, NCH * D], BF16))
        woT = ph.enter_context(nc.sbuf_tensor("pTwo", [128, NCH * D], BF16))
        xnt = [ph.enter_context(nc.sbuf_tensor(f"pTx{i}", [128, NCH * TT], BF16))
               for i in range(2)]
        y2t = [ph.enter_context(nc.sbuf_tensor(f"pTy{i}", [128, NCH * TT], F32))
               for i in range(2)]
        x1t = [ph.enter_context(nc.sbuf_tensor(f"pTx1{i}", [128, NCH * TT], F32))
               for i in range(2)]
        qt = ph.enter_context(nc.sbuf_tensor("pTq", [128, NCH * TT], BF16))
        t1t = ph.enter_context(nc.sbuf_tensor("pTt1", [128, NCH * TT], BF16))
        t2t = ph.enter_context(nc.sbuf_tensor("pTt2", [128, NCH * TT], BF16))
        gt = ph.enter_context(nc.sbuf_tensor("pTg", [128, TT], F32))
        ot = ph.enter_context(nc.sbuf_tensor("pTo", [128, NCH * TT], F32))
        psX = [ph.enter_context(nc.psum_tensor(f"pTps{i}", [128, TT], F32))
               for i in range(2)]
        load_w("gp", wqT, "l1_wqT", D)
        load_w("gp", woT, "l1_woutT", D)
        p.wait("te", "d_gp", p.cnt["d_gp"])

        def ldtile(t, buf):
            load_fm_tile("sy", xnt[buf], lambda c: c * TT, "xn1h", t * TT, TT)
            load_fm_tile("sy", y2t[buf], lambda c: c * TT, "y2h", t * TT, TT)
            load_fm_tile("sy", x1t[buf], lambda c: c * TT, "x1h", t * TT, TT)
            return p.cnt["d_sy"]

        lds = [ldtile(0, 0)]
        st_o = 0
        consX = [(0, 0), (0, 0)]
        for t in range(NTILE):
            if t + 1 < NTILE:
                p.wait_compute("sy")
                lds.append(ldtile(t + 1, (t + 1) % 2))
            xn, y2, x1 = xnt[t % 2], y2t[t % 2], x1t[t % 2]
            p.wait("te", "d_sy", lds[t])

            def proj(wsb, rhs, dst, fn):
                cons = {0: consX[0], 1: consX[1]}
                p.wait("te", "se", p.cnt["se"])
                for oc in range(NCH):
                    par = oc % 2
                    PP = psX[par]
                    p.wait("te", "se", cons[par][0])
                    p.wait("te", "ve", cons[par][1])
                    for dc in range(NCH):
                        p.op("te", "matmul", PP[:, 0:TT],
                             wsb[:, dc * D + oc * 128:dc * D + oc * 128 + 128],
                             rhs[:, dc * TT:(dc + 1) * TT],
                             start=(dc == 0), stop=(dc == NCH - 1), inc=(dc == NCH - 1))
                    p.wait("se", "te", p.cnt["te"])
                    p.op("se", "activation", dst[:, oc * TT:(oc + 1) * TT], PP[:, 0:TT],
                         fn)
                    cons[par] = (p.cnt["se"], cons[par][1])
                consX[0], consX[1] = cons[0], cons[1]
            proj(wqT, xn, qt, AF.Copy)
            proj(w0nT, qt, t1t, AF.Silu)
            proj(w1nT, t1t, t2t, AF.Copy)
            consF = dict(enumerate([consX[0], consX[1]]))
            p.wait("te", "se", p.cnt["se"])
            for oc in range(NCH):
                par = oc % 2
                PP = psX[par]
                p.wait("te", "se", consF[par][0])
                p.wait("te", "ve", consF[par][1])
                for dc in range(NCH):
                    p.op("te", "matmul", PP[:, 0:TT],
                         woT[:, dc * D + oc * 128:dc * D + oc * 128 + 128],
                         t2t[:, dc * TT:(dc + 1) * TT],
                         start=(dc == 0), stop=(dc == NCH - 1), inc=(dc == NCH - 1))
                p.wait("se", "d_sy", lds[t])
                p.wait("se", "ve", p.cnt["ve"])
                p.op("se", "activation", gt[:, 0:TT], y2[:, oc * TT:(oc + 1) * TT],
                     AF.Silu)
                p.wait("ve", "te", p.cnt["te"])
                p.wait("ve", "se", p.cnt["se"])
                p.wait("ve", "d_gp", st_o)
                p.op("ve", "tensor_tensor", ot[:, oc * TT:(oc + 1) * TT], PP[:, 0:TT],
                     gt[:, 0:TT], ALU.mult)
                p.op("ve", "tensor_tensor", ot[:, oc * TT:(oc + 1) * TT],
                     ot[:, oc * TT:(oc + 1) * TT], x1[:, oc * TT:(oc + 1) * TT], ALU.add)
                consF[par] = (p.cnt["se"], p.cnt["ve"])
            consX[0], consX[1] = consF[0], consF[1]
            p.wait("gp", "ve", p.cnt["ve"])
            for c in range(NCH):
                p.dma("gp", out_fm[c * 128:(c + 1) * 128, t * TT:(t + 1) * TT],
                      ot[:, c * TT:(c + 1) * TT])
            if "qt" in debug_want:
                p.wait_compute("gp")
                for c in range(NCH):
                    p.dma("gp", tap("qt", D, S, BF16)[c * 128:(c + 1) * 128, t * TT:(t + 1) * TT],
                          qt[:, c * TT:(c + 1) * TT])
                    p.dma("gp", tap("t1t", D, S, BF16)[c * 128:(c + 1) * 128, t * TT:(t + 1) * TT],
                          t1t[:, c * TT:(c + 1) * TT])
                    p.dma("gp", tap("t2t", D, S, BF16)[c * 128:(c + 1) * 128, t * TT:(t + 1) * TT],
                          t2t[:, c * TT:(c + 1) * TT])
            st_o = p.cnt["d_gp"]
        p.wait("gp", "d_gp", p.cnt["d_gp"])
        p.wait("sy", "d_gp", p.cnt["d_gp"])

    scr_shapes = {"x1h": (D, S, F32), "xn1h": (D, S, BF16), "kfmh": (D, S, BF16),
                  "vfmh": (D, S, BF16), "a1tmh": (S, D, BF16), "d2tmh": (S, D, BF16),
                  "d1tmh": (S, D, BF16), "ktmh": (S, D, BF16), "yAh": (D, S, F32),
                  "yBh": (D, S, F32), "y2h": (D, S, F32), "cc_out": (2050, D, F32),
                  "tabC": (128, S, F32), "tabS": (128, S, F32),
                  "tabCi": (128, S, F32), "tabSi": (128, S, F32),
                  "acth": (NJ * 128, S, BF16)}
    p.barrier()
    if "cc_out" in debug_want:
        p.wait("gp", "cc", p.cnt["cc"])
    for nm in debug_want:
        if nm in scr_shapes:
            r_, c_, dt_ = scr_shapes[nm]
            t = tap(nm, r_, c_, dt_)
            for rr in range(0, r_, 1024):
                p.dma("gp", t[rr:min(rr + 1024, r_), :],
                      hb[nm][rr:min(rr + 1024, r_), :])
    p.wait("gp", "d_gp", p.cnt["d_gp"])

    with nc.Block() as blk:
        p.finalize(blk)
    es.close()
    return nc


def kernel(**inputs):
    hp = _host_prep(inputs)
    x = np.asarray(inputs["L0_attn_query"], dtype=np.float32)
    nc = build_nc(tuple(DEBUG_WANT))
    in_maps = []
    for b in range(NCORE):
        m = {"x_fm": np.ascontiguousarray(x[b].T)}
        m.update(hp)
        in_maps.append(m)
    res = run_bass_kernel_spmd(nc, in_maps, core_ids=list(range(NCORE)))
    out = np.stack([np.asarray(r["out_fm"]).T for r in res.results], axis=0)
    if DEBUG_WANT:
        kernel.last_taps = [{k: np.asarray(v) for k, v in r.items() if k.startswith("tap_")}
                            for r in res.results]
    return out.astype(np.float32)


# version 12
# speedup vs baseline: 1.3388x; 1.0145x over previous
import math
import contextlib
import numpy as np

import concourse.bass as bass
import concourse.mybir as mybir
from concourse.bass_utils import run_bass_kernel_spmd

F32 = mybir.dt.float32
BF16 = mybir.dt.bfloat16
I32 = mybir.dt.int32
AF = mybir.ActivationFunctionType
ALU = mybir.AluOpType

NCORE = 8
S = 4096
D = 1024
NCH = 8
TT = 512
NTILE = S // TT
H0, DH0 = 16, 64
FH, WIN, INNER = 8, 512, 512
HID = 2730
W1OUT = 5460
NJ = 22
CHJ = [128] * 21 + [42]
SCALE2S = 2.0 / 4096.0
NEG_BIG = -1.0e38

DEBUG_WANT = []
DEBUG_TAPS = []
BUILD_UNTIL = 99
FFN_SUBS = "AB12"
FFN_ASEC = "all"

ENGS = ("te", "ve", "se", "gp", "sy")
EATTR = {"te": "tensor", "ve": "vector", "se": "scalar", "gp": "gpsimd", "sy": "sync"}


class P:
    def __init__(self, nc, es):
        self.nc = nc
        self.q = {e: [] for e in ENGS}
        self.cnt = {}
        self.sem = {}
        self.last_wait = {}
        for k in ("te", "ve", "se", "gp", "d_sy", "d_gp", "d_se", "d_ve", "cc"):
            self.sem[k] = es.enter_context(nc.semaphore(k))
            self.cnt[k] = 0

    def op(self, eng, meth, *a, inc=True, **kw):
        if inc:
            self.cnt[eng] += 1
        def f(E, meth=meth, a=a, kw=kw, inc=inc, sem=self.sem[eng]):
            r = getattr(E, meth)(*a, **kw)
            if inc:
                r.then_inc(sem, 1)
        self.q[eng].append(f)
        return self.cnt[eng]

    def dma(self, qeng, out, in_, **kw):
        key = "d_" + qeng
        self.cnt[key] += 16
        def f(E, out=out, in_=in_, kw=kw, sem=self.sem[key]):
            E.dma_start(out=out, in_=in_, **kw).then_inc(sem, 16)
        self.q[qeng].append(f)
        return self.cnt[key]

    def ccop(self, *a, **kw):
        self.cnt["cc"] += 1
        def f(E, a=a, kw=kw, sem=self.sem["cc"]):
            E.collective_compute(*a, **kw).then_inc(sem, 1)
        self.q["gp"].append(f)
        return self.cnt["cc"]

    def wait(self, eng, key, val):
        if val <= 0:
            return
        if val <= self.last_wait.get((eng, key), -1):
            return
        self.last_wait[(eng, key)] = val
        def f(E, sem=self.sem[key], val=val):
            E.wait_ge(sem, val)
        self.q[eng].append(f)

    def vsync(self, eng="ve"):
        self.wait(eng, eng, self.cnt[eng])

    def wait_compute(self, eng):
        for k in ("te", "ve", "se", "gp"):
            self.wait(eng, k, self.cnt[k])

    def wait_all(self, eng):
        for k in self.sem:
            if k != "cc":
                self.wait(eng, k, self.cnt[k])

    def barrier(self):
        for e in ENGS:
            self.wait_all(e)

    def finalize(self, blk):
        for e in ENGS:
            fns = self.q[e]
            def runner(E, fns=fns):
                for f in fns:
                    f(E)
            getattr(blk, EATTR[e])(runner)


def _host_prep(I):
    f = np.float32
    hp = {}
    hp["l0_wkv"] = np.ascontiguousarray(
        np.concatenate([np.asarray(I["L0_attn_Wk"]), np.asarray(I["L0_attn_Wv"])], axis=2)
        .transpose(1, 0, 2).reshape(DH0, H0 * 128)).astype(f)
    hp["l0_wq"] = np.ascontiguousarray(
        np.asarray(I["L0_attn_Wq"]).transpose(1, 0, 2).reshape(DH0, H0 * DH0)).astype(f)
    hp["l0_wo"] = np.ascontiguousarray(I["L0_attn_Wo"]).astype(f)
    hp["l0_bwT"] = np.ascontiguousarray(np.asarray(I["L0_ffn_base_w"]).T).astype(f)
    hp["l0_bb"] = np.ascontiguousarray(np.asarray(I["L0_ffn_base_b"]).reshape(NCH, 128).T).astype(f)
    hp["l0_rwT"] = np.ascontiguousarray(np.asarray(I["L0_ffn_router_w"]).T).astype(f)
    hp["l0_rb"] = np.ascontiguousarray(np.asarray(I["L0_ffn_router_b"]).reshape(8, 1)).astype(f)
    hp["l0_lA"] = np.ascontiguousarray(I["L0_ffn_lora_A"]).astype(f)
    hp["l0_lB"] = np.ascontiguousarray(np.asarray(I["L0_ffn_lora_B"]).reshape(64, D)).astype(f)
    for nm, k in [("l1_wkT", "L1_attn_Wk"), ("l1_wvT", "L1_attn_Wv"),
                  ("l1_wqT", "L1_attn_Wq"), ("l1_woutT", "L1_attn_Wout"),
                  ("l1_w0T", "L1_attn_W0"), ("l1_w1T", "L1_attn_W1")]:
        hp[nm] = np.ascontiguousarray(np.asarray(I[k]).T).astype(f)
    hp["l1_w1n"] = np.ascontiguousarray(I["L1_attn_W1"]).astype(f)
    hp["l1_wg"] = np.ascontiguousarray(
        np.concatenate([I["L1_attn_Wg_decay"], I["L1_attn_Wg_lr"]], axis=0).T).astype(f)
    hp["l1_bg"] = np.ascontiguousarray(
        np.concatenate([I["L1_attn_bg_decay"], I["L1_attn_bg_lr"]]).reshape(2, 1)).astype(f)
    hp["f_wqkv"] = np.ascontiguousarray(np.asarray(I["L1_ffn_wqkv"]).reshape(2 * D, 3 * INNER)).astype(f)
    hp["f_wo"] = np.ascontiguousarray(np.asarray(I["L1_ffn_wo"]).reshape(2 * INNER, D)).astype(f)
    hp["f_w1"] = np.ascontiguousarray(np.asarray(I["L1_ffn_w1"]).reshape(2 * D, W1OUT)).astype(f)
    hp["f_w2"] = np.ascontiguousarray(np.asarray(I["L1_ffn_w2"]).reshape(2 * HID, D)).astype(f)
    hp["f_qs"] = np.ascontiguousarray(np.tile(np.asarray(I["L1_ffn_q_scale"]), (1, 2)).T).astype(f)
    hp["f_ks"] = np.ascontiguousarray(np.tile(np.asarray(I["L1_ffn_k_scale"]), (1, 2)).T).astype(f)
    hp["f_lng"] = np.ascontiguousarray(np.concatenate(
        [np.asarray(I["L1_ffn_ln_g"])[l].reshape(NCH, 128).T for l in range(2)], axis=1)).astype(f)
    hp["f_lnb"] = np.ascontiguousarray(np.concatenate(
        [np.asarray(I["L1_ffn_ln_b"])[l].reshape(NCH, 128).T for l in range(2)], axis=1)).astype(f)
    b1 = np.zeros((128, 2 * 44), dtype=f)
    for l in range(2):
        ap_ = np.asarray(I["L1_ffn_b1"])[l][:HID]
        gp_ = np.asarray(I["L1_ffn_b1"])[l][HID:]
        for j in range(NJ):
            m = CHJ[j]
            b1[:m, l * 44 + j] = ap_[j * 128:j * 128 + m]
            b1[:m, l * 44 + 22 + j] = gp_[j * 128:j * 128 + m]
    hp["f_b1"] = np.ascontiguousarray(b1)
    hp["f_b2"] = np.ascontiguousarray(np.concatenate(
        [np.asarray(I["L1_ffn_b2"])[l].reshape(NCH, 128).T for l in range(2)], axis=1)).astype(f)
    hp["c_ident"] = np.eye(128, dtype=f)
    R = np.zeros((64, 64), dtype=f)
    R[0:32, 32:64] = -np.eye(32)
    R[32:64, 0:32] = np.eye(32)
    hp["c_rt"] = np.ascontiguousarray(np.kron(np.eye(2, dtype=f), R.T))
    h2 = np.zeros((128, 2), dtype=f)
    h2[0:64, 0] = 1.0
    h2[64:128, 1] = 1.0
    hp["c_h2"] = h2
    hp["c_b2t"] = np.ascontiguousarray(h2.T)
    g8 = np.zeros((8, 64), dtype=f)
    t8 = np.zeros((8, 64), dtype=f)
    for m in range(64):
        g8[m // 8, m] = 1.0
        t8[m % 8, m] = 1.0
    hp["c_g8"] = g8
    hp["c_t8"] = t8
    iq = np.arange(128)[:, None]
    jj = np.arange(128)[None, :]
    hp["c_tlo"] = np.where(jj < iq, NEG_BIG, 0.0).astype(f)
    hp["c_thi"] = np.where(jj > iq, NEG_BIG, 0.0).astype(f)
    hp["c_j32"] = np.ascontiguousarray((np.arange(128) % 32).reshape(128, 1)).astype(f)
    return hp


def build_nc(debug_want=(), build_until=None):
    global DEBUG_TAPS
    DEBUG_TAPS = []
    BU = build_until if build_until is not None else BUILD_UNTIL
    nc = bass.Bass("TRN2", target_bir_lowering=False)
    es = contextlib.ExitStack()
    hb = {}

    def din(name, shape, dt=F32):
        hb[name] = nc.dram_tensor(name, list(shape), dt, kind="ExternalInput")

    def dscr(name, shape, dt=F32):
        hb[name] = nc.dram_tensor(name, list(shape), dt)

    din("x_fm", (D, S))
    for nm, shp in [("l0_wkv", (64, 2048)), ("l0_wq", (64, 1024)), ("l0_wo", (D, D)),
                    ("l0_bwT", (D, D)), ("l0_bb", (128, 8)), ("l0_rwT", (D, 8)),
                    ("l0_rb", (8, 1)), ("l0_lA", (D, 8)), ("l0_lB", (64, D)),
                    ("l1_wkT", (D, D)), ("l1_wvT", (D, D)), ("l1_wqT", (D, D)),
                    ("l1_woutT", (D, D)), ("l1_w0T", (D, D)), ("l1_w1T", (D, D)),
                    ("l1_w1n", (D, D)), ("l1_wg", (D, 2)), ("l1_bg", (2, 1)),
                    ("f_wqkv", (2 * D, 3 * INNER)), ("f_wo", (2 * INNER, D)),
                    ("f_w1", (2 * D, W1OUT)), ("f_w2", (2 * HID, D)),
                    ("f_qs", (128, 2)), ("f_ks", (128, 2)),
                    ("f_lng", (128, 16)), ("f_lnb", (128, 16)),
                    ("f_b1", (128, 88)), ("f_b2", (128, 16)),
                    ("c_ident", (128, 128)), ("c_rt", (128, 128)),
                    ("c_h2", (128, 2)), ("c_b2t", (2, 128)),
                    ("c_g8", (8, 64)), ("c_t8", (8, 64)),
                    ("c_tlo", (128, 128)), ("c_thi", (128, 128)),
                    ("c_j32", (128, 1))]:
        din(nm, shp)
    out_fm = nc.dram_tensor("out_fm", [D, S], F32, kind="ExternalOutput")

    dscr("x1h", (D, S))
    dscr("xn1h", (D, S), BF16)
    dscr("kfmh", (D, S), BF16)
    dscr("vfmh", (D, S), BF16)
    dscr("a1tmh", (S, D), BF16)
    dscr("d2tmh", (S, D), BF16)
    dscr("d1tmh", (S, D), BF16)
    dscr("ktmh", (S, D), BF16)
    dscr("tabC", (128, S))
    dscr("tabS", (128, S))
    dscr("tabCi", (128, S))
    dscr("tabSi", (128, S))
    dscr("yAh", (D, S))
    dscr("yBh", (D, S))
    dscr("y2h", (D, S))
    dscr("acth", (NJ * 128, S), BF16)
    dscr("cc_in", (2050, D))
    dscr("cc_out", (2050, D))

    taps = {}
    def tap(name, rows, cols, dt=F32):
        if name in taps:
            return taps[name]
        t = nc.dram_tensor("tap_" + name, [rows, cols], dt, kind="ExternalOutput")
        taps[name] = t
        DEBUG_TAPS.append("tap_" + name)
        return t

    p = P(nc, es)

    def sbuf(name, part, free, dt):
        return es.enter_context(nc.sbuf_tensor(name, [part, free], dt))

    ones_b = sbuf("ones_b", 128, 1, BF16)
    ones_f = sbuf("ones_f", 128, 1, F32)
    ones1f = sbuf("ones1f", 1, 128, F32)
    ident = sbuf("ident", 128, 128, F32)
    identb = sbuf("identb", 128, 128, BF16)
    cpool = sbuf("cpool", 128, 8, F32)
    C_EPSLN, C_EPSL0, C_IVF, C_LNS, C_LNB, C_MEAN = 0, 1, 2, 3, 4, 5
    gacc = sbuf("gacc", 2, 1, F32)

    p.op("gp", "memset", ones_b[:, :], 1.0)
    p.op("gp", "memset", ones_f[:, :], 1.0)
    p.op("gp", "memset", ones1f[:, :], 1.0)
    p.op("gp", "memset", cpool[:, C_EPSLN:C_EPSLN + 1], 1e-5)
    p.op("gp", "memset", cpool[:, C_EPSL0:C_EPSL0 + 1], 1e-6)
    p.op("gp", "memset", cpool[:, C_IVF:C_IVF + 1], float(-math.log(10000.0) / 32.0))
    p.op("gp", "memset", cpool[:, C_LNS:C_LNS + 1], float(2.0 / 89.6))
    p.op("gp", "memset", cpool[:, C_LNB:C_LNB + 1], float(25.6 / 89.6))
    p.op("gp", "memset", cpool[:, C_MEAN:C_MEAN + 1], float(1.0 / 32768.0))
    p.op("gp", "memset", gacc[:, :], 0.0)
    p.dma("sy", ident[:, :], hb["c_ident"][:, :])
    p.dma("gp", identb[:, :], hb["c_ident"][:, :])
    p.barrier()

    def load_fm_tile(qeng, dst, dstcols, hbmT, t0, width):
        for c in range(NCH):
            p.dma(qeng, dst[:, dstcols(c):dstcols(c) + width],
                  hb[hbmT][c * 128:(c + 1) * 128, t0:t0 + width])
        return p.cnt["d_" + qeng]

    def store_fm_tile(qeng, hbmT, src, srccols, t0, width):
        for c in range(NCH):
            p.dma(qeng, hb[hbmT][c * 128:(c + 1) * 128, t0:t0 + width],
                  src[:, srccols(c):srccols(c) + width])
        return p.cnt["d_" + qeng]

    def load_w(qeng, dst, hbm_name, K, row0=0, nrows=D):
        for c in range(nrows // 128):
            p.dma(qeng, dst[:, c * K:(c + 1) * K],
                  hb[hbm_name][row0 + c * 128:row0 + (c + 1) * 128, 0:K])
        return p.cnt["d_" + qeng]

    def emit_ln(src_c, dst_c, W, sqb, xbb, tmpf, srow, psS, psQ, psA, psB,
                lg_col=None, lb_col=None):
        for c in range(NCH):
            p.op("se", "activation", sqb[:, c * W:(c + 1) * W], src_c(c), AF.Square)
            p.op("se", "activation", xbb[:, c * W:(c + 1) * W], src_c(c), AF.Copy)
        p.wait("te", "se", p.cnt["se"])
        p.wait("te", "ve", p.cnt["ve"])
        for c in range(NCH):
            p.op("te", "matmul", psS[0:1, 0:W], ones_b[:, :], xbb[:, c * W:(c + 1) * W],
                 start=(c == 0), stop=(c == NCH - 1), inc=(c == NCH - 1))
        for c in range(NCH):
            p.op("te", "matmul", psQ[0:1, 0:W], ones_b[:, :], sqb[:, c * W:(c + 1) * W],
                 start=(c == 0), stop=(c == NCH - 1), inc=(c == NCH - 1))
        p.wait("ve", "te", p.cnt["te"])
        m_, q2 = srow[0:1, 0:W], srow[0:1, W:2 * W]
        a_, b_ = srow[0:1, 2 * W:3 * W], srow[0:1, 3 * W:4 * W]
        p.op("ve", "tensor_scalar", m_, psS[0:1, 0:W], 1.0 / D, None, ALU.mult)
        p.op("ve", "tensor_scalar", q2, psQ[0:1, 0:W], 1.0 / D, None, ALU.mult)
        p.op("ve", "tensor_tensor", a_, m_, m_, ALU.mult)
        p.op("ve", "tensor_tensor", q2, q2, a_, ALU.subtract)
        p.wait("se", "ve", p.cnt["ve"])
        p.op("se", "activation", a_, q2, AF.Sqrt, bias=cpool[0:1, C_EPSLN:C_EPSLN + 1])
        p.wait("ve", "se", p.cnt["se"])
        p.op("ve", "reciprocal", a_, a_)
        p.op("ve", "scalar_tensor_tensor", b_, m_, -1.0, a_, ALU.mult, ALU.mult)
        p.wait("te", "ve", p.cnt["ve"])
        p.op("te", "matmul", psA[:, 0:W], ones1f[0:1, :], a_, start=True, stop=True)
        p.op("te", "matmul", psB[:, 0:W], ones1f[0:1, :], b_, start=True, stop=True)
        p.wait("ve", "te", p.cnt["te"])
        for c in range(NCH):
            p.op("ve", "tensor_tensor", tmpf[:, 0:W], src_c(c), psA[:, 0:W], ALU.mult)
            if lg_col is None:
                p.op("ve", "tensor_tensor", dst_c(c), tmpf[:, 0:W], psB[:, 0:W], ALU.add)
            else:
                p.op("ve", "tensor_tensor", tmpf[:, 0:W], tmpf[:, 0:W], psB[:, 0:W], ALU.add)
                p.op("ve", "tensor_scalar", dst_c(c), tmpf[:, 0:W],
                     lg_col[:, c:c + 1], lb_col[:, c:c + 1], ALU.mult, ALU.add)
        return p.cnt["ve"]

    if BU >= 1:
      with contextlib.ExitStack() as ph:
        tA = ph.enter_context(nc.sbuf_tensor("tA", [128, S], F32))
        tB = ph.enter_context(nc.sbuf_tensor("tB", [128, S], F32))
        tC = ph.enter_context(nc.sbuf_tensor("tC", [128, S], F32))
        tD = ph.enter_context(nc.sbuf_tensor("tD", [128, S], F32))
        tI = ph.enter_context(nc.sbuf_tensor("tI", [128, S], I32))
        col = ph.enter_context(nc.sbuf_tensor("colv", [128, 4], F32))
        inv2pi = float(1.0 / (2.0 * math.pi))
        twopi = float(2.0 * math.pi)
        p.op("gp", "iota", tD[:, :], [[1, S]], channel_multiplier=0,
             allow_small_or_imprecise_dtypes=True)
        ldj = p.dma("sy", col[:, 0:1], hb["c_j32"][:, :])
        p.wait("se", "d_sy", ldj)
        p.wait("se", "gp", p.cnt["gp"])
        p.wait("ve", "gp", p.cnt["gp"])
        p.wait("ve", "d_sy", ldj)
        p.op("se", "activation", col[:, 1:2], col[:, 0:1], AF.Exp,
             scale=cpool[:, C_IVF:C_IVF + 1])
        p.op("se", "activation", col[:, 2:3], col[:, 0:1], AF.Ln,
             scale=cpool[:, C_LNS:C_LNS + 1], bias=cpool[:, C_LNB:C_LNB + 1])
        p.wait("ve", "se", p.cnt["se"])
        p.op("ve", "tensor_scalar", col[:, 2:3], col[:, 2:3], 1.0 / 256.0, None, ALU.mult)
        p.op("ve", "tensor_scalar", tA[:, :], tD[:, :], 2048.0, None, ALU.subtract)
        p.op("ve", "tensor_scalar", tA[:, :], tA[:, :], col[:, 2:3], None, ALU.mult)
        p.wait("se", "ve", p.cnt["ve"])
        p.op("se", "activation", tA[:, :], tA[:, :], AF.Exp)
        p.op("se", "activation", tD[:, :], tD[:, :], AF.Copy, scale=col[:, 1:2])
        p.wait("ve", "se", p.cnt["se"])

        def reduced_sin(dst, shift):
            if shift:
                p.op("ve", "tensor_scalar", tB[:, :], tD[:, :], float(shift), inv2pi,
                     ALU.add, ALU.mult)
            else:
                p.op("ve", "tensor_scalar", tB[:, :], tD[:, :], inv2pi, None, ALU.mult)
            p.op("ve", "tensor_copy", tI[:, :], tB[:, :])
            p.op("ve", "tensor_copy", tB[:, :], tI[:, :])
            p.op("ve", "scalar_tensor_tensor", tB[:, :], tB[:, :], -twopi, tD[:, :],
                 ALU.mult, ALU.add)
            if shift:
                p.op("ve", "tensor_scalar", tB[:, :], tB[:, :], float(shift), None,
                     ALU.add)
            p.op("ve", "tensor_scalar", tB[:, :], tB[:, :], float(-np.pi), float(np.pi),
                 ALU.max, ALU.min)
            p.wait("se", "ve", p.cnt["ve"])
            p.op("se", "activation", dst, tB[:, :], AF.Sin)
            p.wait("ve", "se", p.cnt["se"])

        reduced_sin(tC[:, :], 0.0)
        p.op("ve", "tensor_tensor", tB[:, :], tC[:, :], tA[:, :], ALU.mult)
        p.wait("gp", "ve", p.cnt["ve"])
        st = p.dma("gp", hb["tabS"][:, :], tB[:, :])
        p.wait("ve", "d_gp", st)
        p.op("ve", "reciprocal", tB[:, :], tA[:, :])
        p.op("ve", "tensor_tensor", tC[:, :], tC[:, :], tB[:, :], ALU.mult)
        p.wait("gp", "ve", p.cnt["ve"])
        st = p.dma("gp", hb["tabSi"][:, :], tC[:, :])
        p.wait("ve", "d_gp", st)
        reduced_sin(tC[:, :], math.pi / 2.0)
        p.op("ve", "tensor_tensor", tB[:, :], tC[:, :], tA[:, :], ALU.mult)
        p.wait("gp", "ve", p.cnt["ve"])
        st = p.dma("gp", hb["tabC"][:, :], tB[:, :])
        p.wait("ve", "d_gp", st)
        p.op("ve", "reciprocal", tB[:, :], tA[:, :])
        p.op("ve", "tensor_tensor", tC[:, :], tC[:, :], tB[:, :], ALU.mult)
        p.wait("gp", "ve", p.cnt["ve"])
        p.dma("gp", hb["tabCi"][:, :], tC[:, :])
        p.barrier()

    if BU >= 2:
      with contextlib.ExitStack() as ph0:
        xn0 = ph0.enter_context(nc.sbuf_tensor("xn0", [128, NCH * S], BF16))
        kv_sb = ph0.enter_context(nc.sbuf_tensor("kv_sb", [64, H0 * 64], BF16))
        z_col = ph0.enter_context(nc.sbuf_tensor("z_col", [64, H0], BF16))

        with contextlib.ExitStack() as ph:
            xt = [ph.enter_context(nc.sbuf_tensor(f"p0x{i}", [128, NCH * TT], F32))
                  for i in range(2)]
            sqb = ph.enter_context(nc.sbuf_tensor("p0sq", [128, NCH * TT], BF16))
            xbb = ph.enter_context(nc.sbuf_tensor("p0xb", [128, NCH * TT], BF16))
            tmpf = ph.enter_context(nc.sbuf_tensor("p0tmp", [128, TT], F32))
            srow = ph.enter_context(nc.sbuf_tensor("p0srow", [1, 4 * TT], F32))
            psS = ph.enter_context(nc.psum_tensor("p0psS", [1, TT], F32))
            psQ = ph.enter_context(nc.psum_tensor("p0psQ", [1, TT], F32))
            psA = ph.enter_context(nc.psum_tensor("p0psA", [128, TT], F32))
            psB = ph.enter_context(nc.psum_tensor("p0psB", [128, TT], F32))
            lds = [load_fm_tile("sy", xt[0], lambda c: c * TT, "x_fm", 0, TT)]
            for t in range(NTILE):
                if t + 1 < NTILE:
                    p.wait_compute("sy")
                    lds.append(load_fm_tile("sy", xt[(t + 1) % 2], lambda c: c * TT,
                                            "x_fm", (t + 1) * TT, TT))
                x = xt[t % 2]
                p.wait("se", "d_sy", lds[t])
                p.wait("ve", "d_sy", lds[t])
                emit_ln(lambda c, x=x: x[:, c * TT:(c + 1) * TT],
                        lambda c, t=t: xn0[:, c * S + t * TT:c * S + t * TT + TT],
                        TT, sqb, xbb, tmpf, srow, psS, psQ, psA, psB)
            if "xn0" in debug_want:
                tt = tap("xn0", D, S, BF16)
                p.wait_compute("gp")
                for c in range(NCH):
                    p.dma("gp", tt[c * 128:(c + 1) * 128, :], xn0[:, c * S:(c + 1) * S])
            p.barrier()

        if BU >= 3:
          with contextlib.ExitStack() as ph:
            wkv = ph.enter_context(nc.sbuf_tensor("p1wkv", [128, H0 * 128], BF16))
            kf_fm = ph.enter_context(nc.sbuf_tensor("p1kf", [64, 2 * TT], BF16))
            v_fm = ph.enter_context(nc.sbuf_tensor("p1vf", [64, 2 * TT], BF16))
            kfp_tm = ph.enter_context(nc.sbuf_tensor("p1kfp", [128, 4 * 128], BF16))
            vp_tm = ph.enter_context(nc.sbuf_tensor("p1vp", [128, 4 * 128], BF16))
            tmf = ph.enter_context(nc.sbuf_tensor("p1tmf", [64, 2 * TT], F32))
            tme = ph.enter_context(nc.sbuf_tensor("p1tme", [64, 2 * TT], F32))
            zpart = ph.enter_context(nc.sbuf_tensor("p1zp", [64, 2], F32))
            zcf = ph.enter_context(nc.sbuf_tensor("p1zcf", [64, H0], F32))
            kv_acc = ph.enter_context(nc.sbuf_tensor("p1kvacc", [128, 1024], F32))
            kvb_st = ph.enter_context(nc.sbuf_tensor("p1kvst", [128, 1024], BF16))
            Pa = ph.enter_context(nc.psum_tensor("p1Pa", [64, TT], F32))
            Pb = ph.enter_context(nc.psum_tensor("p1Pb", [64, TT], F32))
            Pc = ph.enter_context(nc.psum_tensor("p1Pc", [64, TT], F32))
            Pd = ph.enter_context(nc.psum_tensor("p1Pd", [64, TT], F32))
            PT0 = ph.enter_context(nc.psum_tensor("p1PT0", [128, 128], BF16))
            PT1 = ph.enter_context(nc.psum_tensor("p1PT1", [128, 128], BF16))
            KVP = [ph.enter_context(nc.psum_tensor(f"p1KVP{i}", [128, 128], F32))
                   for i in range(2)]
            p.dma("gp", wkv[0:64, :], hb["l0_wkv"][:, :])
            ld = p.dma("gp", wkv[64:128, :], hb["l0_wkv"][:, :])
            p.wait("te", "d_gp", ld)
            p.op("gp", "memset", kv_acc[:, :], 0.0)
            p.op("gp", "memset", zcf[:, :], 0.0)
            p.wait("ve", "gp", p.cnt["gp"])
            for t in range(NTILE):
                tc0 = t * TT
                for hp_ in range(8):
                    h0, h1 = 2 * hp_, 2 * hp_ + 1
                    p.wait("te", "ve", p.cnt["ve"])
                    p.wait("te", "se", p.cnt["se"])
                    for (PP, h, kvoff) in ((Pa, h0, 0), (Pb, h0, 64),
                                           (Pc, h1, 0), (Pd, h1, 64)):
                        b = (h % 2) * 64
                        p.op("te", "matmul", PP[:, :],
                             wkv[b:b + 64, h * 128 + kvoff:h * 128 + kvoff + 64],
                             xn0[b:b + 64, hp_ * S + tc0:hp_ * S + tc0 + TT],
                             start=True, stop=True, inc=True)
                    p.wait("ve", "te", p.cnt["te"])
                    p.op("ve", "tensor_scalar", tmf[:, 0:TT], Pa[:, :], 0.0, None,
                         ALU.min)
                    p.op("ve", "tensor_scalar", tmf[:, TT:2 * TT], Pc[:, :], 0.0, None,
                         ALU.min)
                    p.op("ve", "tensor_copy", v_fm[:, 0:TT], Pb[:, :])
                    p.op("ve", "tensor_copy", v_fm[:, TT:2 * TT], Pd[:, :])
                    p.wait("se", "ve", p.cnt["ve"])
                    p.op("se", "activation", tme[:, :], tmf[:, :], AF.Exp)
                    p.wait("ve", "se", p.cnt["se"])
                    p.op("ve", "scalar_tensor_tensor", kf_fm[:, 0:TT], Pa[:, :], 0.0,
                         tme[:, 0:TT], ALU.max, ALU.add)
                    p.op("ve", "scalar_tensor_tensor", kf_fm[:, TT:2 * TT], Pc[:, :],
                         0.0, tme[:, TT:2 * TT], ALU.max, ALU.add)
                    p.op("ve", "reduce_sum", zpart[:, 0:1], kf_fm[:, 0:TT],
                         axis=mybir.AxisListType.X)
                    p.op("ve", "reduce_sum", zpart[:, 1:2], kf_fm[:, TT:2 * TT],
                         axis=mybir.AxisListType.X)
                    p.vsync()
                    p.op("ve", "tensor_tensor", zcf[:, h0:h0 + 1], zcf[:, h0:h0 + 1],
                         zpart[:, 0:1], ALU.add)
                    p.op("ve", "tensor_tensor", zcf[:, h1:h1 + 1], zcf[:, h1:h1 + 1],
                         zpart[:, 1:2], ALU.add)
                    evs = p.cnt["ve"]
                    for (src, dstt) in ((kf_fm, kfp_tm), (v_fm, vp_tm)):
                        for g in range(2):
                            for bb in range(4):
                                k_ = g * 4 + bb
                                PT = PT0 if k_ % 2 == 0 else PT1
                                eng = "ve" if k_ % 2 == 0 else "se"
                                p.wait("te", "ve", p.cnt["ve"])
                                p.wait("te", eng, p.cnt[eng])
                                p.op("te", "transpose", PT[:, 0:64],
                                     src[:, g * TT + bb * 128:g * TT + (bb + 1) * 128],
                                     identb[0:64, 0:64], inc=True)
                                p.wait(eng, "te", p.cnt["te"])
                                dst_ = dstt[:, bb * 128 + g * 64:bb * 128 + g * 64 + 64]
                                if eng == "ve":
                                    p.op("ve", "tensor_copy", dst_, PT[:, 0:64])
                                else:
                                    p.op("se", "activation", dst_, PT[:, 0:64], AF.Copy)
                    p.wait("te", "ve", p.cnt["ve"])
                    p.wait("te", "se", p.cnt["se"])
                    KV = KVP[hp_ % 2]
                    for bb in range(4):
                        p.op("te", "matmul", KV[:, :],
                             kfp_tm[:, bb * 128:(bb + 1) * 128],
                             vp_tm[:, bb * 128:(bb + 1) * 128],
                             start=(bb == 0), stop=(bb == 3), inc=(bb == 3))
                    p.wait("ve", "te", p.cnt["te"])
                    p.op("ve", "tensor_tensor", kv_acc[:, hp_ * 128:(hp_ + 1) * 128],
                         kv_acc[:, hp_ * 128:(hp_ + 1) * 128], KV[:, :], ALU.add)
            p.op("ve", "tensor_copy", kvb_st[:, :], kv_acc[:, :])
            p.op("ve", "tensor_copy", z_col[:, :], zcf[:, :])
            p.wait("gp", "ve", p.cnt["ve"])
            for hp_ in range(8):
                p.dma("gp", kv_sb[0:64, (2 * hp_) * 64:(2 * hp_ + 1) * 64],
                      kvb_st[0:64, hp_ * 128:hp_ * 128 + 64])
                p.dma("gp", kv_sb[0:64, (2 * hp_ + 1) * 64:(2 * hp_ + 2) * 64],
                      kvb_st[64:128, hp_ * 128 + 64:hp_ * 128 + 128])
            p.wait("te", "d_gp", p.cnt["d_gp"])
            p.wait("ve", "d_gp", p.cnt["d_gp"])
            if "kv" in debug_want:
                p.wait_compute("gp")
                p.dma("gp", tap("kv", 64, H0 * 64, BF16)[:, :], kv_sb[:, :])
                p.dma("gp", tap("z", 64, H0, BF16)[:, :], z_col[:, :])
            p.barrier()

        if BU >= 4:
          with contextlib.ExitStack() as ph:
            wq = ph.enter_context(nc.sbuf_tensor("p2wq", [128, 1024], BF16))
            wo = ph.enter_context(nc.sbuf_tensor("p2wo", [128, NCH * D], BF16))
            bwT = ph.enter_context(nc.sbuf_tensor("p2bw", [128, NCH * D], BF16))
            rwT = ph.enter_context(nc.sbuf_tensor("p2rw", [128, NCH * 8], BF16))
            rb = ph.enter_context(nc.sbuf_tensor("p2rb", [8, 1], F32))
            lA = ph.enter_context(nc.sbuf_tensor("p2lA", [128, NCH * 8], BF16))
            lB = ph.enter_context(nc.sbuf_tensor("p2lB", [64, D], BF16))
            bb = ph.enter_context(nc.sbuf_tensor("p2bb", [128, 8], F32))
            g8 = ph.enter_context(nc.sbuf_tensor("p2g8", [8, 64], BF16))
            t8 = ph.enter_context(nc.sbuf_tensor("p2t8", [8, 64], BF16))
            qf = ph.enter_context(nc.sbuf_tensor("p2qf", [64, 2 * TT], BF16))
            rden = ph.enter_context(nc.sbuf_tensor("p2rden", [1, TT], F32))
            ctx = ph.enter_context(nc.sbuf_tensor("p2ctx", [128, NCH * TT], BF16))
            s_sb = ph.enter_context(nc.sbuf_tensor("p2s", [128, NCH * TT], F32))
            g_sb = ph.enter_context(nc.sbuf_tensor("p2g", [128, NCH * TT], F32))
            lgt = ph.enter_context(nc.sbuf_tensor("p2lgt", [8, TT], F32))
            gt_tm = ph.enter_context(nc.sbuf_tensor("p2gtm", [128, 4 * 8], BF16))
            gate_fm = ph.enter_context(nc.sbuf_tensor("p2gfm", [8, TT], BF16))
            mid_sb = ph.enter_context(nc.sbuf_tensor("p2mid", [8, TT], BF16))
            gmid = ph.enter_context(nc.sbuf_tensor("p2gmid", [64, TT], BF16))
            xt = ph.enter_context(nc.sbuf_tensor("p2x", [128, NCH * TT], F32))
            x1t = xt
            xn1t = ph.enter_context(nc.sbuf_tensor("p2xn1", [128, NCH * TT], BF16))
            sqb = ph.enter_context(nc.sbuf_tensor("p2sq", [128, NCH * TT], BF16))
            xbb = ph.enter_context(nc.sbuf_tensor("p2xb", [128, NCH * TT], BF16))
            tmpf = ph.enter_context(nc.sbuf_tensor("p2tmp", [128, TT], F32))
            tmpf2 = ph.enter_context(nc.sbuf_tensor("p2tmp2", [64, 2 * TT], F32))
            srow = ph.enter_context(nc.sbuf_tensor("p2srow", [1, 4 * TT], F32))
            rowm = ph.enter_context(nc.sbuf_tensor("p2rowm", [128, 1], F32))
            PA = ph.enter_context(nc.psum_tensor("p2PA", [128, 1024], F32))
            PB = ph.enter_context(nc.psum_tensor("p2PB", [128, 512], F32))
            PC = ph.enter_context(nc.psum_tensor("p2PC", [128, 512], F32))
            PD = ph.enter_context(nc.psum_tensor("p2PD", [128, 512], F32))
            PE = ph.enter_context(nc.psum_tensor("p2PE", [1, 512], F32))
            PF = ph.enter_context(nc.psum_tensor("p2PF", [1, 512], F32))
            PG = ph.enter_context(nc.psum_tensor("p2PG", [8, 512], BF16))
            p.dma("gp", wq[0:64, :], hb["l0_wq"][:, :])
            p.dma("gp", wq[64:128, :], hb["l0_wq"][:, :])
            load_w("gp", wo, "l0_wo", D)
            load_w("gp", bwT, "l0_bwT", D)
            load_w("gp", rwT, "l0_rwT", 8)
            p.dma("sy", rb[:, :], hb["l0_rb"][:, :])
            load_w("gp", lA, "l0_lA", 8)
            p.dma("gp", lB[:, :], hb["l0_lB"][:, :])
            p.dma("sy", bb[:, :], hb["l0_bb"][:, :])
            p.dma("gp", g8[:, :], hb["c_g8"][:, :])
            ld = p.dma("gp", t8[:, :], hb["c_t8"][:, :])
            p.wait("te", "d_gp", ld)
            p.wait("ve", "d_gp", ld)
            p.wait("te", "d_sy", p.cnt["d_sy"])
            p.wait("ve", "d_sy", p.cnt["d_sy"])
            p.op("ve", "tensor_scalar", lB[:, :], lB[:, :], 2.0, None, ALU.mult)
            st_x1 = 0
            consS = {0: 0, 1: 0}
            for t in range(NTILE):
                tc0 = t * TT
                p.wait_compute("sy")
                p.wait("sy", "d_gp", st_x1)
                ldx = load_fm_tile("sy", xt, lambda c: c * TT, "x_fm", tc0, TT)
                for h in range(H0):
                    b = (h % 2) * 64
                    g = h % 2
                    if g == 0:
                        p.wait("te", "ve", p.cnt["ve"])
                        p.wait("te", "se", p.cnt["se"])
                        for g2 in range(2):
                            h2_ = h + g2
                            b2 = (h2_ % 2) * 64
                            p.op("te", "matmul", PA[0:64, g2 * TT:(g2 + 1) * TT],
                                 wq[b2:b2 + 64, h2_ * 64:(h2_ + 1) * 64],
                                 xn0[b2:b2 + 64, (h2_ // 2) * S + tc0:(h2_ // 2) * S + tc0 + TT],
                                 start=True, stop=True, inc=(g2 == 1))
                        p.wait("ve", "te", p.cnt["te"])
                        p.op("ve", "tensor_scalar", tmpf2[:, :], PA[0:64, 0:2 * TT],
                             0.0, None, ALU.min)
                        p.wait("se", "ve", p.cnt["ve"])
                        p.op("se", "activation", tmpf2[:, :], tmpf2[:, :], AF.Exp)
                        p.wait("ve", "se", p.cnt["se"])
                        p.op("ve", "scalar_tensor_tensor", qf[:, :],
                             PA[0:64, 0:2 * TT], 0.0, tmpf2[:, :], ALU.max, ALU.add)
                    p.wait("te", "ve", p.cnt["ve"])
                    p.op("te", "matmul", PC[0:64, 0:TT], kv_sb[:, h * 64:(h + 1) * 64],
                         qf[:, g * TT:(g + 1) * TT], start=True, stop=True)
                    p.op("te", "matmul", PE[0:1, 0:TT], z_col[:, h:h + 1],
                         qf[:, g * TT:(g + 1) * TT], start=True, stop=True, inc=True)
                    p.wait("ve", "te", p.cnt["te"])
                    p.op("ve", "tensor_scalar", rden[0:1, 0:TT], PE[0:1, 0:TT],
                         1e-6, None, ALU.add)
                    p.op("ve", "reciprocal", rden[0:1, 0:TT], rden[0:1, 0:TT])
                    p.wait("te", "ve", p.cnt["ve"])
                    p.op("te", "matmul", PD[0:64, 0:TT], ones1f[0:1, 0:64],
                         rden[0:1, 0:TT], start=True, stop=True, inc=True)
                    p.wait("ve", "te", p.cnt["te"])
                    p.op("ve", "tensor_copy", tmpf2[:, 0:TT], PC[0:64, 0:TT])
                    p.op("ve", "tensor_tensor",
                         ctx[b:b + 64, (h // 2) * TT:(h // 2) * TT + TT],
                         tmpf2[:, 0:TT], PD[0:64, 0:TT], ALU.mult)
                ctx_done = p.cnt["ve"]
                for oc in range(NCH):
                    par = oc % 2
                    PP = PB if par == 0 else PA
                    p.wait("te", "ve", ctx_done)
                    p.wait("te", "se", consS[par])
                    for ec in range(NCH):
                        p.op("te", "matmul", PP[:, 0:TT],
                             wo[:, ec * D + oc * 128:ec * D + oc * 128 + 128],
                             ctx[:, ec * TT:(ec + 1) * TT],
                             start=(ec == 0), stop=(ec == NCH - 1), inc=(ec == NCH - 1))
                    p.wait("se", "te", p.cnt["te"])
                    p.op("se", "activation", s_sb[:, oc * TT:(oc + 1) * TT], PP[:, 0:TT],
                         AF.Copy)
                    consS[par] = p.cnt["se"]
                p.wait("te", "se", p.cnt["se"])
                p.wait("te", "ve", p.cnt["ve"])
                for dc in range(NCH):
                    p.op("te", "matmul", PC[0:8, 0:TT], rwT[:, dc * 8:(dc + 1) * 8],
                         xn0[:, dc * S + tc0:dc * S + tc0 + TT],
                         start=(dc == 0), stop=(dc == NCH - 1), inc=(dc == NCH - 1))
                p.wait("ve", "te", p.cnt["te"])
                p.op("ve", "tensor_scalar", lgt[:, :], PC[0:8, 0:TT], rb[:, 0:1], None,
                     ALU.add)
                for j in range(4):
                    p.wait("te", "ve", p.cnt["ve"])
                    p.wait("te", "se", p.cnt["se"])
                    p.op("te", "transpose", PD[:, 0:8],
                         lgt[0:8, j * 128:(j + 1) * 128], ident[0:8, 0:8], inc=True)
                    p.wait("ve", "te", p.cnt["te"])
                    p.op("ve", "reduce_max", rowm[:, 0:1], PD[:, 0:8],
                         axis=mybir.AxisListType.X, negate=True)
                    p.wait("se", "ve", p.cnt["ve"])
                    p.op("se", "activation", tmpf[:, 0:8], PD[:, 0:8],
                         AF.Exp, bias=rowm[:, 0:1])
                    p.wait("ve", "se", p.cnt["se"])
                    p.op("ve", "reduce_sum", rowm[:, 0:1], tmpf[:, 0:8],
                         axis=mybir.AxisListType.X)
                    p.vsync()
                    p.op("ve", "reciprocal", rowm[:, 0:1], rowm[:, 0:1])
                    p.vsync()
                    p.op("ve", "tensor_scalar", gt_tm[:, j * 8:(j + 1) * 8],
                         tmpf[:, 0:8], rowm[:, 0:1], None, ALU.mult)
                for j in range(4):
                    p.wait("te", "ve", p.cnt["ve"])
                    p.op("te", "transpose", PG[0:8, 0:128],
                         gt_tm[:, j * 8:(j + 1) * 8], identb[:, 0:128], inc=True)
                    p.wait("ve", "te", p.cnt["te"])
                    p.op("ve", "tensor_copy", gate_fm[:, j * 128:(j + 1) * 128],
                         PG[0:8, 0:128])
                p.wait("te", "ve", p.cnt["ve"])
                for dc in range(NCH):
                    p.op("te", "matmul", PD[0:8, 0:TT], lA[:, dc * 8:(dc + 1) * 8],
                         xn0[:, dc * S + tc0:dc * S + tc0 + TT],
                         start=(dc == 0), stop=(dc == NCH - 1), inc=(dc == NCH - 1))
                p.wait("ve", "te", p.cnt["te"])
                p.op("ve", "tensor_copy", mid_sb[:, :], PD[0:8, 0:TT])
                p.wait("te", "ve", p.cnt["ve"])
                p.op("te", "matmul", PA[0:64, 0:TT], g8[:, :], gate_fm[:, :],
                     start=True, stop=True)
                p.op("te", "matmul", PA[0:64, TT:2 * TT], t8[:, :], mid_sb[:, :],
                     start=True, stop=True, inc=True)
                p.wait("ve", "te", p.cnt["te"])
                p.op("ve", "tensor_copy", tmpf2[:, 0:TT], PA[0:64, 0:TT])
                p.op("ve", "tensor_tensor", gmid[:, :], tmpf2[:, 0:TT],
                     PA[0:64, TT:2 * TT], ALU.mult)
                gmid_done = p.cnt["ve"]
                for oc in range(NCH):
                    par = oc % 2
                    PP = PB if par == 0 else PA
                    p.wait("te", "ve", gmid_done)
                    p.wait("te", "se", consS[par])
                    for dc in range(NCH):
                        p.op("te", "matmul", PP[:, 0:TT],
                             bwT[:, dc * D + oc * 128:dc * D + oc * 128 + 128],
                             xn0[:, dc * S + tc0:dc * S + tc0 + TT],
                             start=(dc == 0), stop=False)
                    p.op("te", "matmul", PP[:, 0:TT], lB[:, oc * 128:(oc + 1) * 128],
                         gmid[:, :], start=False, stop=True, inc=True)
                    p.wait("se", "te", p.cnt["te"])
                    p.op("se", "activation", g_sb[:, oc * TT:(oc + 1) * TT], PP[:, 0:TT],
                         AF.Silu, bias=bb[:, oc:oc + 1])
                    consS[par] = p.cnt["se"]
                p.wait("ve", "d_sy", ldx)
                p.wait("ve", "se", p.cnt["se"])
                p.wait("ve", "d_gp", st_x1)
                for c in range(NCH):
                    p.op("ve", "tensor_tensor", tmpf[:, 0:TT], s_sb[:, c * TT:(c + 1) * TT],
                         g_sb[:, c * TT:(c + 1) * TT], ALU.mult)
                    p.op("ve", "tensor_tensor", x1t[:, c * TT:(c + 1) * TT],
                         tmpf[:, 0:TT], xt[:, c * TT:(c + 1) * TT], ALU.add)
                emit_ln(lambda c: x1t[:, c * TT:(c + 1) * TT],
                        lambda c: xn1t[:, c * TT:(c + 1) * TT],
                        TT, sqb, xbb, tmpf, srow, PE, PF, PC, PD)
                p.wait("gp", "ve", p.cnt["ve"])
                store_fm_tile("gp", "x1h", x1t, lambda c: c * TT, tc0, TT)
                st_x1 = store_fm_tile("gp", "xn1h", xn1t, lambda c: c * TT, tc0, TT)
                if "gate_l0" in debug_want:
                    p.wait_compute("gp")
                    p.dma("gp", tap("gate_l0", 8, S, BF16)[:, tc0:tc0 + TT], gate_fm[:, :])
                    p.dma("gp", tap("mid_l0", 8, S, BF16)[:, tc0:tc0 + TT], mid_sb[:, :])
                    p.dma("gp", tap("gmid_l0", 64, S, BF16)[:, tc0:tc0 + TT], gmid[:, :])
                    p.dma("gp", tap("lgt_l0", 8, S)[:, tc0:tc0 + TT], lgt[:, :])
                if "s0" in debug_want:
                    for c in range(NCH):
                        p.dma("gp", tap("s0", D, S)[c * 128:(c + 1) * 128, tc0:tc0 + TT],
                              s_sb[:, c * TT:(c + 1) * TT])
                if "g0" in debug_want:
                    for c in range(NCH):
                        p.dma("gp", tap("g0", D, S)[c * 128:(c + 1) * 128, tc0:tc0 + TT],
                              g_sb[:, c * TT:(c + 1) * TT])
            p.barrier()

    if BU >= 5:
      with contextlib.ExitStack() as ph:
        wk = ph.enter_context(nc.sbuf_tensor("p3wk", [128, NCH * D], BF16))
        wv = ph.enter_context(nc.sbuf_tensor("p3wv", [128, NCH * D], BF16))
        wg = ph.enter_context(nc.sbuf_tensor("p3wg", [128, NCH * 2], BF16))
        xnt = [ph.enter_context(nc.sbuf_tensor(f"p3x{i}", [128, NCH * TT], BF16))
               for i in range(2)]
        kt = ph.enter_context(nc.sbuf_tensor("p3k", [128, NCH * TT], BF16))
        vt = ph.enter_context(nc.sbuf_tensor("p3v", [128, NCH * TT], BF16))
        grow = ph.enter_context(nc.sbuf_tensor("p3grow", [2, 1], F32))
        P3A = ph.enter_context(nc.psum_tensor("p3A", [128, 512], F32))
        P3B = ph.enter_context(nc.psum_tensor("p3B", [128, 512], F32))
        P3C = ph.enter_context(nc.psum_tensor("p3C", [2, 512], F32))
        load_w("gp", wk, "l1_wkT", D)
        load_w("gp", wv, "l1_wvT", D)
        ld = load_w("gp", wg, "l1_wg", 2)
        p.wait("te", "d_gp", ld)
        lds = [load_fm_tile("sy", xnt[0], lambda c: c * TT, "xn1h", 0, TT)]
        st_done = 0
        cons3 = {0: 0, 1: 0}
        for t in range(NTILE):
            if t + 1 < NTILE:
                p.wait_compute("sy")
                lds.append(load_fm_tile("sy", xnt[(t + 1) % 2], lambda c: c * TT,
                                        "xn1h", (t + 1) * TT, TT))
            xn = xnt[t % 2]
            p.wait("te", "d_sy", lds[t])
            p.wait("se", "d_gp", st_done)
            for wsb, dst in ((wk, kt), (wv, vt)):
                for oc in range(NCH):
                    par = oc % 2
                    PP = P3A if par == 0 else P3B
                    p.wait("te", "se", cons3[par])
                    for dc in range(NCH):
                        p.op("te", "matmul", PP[:, 0:TT],
                             wsb[:, dc * D + oc * 128:dc * D + oc * 128 + 128],
                             xn[:, dc * TT:(dc + 1) * TT],
                             start=(dc == 0), stop=(dc == NCH - 1), inc=(dc == NCH - 1))
                    p.wait("se", "te", p.cnt["te"])
                    p.op("se", "activation", dst[:, oc * TT:(oc + 1) * TT], PP[:, 0:TT],
                         AF.Copy)
                    cons3[par] = p.cnt["se"]
            p.wait("te", "ve", p.cnt["ve"])
            for dc in range(NCH):
                p.op("te", "matmul", P3C[0:2, 0:TT], wg[:, dc * 2:(dc + 1) * 2],
                     xn[:, dc * TT:(dc + 1) * TT],
                     start=(dc == 0), stop=(dc == NCH - 1), inc=(dc == NCH - 1))
            p.wait("ve", "te", p.cnt["te"])
            p.op("ve", "reduce_sum", grow[:, 0:1], P3C[0:2, 0:TT],
                 axis=mybir.AxisListType.X)
            p.vsync()
            p.op("ve", "tensor_tensor", gacc[:, :], gacc[:, :], grow[:, :], ALU.add)
            p.wait("gp", "se", p.cnt["se"])
            store_fm_tile("gp", "kfmh", kt, lambda c: c * TT, t * TT, TT)
            st_done = store_fm_tile("gp", "vfmh", vt, lambda c: c * TT, t * TT, TT)
        p.barrier()

    if BU >= 6:
      with contextlib.ExitStack() as ph:
        w0T = ph.enter_context(nc.sbuf_tensor("p4w0", [128, NCH * D], BF16))
        w1T = ph.enter_context(nc.sbuf_tensor("p4w1", [128, NCH * D], BF16))
        w1n = ph.enter_context(nc.sbuf_tensor("p4w1n", [128, NCH * D], BF16))
        ktf = [ph.enter_context(nc.sbuf_tensor(f"p4k{i}", [128, NCH * TT], BF16))
               for i in range(2)]
        vtf = [ph.enter_context(nc.sbuf_tensor(f"p4v{i}", [128, NCH * TT], BF16))
               for i in range(2)]
        a1t = ph.enter_context(nc.sbuf_tensor("p4a1", [128, NCH * TT], BF16))
        d2t = ph.enter_context(nc.sbuf_tensor("p4d2", [128, NCH * TT], BF16))
        d1t = ph.enter_context(nc.sbuf_tensor("p4d1", [128, NCH * TT], BF16))
        sdt = ph.enter_context(nc.sbuf_tensor("p4sd", [128, NCH * TT], F32))
        sgt = ph.enter_context(nc.sbuf_tensor("p4sg", [128, 2 * TT], F32))
        tmpa = ph.enter_context(nc.sbuf_tensor("p4tmpa", [128, 2 * TT], F32))
        tmtile = ph.enter_context(nc.sbuf_tensor("p4tm", [128, 4 * D], BF16))
        P4A = ph.enter_context(nc.psum_tensor("p4A", [128, 512], F32))
        P4B = ph.enter_context(nc.psum_tensor("p4B", [128, 512], F32))
        P4C = ph.enter_context(nc.psum_tensor("p4C", [128, 512], F32))
        P4T = ph.enter_context(nc.psum_tensor("p4T", [128, 128], BF16))
        P4T2 = ph.enter_context(nc.psum_tensor("p4T2", [128, 128], BF16))
        P4A2 = ph.enter_context(nc.psum_tensor("p4A2", [128, 512], F32))
        P4B2 = ph.enter_context(nc.psum_tensor("p4B2", [128, 512], F32))
        P4C2 = ph.enter_context(nc.psum_tensor("p4C2", [128, 512], F32))
        load_w("gp", w0T, "l1_w0T", D)
        load_w("gp", w1T, "l1_w1T", D)
        ld = load_w("gp", w1n, "l1_w1n", D)
        p.wait("te", "d_gp", ld)
        st_tm = [0]

        def transpose_out(src_b, hbm_name, t):
            for oc in range(NCH):
                p.wait("ve", "d_gp", st_tm[0])
                p.wait("se", "d_gp", st_tm[0])
                for s_ in range(4):
                    PT = P4T if s_ % 2 == 0 else P4T2
                    eng = "ve" if s_ % 2 == 0 else "se"
                    p.wait("te", eng, p.cnt[eng])
                    p.op("te", "transpose", PT[:, :],
                         src_b[:, oc * TT + s_ * 128:oc * TT + (s_ + 1) * 128],
                         identb[:, :], inc=True)
                    p.wait(eng, "te", p.cnt["te"])
                    dst_ = tmtile[:, s_ * D + oc * 128:s_ * D + oc * 128 + 128]
                    if eng == "ve":
                        p.op("ve", "tensor_copy", dst_, PT[:, :])
                    else:
                        p.op("se", "activation", dst_, PT[:, :], AF.Copy)
            p.wait("gp", "ve", p.cnt["ve"])
            p.wait("gp", "se", p.cnt["se"])
            for s_ in range(4):
                p.dma("gp", hb[hbm_name][t * TT + s_ * 128:t * TT + (s_ + 1) * 128, :],
                      tmtile[:, s_ * D:(s_ + 1) * D])
            st_tm[0] = p.cnt["d_gp"]

        lds = [(load_fm_tile("sy", ktf[0], lambda c: c * TT, "kfmh", 0, TT),
                load_fm_tile("sy", vtf[0], lambda c: c * TT, "vfmh", 0, TT))]
        consA = {0: (0, 0), 1: (0, 0)}
        consB = {0: 0, 1: 0}
        consC = {0: 0, 1: 0}
        for t in range(NTILE):
            if t + 1 < NTILE:
                p.wait_compute("sy")
                lds.append((load_fm_tile("sy", ktf[(t + 1) % 2], lambda c: c * TT,
                                         "kfmh", (t + 1) * TT, TT),
                            load_fm_tile("sy", vtf[(t + 1) % 2], lambda c: c * TT,
                                         "vfmh", (t + 1) * TT, TT)))
            kt, vt = ktf[t % 2], vtf[t % 2]
            p.wait("te", "d_sy", lds[t][1])
            for oc in range(NCH):
                par = oc % 2
                PP = P4A if par == 0 else P4A2
                sg_ = sgt[:, 0:TT] if par == 0 else sgt[:, TT:2 * TT]
                tm_ = tmpa[:, 0:TT] if par == 0 else tmpa[:, TT:2 * TT]
                p.wait("te", "se", consA[par][0])
                p.wait("te", "ve", consA[par][1])
                for dc in range(NCH):
                    p.op("te", "matmul", PP[:, 0:TT],
                         w0T[:, dc * D + oc * 128:dc * D + oc * 128 + 128],
                         kt[:, dc * TT:(dc + 1) * TT],
                         start=(dc == 0), stop=(dc == NCH - 1), inc=(dc == NCH - 1))
                p.wait("se", "te", p.cnt["te"])
                p.op("se", "activation", a1t[:, oc * TT:(oc + 1) * TT], PP[:, 0:TT],
                     AF.Silu)
                p.op("se", "activation", sg_, PP[:, 0:TT], AF.Sigmoid)
                p.wait("ve", "se", p.cnt["se"])
                p.op("ve", "tensor_tensor", tm_, sg_,
                     a1t[:, oc * TT:(oc + 1) * TT], ALU.mult)
                p.op("ve", "tensor_tensor", sdt[:, oc * TT:(oc + 1) * TT], sg_,
                     a1t[:, oc * TT:(oc + 1) * TT], ALU.add)
                p.op("ve", "tensor_tensor", sdt[:, oc * TT:(oc + 1) * TT],
                     sdt[:, oc * TT:(oc + 1) * TT], tm_, ALU.subtract)
                consA[par] = (p.cnt["se"], p.cnt["ve"])
            p.wait("te", "se", p.cnt["se"])
            for oc in range(NCH):
                par = oc % 2
                PP = P4B if par == 0 else P4B2
                tm_ = tmpa[:, 0:TT] if par == 0 else tmpa[:, TT:2 * TT]
                p.wait("te", "ve", consB[par])
                for ec in range(NCH):
                    p.op("te", "matmul", PP[:, 0:TT],
                         w1T[:, ec * D + oc * 128:ec * D + oc * 128 + 128],
                         a1t[:, ec * TT:(ec + 1) * TT],
                         start=(ec == 0), stop=(ec == NCH - 1), inc=(ec == NCH - 1))
                p.wait("ve", "te", p.cnt["te"])
                p.op("ve", "tensor_tensor", tm_, PP[:, 0:TT],
                     vt[:, oc * TT:(oc + 1) * TT], ALU.subtract)
                p.op("ve", "tensor_scalar", d2t[:, oc * TT:(oc + 1) * TT], tm_,
                     -5.0, 5.0, ALU.max, ALU.min)
                consB[par] = p.cnt["ve"]
            p.wait("te", "ve", p.cnt["ve"])
            for oc in range(NCH):
                par = oc % 2
                PP = P4C if par == 0 else P4C2
                p.wait("te", "ve", consC[par])
                for dc in range(NCH):
                    p.op("te", "matmul", PP[:, 0:TT],
                         w1n[:, dc * D + oc * 128:dc * D + oc * 128 + 128],
                         d2t[:, dc * TT:(dc + 1) * TT],
                         start=(dc == 0), stop=(dc == NCH - 1), inc=(dc == NCH - 1))
                p.wait("ve", "te", p.cnt["te"])
                p.op("ve", "scalar_tensor_tensor", d1t[:, oc * TT:(oc + 1) * TT],
                     PP[:, 0:TT], SCALE2S, sdt[:, oc * TT:(oc + 1) * TT],
                     ALU.mult, ALU.mult)
                consC[par] = p.cnt["ve"]
            transpose_out(a1t, "a1tmh", t)
            transpose_out(d2t, "d2tmh", t)
            transpose_out(d1t, "d1tmh", t)
            transpose_out(kt, "ktmh", t)
        p.barrier()

    if BU >= 7:
      with contextlib.ExitStack() as ph:
        G0T = ph.enter_context(nc.sbuf_tensor("G0T", [128, NCH * D], F32))
        G1T = ph.enter_context(nc.sbuf_tensor("G1T", [128, NCH * D], F32))
        rhsb = [ph.enter_context(nc.sbuf_tensor(f"p5r{i}", [128, D], BF16))
                for i in range(2)]
        lhsb = [ph.enter_context(nc.sbuf_tensor(f"p5l{i}", [128, 512], BF16))
                for i in range(2)]
        GP = [ph.enter_context(nc.psum_tensor(f"p5G{j}", [128, D], F32))
              for j in range(4)]
        NB = S // 128
        for (lhs_name, rhs_name, dst) in (("a1tmh", "d2tmh", G1T),
                                          ("ktmh", "d1tmh", G0T)):
            for half in range(2):
                lds = [(p.dma("sy", lhsb[0][:, :],
                              hb[lhs_name][0:128, half * 512:(half + 1) * 512]),
                        p.dma("sy", rhsb[0][:, :], hb[rhs_name][0:128, :]))]
                for tb in range(NB):
                    if tb + 1 < NB:
                        p.wait_compute("sy")
                        lds.append((
                            p.dma("sy", lhsb[(tb + 1) % 2][:, :],
                                  hb[lhs_name][(tb + 1) * 128:(tb + 2) * 128,
                                               half * 512:(half + 1) * 512]),
                            p.dma("sy", rhsb[(tb + 1) % 2][:, :],
                                  hb[rhs_name][(tb + 1) * 128:(tb + 2) * 128, :])))
                    p.wait("te", "d_sy", lds[tb][1])
                    for j in range(4):
                        for g in range(2):
                            p.op("te", "matmul", GP[j][:, g * 512:(g + 1) * 512],
                                 lhsb[tb % 2][:, j * 128:(j + 1) * 128],
                                 rhsb[tb % 2][:, g * 512:(g + 1) * 512],
                                 start=(tb == 0), stop=(tb == NB - 1),
                                 inc=(j == 3 and g == 1),
                                 skip_group_check=True)
                p.wait("ve", "te", p.cnt["te"])
                for j in range(4):
                    p.op("ve", "tensor_copy",
                         dst[:, (half * 4 + j) * D:(half * 4 + j + 1) * D], GP[j][:, :])
                p.wait("te", "ve", p.cnt["ve"])
        p.wait("gp", "ve", p.cnt["ve"])
        for c in range(NCH):
            p.dma("gp", hb["cc_in"][c * 128:(c + 1) * 128, :], G0T[:, c * D:(c + 1) * D])
            p.dma("gp", hb["cc_in"][1024 + c * 128:1024 + (c + 1) * 128, :],
                  G1T[:, c * D:(c + 1) * D])
        p.dma("gp", hb["cc_in"][2048:2050, 0:1], gacc[0:2, 0:1], allow_slow_non_contiguous=True)
        p.wait("gp", "d_gp", p.cnt["d_gp"])
        p.ccop("AllReduce", ALU.add, replica_groups=[list(range(NCORE))],
               ins=[hb["cc_in"].ap()], outs=[hb["cc_out"].ap()])
        p.barrier()

    def ffn_layer(l, in_name, out_name):
        sfx = str(l)
        subs = FFN_SUBS
        if "A" in subs:
          with contextlib.ExitStack() as ph:
            wqkv = ph.enter_context(nc.sbuf_tensor("pAwqkv" + sfx, [128, NCH * 3 * INNER], BF16))
            wo = ph.enter_context(nc.sbuf_tensor("pAwo" + sfx, [128, 4 * D], BF16))
            qsv = ph.enter_context(nc.sbuf_tensor("pAqs" + sfx, [128, 2], F32))
            ksv = ph.enter_context(nc.sbuf_tensor("pAks" + sfx, [128, 1], F32))
            h2b = ph.enter_context(nc.sbuf_tensor("pAh2" + sfx, [128, 2], BF16))
            b2t = ph.enter_context(nc.sbuf_tensor("pAb2t" + sfx, [2, 128], F32))
            rtp = ph.enter_context(nc.sbuf_tensor("pArt" + sfx, [128, 128], F32))
            tlo = ph.enter_context(nc.sbuf_tensor("pAtlo" + sfx, [128, 128], F32))
            thi = ph.enter_context(nc.sbuf_tensor("pAthi" + sfx, [128, 128], F32))
            ywm = ph.enter_context(nc.sbuf_tensor("pAywm" + sfx, [128, NCH * WIN], BF16))
            ywr = ph.enter_context(nc.sbuf_tensor("pAywr" + sfx, [128, NCH * WIN], F32))
            hbot = ph.enter_context(nc.sbuf_tensor("pAhbot" + sfx, [128, 4 * WIN], BF16))
            qf32 = ph.enter_context(nc.sbuf_tensor("pAqf" + sfx, [128, 4 * WIN], F32))
            kf32 = ph.enter_context(nc.sbuf_tensor("pAkf" + sfx, [128, 4 * WIN], F32))
            vbf = ph.enter_context(nc.sbuf_tensor("pAvbf" + sfx, [128, WIN], BF16))
            sqb = ph.enter_context(nc.sbuf_tensor("pAsq" + sfx, [128, WIN], BF16))
            nrow = ph.enter_context(nc.sbuf_tensor("pAnrow" + sfx, [2, WIN], F32))
            tm1 = ph.enter_context(nc.sbuf_tensor("pAtm1" + sfx, [128, WIN], F32))
            tm2 = ph.enter_context(nc.sbuf_tensor("pAtm2" + sfx, [128, WIN], F32))
            tabs = ph.enter_context(nc.sbuf_tensor("pAtabs" + sfx, [128, 4 * WIN], F32))
            qrot = ph.enter_context(nc.sbuf_tensor("pAqrot" + sfx, [128, 4 * WIN], BF16))
            krot = [ph.enter_context(nc.sbuf_tensor(f"pAkrot{i}" + sfx, [128, 4 * WIN], BF16))
                    for i in range(2)]
            vtm = [ph.enter_context(nc.sbuf_tensor(f"pAvtm{i}" + sfx, [128, 4 * WIN], BF16))
                   for i in range(2)]
            attn2_ = [ph.enter_context(nc.sbuf_tensor(f"pAattn{i}" + sfx, [128, 1024], BF16))
                      for i in range(2)]
            attnT2_ = [ph.enter_context(nc.sbuf_tensor(f"pAattnT{i}" + sfx, [128, 640], BF16))
                       for i in range(2)]
            rowm = ph.enter_context(nc.sbuf_tensor("pArowm" + sfx, [128, 4], F32))
            ofm = ph.enter_context(nc.sbuf_tensor("pAofm" + sfx, [128, 4 * WIN], BF16))
            yA = ph.enter_context(nc.sbuf_tensor("pAyA" + sfx, [128, NCH * WIN], F32))
            psA = ph.enter_context(nc.psum_tensor("pApsA" + sfx, [128, 512], F32))
            pscp2_ = [ph.enter_context(nc.psum_tensor(f"pAscp{i}" + sfx, [128, 1024], F32))
                      for i in range(2)]
            atpA = ph.enter_context(nc.psum_tensor("pAatpA" + sfx, [128, 128], BF16))
            atpB = ph.enter_context(nc.psum_tensor("pAatpB" + sfx, [128, 128], BF16))
            avp = ph.enter_context(nc.psum_tensor("pAavp" + sfx, [64, 128], F32))
            load_w("gp", wqkv, "f_wqkv", 3 * INNER, row0=l * D)
            load_w("gp", wo, "f_wo", D, row0=l * INNER, nrows=INNER)
            p.dma("sy", qsv[:, 0:1], hb["f_qs"][:, l:l + 1], allow_slow_non_contiguous=True)
            p.dma("sy", ksv[:, :], hb["f_ks"][:, l:l + 1], allow_slow_non_contiguous=True)
            p.dma("gp", h2b[:, :], hb["c_h2"][:, :])
            p.dma("sy", b2t[:, :], hb["c_b2t"][:, :])
            p.dma("sy", rtp[:, :], hb["c_rt"][:, :])
            p.dma("sy", tlo[:, :], hb["c_tlo"][:, :])
            ldw = p.dma("sy", thi[:, :], hb["c_thi"][:, :])
            p.wait("ve", "d_sy", ldw)
            p.wait("ve", "d_gp", p.cnt["d_gp"])
            p.wait("te", "d_gp", p.cnt["d_gp"])
            p.op("ve", "tensor_scalar", qsv[:, 1:2], qsv[:, 0:1], 8.0, None, ALU.mult)
            st_yA = 0
            for w in range(NCH):
                t0 = w * WIN
                p.wait_compute("sy")
                p.wait_compute("gp")
                load_fm_tile("gp", ywm, lambda c: c * WIN, in_name, t0, WIN)
                load_fm_tile("gp", ywr, lambda c: c * WIN, in_name, t0, WIN)
                if w == 0:
                    for c in range(4):
                        p.op("gp", "memset", hbot[:, c * WIN:c * WIN + 1], 0.0)
                    for c in range(4):
                        p.dma("gp", hbot[:, c * WIN + 1:(c + 1) * WIN],
                              hb[in_name][(4 + c) * 128:(5 + c) * 128, 0:WIN - 1])
                else:
                    for c in range(4):
                        p.dma("gp", hbot[:, c * WIN:(c + 1) * WIN],
                              hb[in_name][(4 + c) * 128:(5 + c) * 128, t0 - 1:t0 + WIN - 1])
                for ti, tn in enumerate(("tabC", "tabS", "tabCi", "tabSi")):
                    p.dma("sy", tabs[:, ti * WIN:(ti + 1) * WIN], hb[tn][:, t0:t0 + WIN])
                ld2 = p.cnt["d_sy"]
                ld2g = p.cnt["d_gp"]
                consQK = {0: 0, 1: 0}
                for oc in range(12):
                    par = oc % 2
                    PQK = psA if (par == 0 or oc >= 8) else pscp2_[0]
                    p.wait("te", "d_sy", ld2)
                    p.wait("te", "d_gp", ld2g)
                    if oc == 0:
                        p.wait("te", "ve", p.cnt["ve"])
                        p.wait("te", "se", p.cnt["se"])
                    if oc >= 8:
                        p.wait("te", "se", p.cnt["se"])
                        p.wait("te", "ve", p.cnt["ve"])
                    else:
                        p.wait("te", "se", consQK[par])
                    for dc in range(NCH):
                        rhs = (ywm[:, dc * WIN:(dc + 1) * WIN] if dc < 4 else
                               hbot[:, (dc - 4) * WIN:(dc - 3) * WIN])
                        p.op("te", "matmul", PQK[:, 0:WIN],
                             wqkv[:, dc * 3 * INNER + oc * 128:dc * 3 * INNER + oc * 128 + 128],
                             rhs, start=(dc == 0), stop=(dc == NCH - 1),
                             inc=(dc == NCH - 1))
                    p.wait("se", "te", p.cnt["te"])
                    if oc < 4:
                        p.op("se", "activation", qf32[:, oc * WIN:(oc + 1) * WIN],
                             PQK[:, 0:WIN], AF.Copy)
                        consQK[par] = p.cnt["se"]
                    elif oc < 8:
                        p.op("se", "activation", kf32[:, (oc - 4) * WIN:(oc - 3) * WIN],
                             PQK[:, 0:WIN], AF.Copy)
                        consQK[par] = p.cnt["se"]
                    else:
                        p.op("se", "activation", vbf[:, :], PQK[:, 0:WIN], AF.Copy)
                        for s_ in range(4):
                            PT = atpA if s_ % 2 == 0 else atpB
                            eng = "ve" if s_ % 2 == 0 else "se"
                            p.wait("te", "se", p.cnt["se"])
                            p.wait("te", eng, p.cnt[eng])
                            p.op("te", "transpose", PT[:, :],
                                 vbf[:, s_ * 128:(s_ + 1) * 128], identb[:, :],
                                 inc=True)
                            p.wait(eng, "te", p.cnt["te"])
                            dst_ = vtm[w % 2][:, s_ * WIN + (oc - 8) * 128:
                                              s_ * WIN + (oc - 8) * 128 + 128]
                            if eng == "ve":
                                p.op("ve", "tensor_copy", dst_, PT[:, :])
                            else:
                                p.op("se", "activation", dst_, PT[:, :], AF.Copy)
                for kq in (range(2) if FFN_ASEC in ("rot", "attn", "all") else ()):
                    src = qf32 if kq == 0 else kf32
                    dst = qrot if kq == 0 else krot[w % 2]
                    ct0 = 0 if kq == 0 else 2 * WIN
                    scl = qsv[:, 1:2] if kq == 0 else ksv[:, 0:1]
                    for c in range(4):
                        p.wait("se", "ve", p.cnt["ve"])
                        p.op("se", "activation", sqb[:, :], src[:, c * WIN:(c + 1) * WIN],
                             AF.Square)
                        p.wait("te", "se", p.cnt["se"])
                        p.op("te", "matmul", psA[0:2, 0:WIN], h2b[:, :], sqb[:, :],
                             start=True, stop=True, inc=True)
                        p.wait("se", "te", p.cnt["te"])
                        p.op("se", "activation", nrow[:, :], psA[0:2, 0:WIN], AF.Sqrt)
                        p.wait("ve", "se", p.cnt["se"])
                        p.op("ve", "tensor_scalar", nrow[:, :], nrow[:, :], 1e-12, None,
                             ALU.max)
                        p.op("ve", "reciprocal", nrow[:, :], nrow[:, :])
                        p.wait("te", "ve", p.cnt["ve"])
                        p.op("te", "matmul", psA[:, 0:WIN], b2t[:, :], nrow[:, :],
                             start=True, stop=True, inc=True)
                        p.wait("ve", "te", p.cnt["te"])
                        p.op("ve", "tensor_tensor", tm1[:, :],
                             src[:, c * WIN:(c + 1) * WIN], psA[:, 0:WIN], ALU.mult)
                        p.op("ve", "tensor_scalar", tm1[:, :], tm1[:, :], scl, None,
                             ALU.mult)
                        p.wait("te", "ve", p.cnt["ve"])
                        p.op("te", "matmul", psA[:, 0:WIN], rtp[:, :], tm1[:, :],
                             start=True, stop=True, inc=True)
                        p.wait("ve", "te", p.cnt["te"])
                        p.op("ve", "tensor_tensor", tm2[:, :], tm1[:, :],
                             tabs[:, ct0:ct0 + WIN], ALU.mult)
                        p.op("ve", "tensor_tensor", tm1[:, :], psA[:, 0:WIN],
                             tabs[:, ct0 + WIN:ct0 + 2 * WIN], ALU.mult)
                        p.op("ve", "tensor_tensor", dst[:, c * WIN:(c + 1) * WIN],
                             tm2[:, :], tm1[:, :], ALU.add)
                its = ([(h, qc) for h in range(FH) for qc in range(4)]
                       if FFN_ASEC in ("attn", "all") else [])
                s1_te = {}
                pscp_free = {0: (0, 0), 1: (0, 0)}
                avp_free = [0]

                def emit_S1(i):
                    h, qc = its[i]
                    hbase = (h % 2) * 64
                    hc = h // 2
                    lo = qc * 128
                    bhi = lo + 640 if w > 0 else lo + 128
                    par = i % 2
                    PSC = pscp2_[par]
                    p.wait("te", "ve", pscp_free[par][0])
                    p.wait("te", "se", pscp_free[par][1])
                    if i == 0:
                        p.wait("te", "ve", p.cnt["ve"])
                        p.wait("te", "se", p.cnt["se"])
                    if w > 0:
                        p.op("te", "matmul", PSC[:, lo:512],
                             qrot[hbase:hbase + 64, hc * WIN + lo:hc * WIN + lo + 128],
                             krot[(w + 1) % 2][hbase:hbase + 64, hc * WIN + lo:hc * WIN + 512],
                             start=True, stop=True)
                        p.op("te", "matmul", PSC[:, 512:bhi],
                             qrot[hbase:hbase + 64, hc * WIN + lo:hc * WIN + lo + 128],
                             krot[w % 2][hbase:hbase + 64, hc * WIN:hc * WIN + bhi - 512],
                             start=True, stop=True, inc=True)
                    else:
                        p.op("te", "matmul", PSC[:, 0:bhi],
                             qrot[hbase:hbase + 64, hc * WIN + lo:hc * WIN + lo + 128],
                             krot[0][hbase:hbase + 64, hc * WIN:hc * WIN + bhi],
                             start=True, stop=True, inc=True)
                    s1_te[i] = p.cnt["te"]

                def emit_S2(i):
                    h, qc = its[i]
                    hbase = (h % 2) * 64
                    hc = h // 2
                    lo = qc * 128
                    if w > 0:
                        blo, bhi = lo, lo + 640
                    else:
                        blo, bhi = 0, lo + 128
                    par = i % 2
                    PSC = pscp2_[par]
                    attn = attn2_[par]
                    attnT = attnT2_[par]
                    rm0 = rowm[:, 2 * par:2 * par + 1]
                    rm1 = rowm[:, 2 * par + 1:2 * par + 2]
                    p.wait("ve", "te", s1_te[i])
                    if w > 0:
                        p.op("ve", "tensor_tensor", PSC[:, lo:lo + 128],
                             PSC[:, lo:lo + 128], tlo[:, :], ALU.add)
                    p.op("ve", "tensor_tensor", PSC[:, bhi - 128:bhi],
                         PSC[:, bhi - 128:bhi], thi[:, :], ALU.add)
                    p.op("ve", "reduce_max", rm0, PSC[:, blo:bhi],
                         axis=mybir.AxisListType.X, negate=True)
                    ve_masks = p.cnt["ve"]
                    p.wait("se", "ve", ve_masks)
                    p.op("se", "activation", attn[:, blo:bhi], PSC[:, blo:bhi],
                         AF.Exp, bias=rm0, accum_out=rm1)
                    se_exp = p.cnt["se"]
                    pscp_free[par] = (ve_masks, se_exp)
                    p.wait("ve", "se", se_exp)
                    p.op("ve", "reciprocal", rm1, rm1)
                    p.vsync()
                    p.op("ve", "tensor_scalar", attn[:, blo:bhi], attn[:, blo:bhi],
                         rm1, None, ALU.mult)
                    nj = (bhi - blo) // 128
                    for jb in range(nj):
                        PT = atpA if jb % 2 == 0 else atpB
                        eng = "ve" if jb % 2 == 0 else "se"
                        p.wait("te", eng, p.cnt[eng])
                        p.op("te", "transpose", PT[:, :],
                             attn[:, blo + jb * 128:blo + (jb + 1) * 128],
                             identb[:, :], inc=True)
                        p.wait(eng, "te", p.cnt["te"])
                        if eng == "ve":
                            p.op("ve", "tensor_copy",
                                 attnT[:, jb * 128:(jb + 1) * 128], PT[:, :])
                        else:
                            p.op("se", "activation",
                                 attnT[:, jb * 128:(jb + 1) * 128], PT[:, :], AF.Copy)
                    p.wait("te", "ve", p.cnt["ve"])
                    p.wait("te", "se", p.cnt["se"])
                    for jb in range(nj):
                        jl = blo + jb * 128
                        if w > 0:
                            if jl < 512:
                                vsl = vtm[(w + 1) % 2][:, (jl // 128) * WIN + h * 64:
                                                       (jl // 128) * WIN + h * 64 + 64]
                            else:
                                vsl = vtm[w % 2][:, ((jl - 512) // 128) * WIN + h * 64:
                                                 ((jl - 512) // 128) * WIN + h * 64 + 64]
                        else:
                            vsl = vtm[0][:, (jl // 128) * WIN + h * 64:
                                         (jl // 128) * WIN + h * 64 + 64]
                        p.op("te", "matmul", avp[:, 0:128], vsl,
                             attnT[:, jb * 128:(jb + 1) * 128],
                             start=(jb == 0), stop=(jb == nj - 1), inc=(jb == nj - 1))
                    p.wait("se", "te", p.cnt["te"])
                    p.op("se", "activation",
                         ofm[hbase:hbase + 64, hc * WIN + lo:hc * WIN + lo + 128],
                         avp[:, 0:128], AF.Copy)
                    avp_free[0] = p.cnt["se"]

                if its:
                    emit_S1(0)
                    for i in range(len(its)):
                        if i + 1 < len(its):
                            emit_S1(i + 1)
                        emit_S2(i)
                ofm_done = p.cnt["ve"]
                consWO = {0: 0, 1: 0}
                first_wo = [True]
                for oc in (range(NCH) if FFN_ASEC == "all" else ()):
                    par = oc % 2
                    PWO = psA if par == 0 else pscp2_[0]
                    if first_wo[0]:
                        p.wait("te", "ve", p.cnt["ve"])
                        p.wait("te", "se", p.cnt["se"])
                        first_wo[0] = False
                    p.wait("te", "ve", consWO[par])
                    for kc in range(4):
                        p.op("te", "matmul", PWO[:, 0:WIN],
                             wo[:, kc * D + oc * 128:kc * D + oc * 128 + 128],
                             ofm[:, kc * WIN:(kc + 1) * WIN],
                             start=(kc == 0), stop=(kc == 3), inc=(kc == 3))
                    p.wait("ve", "te", p.cnt["te"])
                    p.wait("ve", "d_sy", ld2)
                    p.wait("ve", "d_gp", ld2g)
                    p.op("ve", "tensor_tensor", yA[:, oc * WIN:(oc + 1) * WIN],
                         PWO[:, 0:WIN], ywr[:, oc * WIN:(oc + 1) * WIN], ALU.add)
                    consWO[par] = p.cnt["ve"]
                if FFN_ASEC == "all":
                    p.wait("gp", "ve", p.cnt["ve"])
                    st_yA = store_fm_tile("gp", "yAh", yA, lambda c: c * WIN, t0, WIN)
            p.barrier()
        if "1" in subs:
          with contextlib.ExitStack() as ph:
            w1 = ph.enter_context(nc.sbuf_tensor("pB1w1" + sfx, [128, NCH * W1OUT], BF16))
            lg = ph.enter_context(nc.sbuf_tensor("pB1lg" + sfx, [128, NCH], F32))
            lb = ph.enter_context(nc.sbuf_tensor("pB1lb" + sfx, [128, NCH], F32))
            b1 = ph.enter_context(nc.sbuf_tensor("pB1b1" + sfx, [128, 44], F32))
            h2 = ph.enter_context(nc.sbuf_tensor("pB1h2" + sfx, [128, NCH * WIN], F32))
            h2n = ph.enter_context(nc.sbuf_tensor("pB1h2n" + sfx, [128, NCH * WIN], BF16))
            sqb = ph.enter_context(nc.sbuf_tensor("pB1sq" + sfx, [128, NCH * WIN], BF16))
            xbb = ph.enter_context(nc.sbuf_tensor("pB1xb" + sfx, [128, NCH * WIN], BF16))
            tmpf = ph.enter_context(nc.sbuf_tensor("pB1tmp" + sfx, [128, WIN], F32))
            gel = ph.enter_context(nc.sbuf_tensor("pB1gel" + sfx, [128, 2 * WIN], F32))
            srow = ph.enter_context(nc.sbuf_tensor("pB1srow" + sfx, [1, 4 * WIN], F32))
            act = ph.enter_context(nc.sbuf_tensor("pB1act" + sfx, [128, NJ * WIN], BF16))
            psS = ph.enter_context(nc.psum_tensor("pB1psS" + sfx, [1, WIN], F32))
            psQ = ph.enter_context(nc.psum_tensor("pB1psQ" + sfx, [1, WIN], F32))
            psA = ph.enter_context(nc.psum_tensor("pB1psA" + sfx, [128, WIN], F32))
            psB = ph.enter_context(nc.psum_tensor("pB1psB" + sfx, [128, WIN], F32))
            psa = [ph.enter_context(nc.psum_tensor(f"pB1psa{i}" + sfx, [128, WIN], F32))
                   for i in range(2)]
            psg = [ph.enter_context(nc.psum_tensor(f"pB1psg{i}" + sfx, [128, WIN], F32))
                   for i in range(2)]
            load_w("gp", w1, "f_w1", W1OUT, row0=l * D)
            p.wait("te", "d_gp", p.cnt["d_gp"])
            p.dma("sy", lg[:, :], hb["f_lng"][:, l * 8:(l + 1) * 8])
            p.dma("sy", lb[:, :], hb["f_lnb"][:, l * 8:(l + 1) * 8])
            p.dma("sy", b1[:, :], hb["f_b1"][:, l * 44:(l + 1) * 44])
            st_act = 0
            consW1 = {0: (0, 0), 1: (0, 0)}
            for w in range(NCH):
                t0 = w * WIN
                p.wait_compute("sy")
                for c in range(4):
                    p.dma("sy", h2[:, c * WIN:(c + 1) * WIN],
                          hb["yAh"][c * 128:(c + 1) * 128, t0:t0 + WIN])
                if w == 0:
                    for c in range(4):
                        p.op("gp", "memset", h2[:, (4 + c) * WIN:(4 + c) * WIN + 1], 0.0)
                    p.wait("sy", "gp", p.cnt["gp"])
                    for c in range(4):
                        p.dma("sy", h2[:, (4 + c) * WIN + 1:(5 + c) * WIN],
                              hb["yAh"][(4 + c) * 128:(5 + c) * 128, 0:WIN - 1])
                else:
                    for c in range(4):
                        p.dma("sy", h2[:, (4 + c) * WIN:(5 + c) * WIN],
                              hb["yAh"][(4 + c) * 128:(5 + c) * 128, t0 - 1:t0 + WIN - 1])
                ldh = p.cnt["d_sy"]
                p.wait("se", "d_sy", ldh)
                p.wait("ve", "d_sy", ldh)
                emit_ln(lambda c: h2[:, c * WIN:(c + 1) * WIN],
                        lambda c: h2n[:, c * WIN:(c + 1) * WIN],
                        WIN, sqb, xbb, tmpf, srow, psS, psQ, psA, psB,
                        lg_col=lg, lb_col=lb)
                p.wait("te", "ve", p.cnt["ve"])
                for j in range(NJ):
                    m = CHJ[j]
                    par = j % 2
                    PSA, PSG = psa[par], psg[par]
                    gl_ = gel[:, 0:WIN] if par == 0 else gel[:, WIN:2 * WIN]
                    p.wait("te", "ve", consW1[par][1])
                    p.wait("te", "se", consW1[par][0])
                    for dc in range(NCH):
                        p.op("te", "matmul", PSA[0:m, 0:WIN],
                             w1[:, dc * W1OUT + j * 128:dc * W1OUT + j * 128 + m],
                             h2n[:, dc * WIN:(dc + 1) * WIN],
                             start=(dc == 0), stop=(dc == NCH - 1), inc=(dc == NCH - 1))
                    for dc in range(NCH):
                        p.op("te", "matmul", PSG[0:m, 0:WIN],
                             w1[:, dc * W1OUT + HID + j * 128:dc * W1OUT + HID + j * 128 + m],
                             h2n[:, dc * WIN:(dc + 1) * WIN],
                             start=(dc == 0), stop=(dc == NCH - 1), inc=(dc == NCH - 1))
                    p.wait("se", "te", p.cnt["te"])
                    p.op("se", "activation", gl_[0:m, 0:WIN], PSG[0:m, 0:WIN], AF.Gelu,
                         bias=b1[0:m, 22 + j:23 + j])
                    p.wait("ve", "se", p.cnt["se"])
                    p.wait("ve", "d_gp", st_act)
                    p.op("ve", "scalar_tensor_tensor", act[0:m, j * WIN:(j + 1) * WIN],
                         PSA[0:m, 0:WIN], b1[0:m, j:j + 1], gl_[0:m, 0:WIN],
                         ALU.add, ALU.mult)
                    consW1[par] = (p.cnt["se"], p.cnt["ve"])
                p.wait("gp", "ve", p.cnt["ve"])
                for j in range(NJ):
                    p.dma("gp", hb["acth"][j * 128:j * 128 + CHJ[j], t0:t0 + WIN],
                          act[0:CHJ[j], j * WIN:(j + 1) * WIN])
                st_act = p.cnt["d_gp"]
            p.barrier()
        if "2" in subs:
          with contextlib.ExitStack() as ph:
            w2 = ph.enter_context(nc.sbuf_tensor("pB2w2" + sfx, [128, NJ * D], BF16))
            b2 = ph.enter_context(nc.sbuf_tensor("pB2b2" + sfx, [128, NCH], F32))
            actt = [ph.enter_context(nc.sbuf_tensor(f"pB2a{i}" + sfx, [128, NJ * WIN], BF16))
                    for i in range(2)]
            ywr = [ph.enter_context(nc.sbuf_tensor(f"pB2y{i}" + sfx, [128, NCH * WIN], F32))
                   for i in range(2)]
            yo = ph.enter_context(nc.sbuf_tensor("pB2yo" + sfx, [128, NCH * WIN], F32))
            ps2 = [ph.enter_context(nc.psum_tensor(f"pB2ps{i}" + sfx, [128, WIN], F32))
                   for i in range(2)]
            for j in range(NJ):
                p.dma("gp", w2[0:CHJ[j], j * D:j * D + D],
                      hb["f_w2"][l * HID + j * 128:l * HID + j * 128 + CHJ[j], 0:D])
            p.dma("sy", b2[:, :], hb["f_b2"][:, l * 8:(l + 1) * 8])
            p.wait("te", "d_gp", p.cnt["d_gp"])

            def ldact(w, buf):
                for j in range(NJ):
                    p.dma("sy", actt[buf][0:CHJ[j], j * WIN:(j + 1) * WIN],
                          hb["acth"][j * 128:j * 128 + CHJ[j], w * WIN:(w + 1) * WIN])
                for c in range(NCH):
                    p.dma("sy", ywr[buf][:, c * WIN:(c + 1) * WIN],
                          hb["yAh"][c * 128:(c + 1) * 128, w * WIN:(w + 1) * WIN])
                return p.cnt["d_sy"]

            lds = [ldact(0, 0)]
            st_yo = 0
            consW2 = {0: 0, 1: 0}
            for w in range(NCH):
                if w + 1 < NCH:
                    p.wait_compute("sy")
                    lds.append(ldact(w + 1, (w + 1) % 2))
                p.wait("te", "d_sy", lds[w])
                for oc in range(NCH):
                    par = oc % 2
                    PP = ps2[par]
                    p.wait("te", "ve", consW2[par])
                    for j in range(NJ):
                        p.op("te", "matmul", PP[:, 0:WIN],
                             w2[0:CHJ[j], j * D + oc * 128:j * D + oc * 128 + 128],
                             actt[w % 2][0:CHJ[j], j * WIN:(j + 1) * WIN],
                             start=(j == 0), stop=(j == NJ - 1), inc=(j == NJ - 1))
                    p.wait("ve", "te", p.cnt["te"])
                    p.wait("ve", "d_gp", st_yo)
                    p.op("ve", "tensor_scalar", yo[:, oc * WIN:(oc + 1) * WIN],
                         PP[:, 0:WIN], b2[:, oc:oc + 1], None, ALU.add)
                    p.op("ve", "tensor_tensor", yo[:, oc * WIN:(oc + 1) * WIN],
                         yo[:, oc * WIN:(oc + 1) * WIN],
                         ywr[w % 2][:, oc * WIN:(oc + 1) * WIN], ALU.add)
                    consW2[par] = p.cnt["ve"]
                p.wait("gp", "ve", p.cnt["ve"])
                st_yo = store_fm_tile("gp", out_name, yo, lambda c: c * WIN, w * WIN, WIN)
            p.barrier()

    if BU >= 8:
        ffn_layer(0, "xn1h", "yBh")
    if BU >= 9:
        ffn_layer(1, "yBh", "y2h")

    w0nT = sbuf("w0nT", 128, NCH * D, BF16)
    w1nT = sbuf("w1nT", 128, NCH * D, BF16)
    if BU >= 10:
      with contextlib.ExitStack() as ph:
        wtf = ph.enter_context(nc.sbuf_tensor("p9wt", [128, D], F32))
        gtf = ph.enter_context(nc.sbuf_tensor("p9gt", [128, D], F32))
        grow = ph.enter_context(nc.sbuf_tensor("p9grow", [2, 1], F32))
        grow2 = ph.enter_context(nc.sbuf_tensor("p9grow2", [1, 2], F32))
        bgs = ph.enter_context(nc.sbuf_tensor("p9bgs", [2, 1], F32))
        amul = ph.enter_context(nc.sbuf_tensor("p9amul", [128, 2], F32))
        ps9 = ph.enter_context(nc.psum_tensor("p9ps", [128, 2], F32))
        p.wait("gp", "cc", 1)
        p.wait("sy", "cc", 1)
        p.dma("gp", grow[:, :], hb["cc_out"][2048:2050, 0:1], allow_slow_non_contiguous=True)
        p.dma("gp", bgs[:, :], hb["l1_bg"][:, :])
        p.wait("se", "d_gp", p.cnt["d_gp"])
        p.op("se", "activation", grow[:, :], grow[:, :], AF.Copy,
             scale=cpool[0:2, C_MEAN:C_MEAN + 1])
        p.wait("ve", "se", p.cnt["se"])
        p.op("ve", "tensor_tensor", grow[:, :], grow[:, :], bgs[:, :], ALU.add)
        p.wait("se", "ve", p.cnt["ve"])
        p.op("se", "activation", grow[:, :], grow[:, :], AF.Sigmoid)
        p.wait("te", "se", p.cnt["se"])
        p.op("te", "transpose", ps9[0:1, 0:2], grow[0:2, 0:1], ident[0:2, 0:2], inc=True)
        p.wait("ve", "te", p.cnt["te"])
        p.op("ve", "tensor_copy", grow2[:, :], ps9[0:1, 0:2])
        p.wait("te", "ve", p.cnt["ve"])
        p.op("te", "matmul", ps9[:, :], ones1f[0:1, :], grow2[0:1, :],
             start=True, stop=True, inc=True)
        p.wait("ve", "te", p.cnt["te"])
        p.op("ve", "tensor_scalar", amul[:, 0:1], ps9[:, 0:1], -1.0, 1.0,
             ALU.mult, ALU.add)
        p.op("ve", "tensor_copy", amul[:, 1:2], ps9[:, 1:2])
        p.vsync()
        for (g_row0, wt_name, dst, sc) in ((0, "l1_w0T", w0nT, None),
                                           (1024, "l1_w1T", w1nT, SCALE2S)):
            for c in range(NCH):
                p.dma("sy", wtf[:, :], hb[wt_name][c * 128:(c + 1) * 128, :])
                ldg = p.dma("sy", gtf[:, :],
                            hb["cc_out"][g_row0 + c * 128:g_row0 + (c + 1) * 128, :])
                p.wait("ve", "d_sy", ldg)
                if sc is not None:
                    p.op("ve", "tensor_scalar", gtf[:, :], gtf[:, :], sc, None, ALU.mult)
                p.op("ve", "tensor_scalar", gtf[:, :], gtf[:, :], -1.0, 1.0,
                     ALU.max, ALU.min)
                p.op("ve", "tensor_scalar", gtf[:, :], gtf[:, :], amul[:, 1:2], None,
                     ALU.mult)
                p.op("ve", "tensor_scalar", wtf[:, :], wtf[:, :], amul[:, 0:1], None,
                     ALU.mult)
                p.op("ve", "tensor_tensor", dst[:, c * D:(c + 1) * D], wtf[:, :],
                     gtf[:, :], ALU.subtract)
                p.wait("sy", "ve", p.cnt["ve"])
        if "w0n" in debug_want:
            p.wait_compute("gp")
            for c in range(NCH):
                p.dma("gp", tap("w0n", D, D, BF16)[c * 128:(c + 1) * 128, :],
                      w0nT[:, c * D:(c + 1) * D])
                p.dma("gp", tap("w1n", D, D, BF16)[c * 128:(c + 1) * 128, :],
                      w1nT[:, c * D:(c + 1) * D])
            p.dma("gp", tap("amul", 128, 2)[:, :], amul[:, :])
        p.barrier()

    if BU >= 11:
      with contextlib.ExitStack() as ph:
        wqT = ph.enter_context(nc.sbuf_tensor("pTwq", [128, NCH * D], BF16))
        woT = ph.enter_context(nc.sbuf_tensor("pTwo", [128, NCH * D], BF16))
        xnt = [ph.enter_context(nc.sbuf_tensor(f"pTx{i}", [128, NCH * TT], BF16))
               for i in range(2)]
        y2t = [ph.enter_context(nc.sbuf_tensor(f"pTy{i}", [128, NCH * TT], F32))
               for i in range(2)]
        x1t = [ph.enter_context(nc.sbuf_tensor(f"pTx1{i}", [128, NCH * TT], F32))
               for i in range(2)]
        qt = ph.enter_context(nc.sbuf_tensor("pTq", [128, NCH * TT], BF16))
        t1t = ph.enter_context(nc.sbuf_tensor("pTt1", [128, NCH * TT], BF16))
        t2t = ph.enter_context(nc.sbuf_tensor("pTt2", [128, NCH * TT], BF16))
        gt = ph.enter_context(nc.sbuf_tensor("pTg", [128, TT], F32))
        ot = ph.enter_context(nc.sbuf_tensor("pTo", [128, NCH * TT], F32))
        psX = [ph.enter_context(nc.psum_tensor(f"pTps{i}", [128, TT], F32))
               for i in range(2)]
        load_w("gp", wqT, "l1_wqT", D)
        load_w("gp", woT, "l1_woutT", D)
        p.wait("te", "d_gp", p.cnt["d_gp"])

        def ldtile(t, buf):
            load_fm_tile("sy", xnt[buf], lambda c: c * TT, "xn1h", t * TT, TT)
            load_fm_tile("sy", y2t[buf], lambda c: c * TT, "y2h", t * TT, TT)
            load_fm_tile("sy", x1t[buf], lambda c: c * TT, "x1h", t * TT, TT)
            return p.cnt["d_sy"]

        lds = [ldtile(0, 0)]
        st_o = 0
        consX = [(0, 0), (0, 0)]
        for t in range(NTILE):
            if t + 1 < NTILE:
                p.wait_compute("sy")
                lds.append(ldtile(t + 1, (t + 1) % 2))
            xn, y2, x1 = xnt[t % 2], y2t[t % 2], x1t[t % 2]
            p.wait("te", "d_sy", lds[t])

            def proj(wsb, rhs, dst, fn):
                cons = {0: consX[0], 1: consX[1]}
                p.wait("te", "se", p.cnt["se"])
                for oc in range(NCH):
                    par = oc % 2
                    PP = psX[par]
                    p.wait("te", "se", cons[par][0])
                    p.wait("te", "ve", cons[par][1])
                    for dc in range(NCH):
                        p.op("te", "matmul", PP[:, 0:TT],
                             wsb[:, dc * D + oc * 128:dc * D + oc * 128 + 128],
                             rhs[:, dc * TT:(dc + 1) * TT],
                             start=(dc == 0), stop=(dc == NCH - 1), inc=(dc == NCH - 1))
                    p.wait("se", "te", p.cnt["te"])
                    p.op("se", "activation", dst[:, oc * TT:(oc + 1) * TT], PP[:, 0:TT],
                         fn)
                    cons[par] = (p.cnt["se"], cons[par][1])
                consX[0], consX[1] = cons[0], cons[1]
            proj(wqT, xn, qt, AF.Copy)
            proj(w0nT, qt, t1t, AF.Silu)
            proj(w1nT, t1t, t2t, AF.Copy)
            consF = dict(enumerate([consX[0], consX[1]]))
            p.wait("te", "se", p.cnt["se"])
            for oc in range(NCH):
                par = oc % 2
                PP = psX[par]
                p.wait("te", "se", consF[par][0])
                p.wait("te", "ve", consF[par][1])
                for dc in range(NCH):
                    p.op("te", "matmul", PP[:, 0:TT],
                         woT[:, dc * D + oc * 128:dc * D + oc * 128 + 128],
                         t2t[:, dc * TT:(dc + 1) * TT],
                         start=(dc == 0), stop=(dc == NCH - 1), inc=(dc == NCH - 1))
                p.wait("se", "d_sy", lds[t])
                p.wait("se", "ve", p.cnt["ve"])
                p.op("se", "activation", gt[:, 0:TT], y2[:, oc * TT:(oc + 1) * TT],
                     AF.Silu)
                p.wait("ve", "te", p.cnt["te"])
                p.wait("ve", "se", p.cnt["se"])
                p.wait("ve", "d_gp", st_o)
                p.op("ve", "tensor_tensor", ot[:, oc * TT:(oc + 1) * TT], PP[:, 0:TT],
                     gt[:, 0:TT], ALU.mult)
                p.op("ve", "tensor_tensor", ot[:, oc * TT:(oc + 1) * TT],
                     ot[:, oc * TT:(oc + 1) * TT], x1[:, oc * TT:(oc + 1) * TT], ALU.add)
                consF[par] = (p.cnt["se"], p.cnt["ve"])
            consX[0], consX[1] = consF[0], consF[1]
            p.wait("gp", "ve", p.cnt["ve"])
            for c in range(NCH):
                p.dma("gp", out_fm[c * 128:(c + 1) * 128, t * TT:(t + 1) * TT],
                      ot[:, c * TT:(c + 1) * TT])
            if "qt" in debug_want:
                p.wait_compute("gp")
                for c in range(NCH):
                    p.dma("gp", tap("qt", D, S, BF16)[c * 128:(c + 1) * 128, t * TT:(t + 1) * TT],
                          qt[:, c * TT:(c + 1) * TT])
                    p.dma("gp", tap("t1t", D, S, BF16)[c * 128:(c + 1) * 128, t * TT:(t + 1) * TT],
                          t1t[:, c * TT:(c + 1) * TT])
                    p.dma("gp", tap("t2t", D, S, BF16)[c * 128:(c + 1) * 128, t * TT:(t + 1) * TT],
                          t2t[:, c * TT:(c + 1) * TT])
            st_o = p.cnt["d_gp"]
        p.wait("gp", "d_gp", p.cnt["d_gp"])
        p.wait("sy", "d_gp", p.cnt["d_gp"])

    scr_shapes = {"x1h": (D, S, F32), "xn1h": (D, S, BF16), "kfmh": (D, S, BF16),
                  "vfmh": (D, S, BF16), "a1tmh": (S, D, BF16), "d2tmh": (S, D, BF16),
                  "d1tmh": (S, D, BF16), "ktmh": (S, D, BF16), "yAh": (D, S, F32),
                  "yBh": (D, S, F32), "y2h": (D, S, F32), "cc_out": (2050, D, F32),
                  "tabC": (128, S, F32), "tabS": (128, S, F32),
                  "tabCi": (128, S, F32), "tabSi": (128, S, F32),
                  "acth": (NJ * 128, S, BF16)}
    p.barrier()
    if "cc_out" in debug_want:
        p.wait("gp", "cc", p.cnt["cc"])
    for nm in debug_want:
        if nm in scr_shapes:
            r_, c_, dt_ = scr_shapes[nm]
            t = tap(nm, r_, c_, dt_)
            for rr in range(0, r_, 1024):
                p.dma("gp", t[rr:min(rr + 1024, r_), :],
                      hb[nm][rr:min(rr + 1024, r_), :])
    p.wait("gp", "d_gp", p.cnt["d_gp"])

    with nc.Block() as blk:
        p.finalize(blk)
    es.close()
    return nc


def kernel(**inputs):
    hp = _host_prep(inputs)
    x = np.asarray(inputs["L0_attn_query"], dtype=np.float32)
    nc = build_nc(tuple(DEBUG_WANT))
    in_maps = []
    for b in range(NCORE):
        m = {"x_fm": np.ascontiguousarray(x[b].T)}
        m.update(hp)
        in_maps.append(m)
    res = run_bass_kernel_spmd(nc, in_maps, core_ids=list(range(NCORE)))
    out = np.stack([np.asarray(r["out_fm"]).T for r in res.results], axis=0)
    if DEBUG_WANT:
        kernel.last_taps = [{k: np.asarray(v) for k, v in r.items() if k.startswith("tap_")}
                            for r in res.results]
    return out.astype(np.float32)


# version 13
# speedup vs baseline: 1.3578x; 1.0142x over previous
import math
import contextlib
import numpy as np

import concourse.bass as bass
import concourse.mybir as mybir
from concourse.bass_utils import run_bass_kernel_spmd

F32 = mybir.dt.float32
BF16 = mybir.dt.bfloat16
I32 = mybir.dt.int32
AF = mybir.ActivationFunctionType
ALU = mybir.AluOpType

NCORE = 8
S = 4096
D = 1024
NCH = 8
TT = 512
NTILE = S // TT
H0, DH0 = 16, 64
FH, WIN, INNER = 8, 512, 512
HID = 2730
W1OUT = 5460
NJ = 22
CHJ = [128] * 21 + [42]
SCALE2S = 2.0 / 4096.0
NEG_BIG = -1.0e38

DEBUG_WANT = []
DEBUG_TAPS = []
BUILD_UNTIL = 99
FFN_SUBS = "AB12"
FFN_ASEC = "all"

ENGS = ("te", "ve", "se", "gp", "sy")
EATTR = {"te": "tensor", "ve": "vector", "se": "scalar", "gp": "gpsimd", "sy": "sync"}


class P:
    def __init__(self, nc, es):
        self.nc = nc
        self.q = {e: [] for e in ENGS}
        self.cnt = {}
        self.sem = {}
        self.last_wait = {}
        for k in ("te", "ve", "se", "gp", "d_sy", "d_gp", "d_se", "d_ve", "cc"):
            self.sem[k] = es.enter_context(nc.semaphore(k))
            self.cnt[k] = 0

    def op(self, eng, meth, *a, inc=True, **kw):
        if inc:
            self.cnt[eng] += 1
        def f(E, meth=meth, a=a, kw=kw, inc=inc, sem=self.sem[eng]):
            r = getattr(E, meth)(*a, **kw)
            if inc:
                r.then_inc(sem, 1)
        self.q[eng].append(f)
        return self.cnt[eng]

    def dma(self, qeng, out, in_, **kw):
        key = "d_" + qeng
        self.cnt[key] += 16
        def f(E, out=out, in_=in_, kw=kw, sem=self.sem[key]):
            E.dma_start(out=out, in_=in_, **kw).then_inc(sem, 16)
        self.q[qeng].append(f)
        return self.cnt[key]

    def ccop(self, *a, **kw):
        self.cnt["cc"] += 1
        def f(E, a=a, kw=kw, sem=self.sem["cc"]):
            E.collective_compute(*a, **kw).then_inc(sem, 1)
        self.q["gp"].append(f)
        return self.cnt["cc"]

    def wait(self, eng, key, val):
        if val <= 0:
            return
        if val <= self.last_wait.get((eng, key), -1):
            return
        self.last_wait[(eng, key)] = val
        def f(E, sem=self.sem[key], val=val):
            E.wait_ge(sem, val)
        self.q[eng].append(f)

    def vsync(self, eng="ve"):
        self.wait(eng, eng, self.cnt[eng])

    def wait_compute(self, eng):
        for k in ("te", "ve", "se", "gp"):
            self.wait(eng, k, self.cnt[k])

    def wait_all(self, eng):
        for k in self.sem:
            if k != "cc":
                self.wait(eng, k, self.cnt[k])

    def barrier(self):
        for e in ENGS:
            self.wait_all(e)

    def finalize(self, blk):
        for e in ENGS:
            fns = self.q[e]
            def runner(E, fns=fns):
                for f in fns:
                    f(E)
            getattr(blk, EATTR[e])(runner)


def _host_prep(I):
    f = np.float32
    hp = {}
    hp["l0_wkv"] = np.ascontiguousarray(
        np.concatenate([np.asarray(I["L0_attn_Wk"]), np.asarray(I["L0_attn_Wv"])], axis=2)
        .transpose(1, 0, 2).reshape(DH0, H0 * 128)).astype(f)
    hp["l0_wq"] = np.ascontiguousarray(
        np.asarray(I["L0_attn_Wq"]).transpose(1, 0, 2).reshape(DH0, H0 * DH0)).astype(f)
    hp["l0_wo"] = np.ascontiguousarray(I["L0_attn_Wo"]).astype(f)
    hp["l0_bwT"] = np.ascontiguousarray(np.asarray(I["L0_ffn_base_w"]).T).astype(f)
    hp["l0_bb"] = np.ascontiguousarray(np.asarray(I["L0_ffn_base_b"]).reshape(NCH, 128).T).astype(f)
    hp["l0_rwT"] = np.ascontiguousarray(np.asarray(I["L0_ffn_router_w"]).T).astype(f)
    hp["l0_rb"] = np.ascontiguousarray(np.asarray(I["L0_ffn_router_b"]).reshape(8, 1)).astype(f)
    hp["l0_lA"] = np.ascontiguousarray(I["L0_ffn_lora_A"]).astype(f)
    hp["l0_lB"] = np.ascontiguousarray(np.asarray(I["L0_ffn_lora_B"]).reshape(64, D)).astype(f)
    for nm, k in [("l1_wkT", "L1_attn_Wk"), ("l1_wvT", "L1_attn_Wv"),
                  ("l1_wqT", "L1_attn_Wq"), ("l1_woutT", "L1_attn_Wout"),
                  ("l1_w0T", "L1_attn_W0"), ("l1_w1T", "L1_attn_W1")]:
        hp[nm] = np.ascontiguousarray(np.asarray(I[k]).T).astype(f)
    hp["l1_w1n"] = np.ascontiguousarray(I["L1_attn_W1"]).astype(f)
    hp["l1_wg"] = np.ascontiguousarray(
        np.concatenate([I["L1_attn_Wg_decay"], I["L1_attn_Wg_lr"]], axis=0).T).astype(f)
    hp["l1_bg"] = np.ascontiguousarray(
        np.concatenate([I["L1_attn_bg_decay"], I["L1_attn_bg_lr"]]).reshape(2, 1)).astype(f)
    hp["f_wqkv"] = np.ascontiguousarray(np.asarray(I["L1_ffn_wqkv"]).reshape(2 * D, 3 * INNER)).astype(f)
    hp["f_wo"] = np.ascontiguousarray(np.asarray(I["L1_ffn_wo"]).reshape(2 * INNER, D)).astype(f)
    hp["f_w1"] = np.ascontiguousarray(np.asarray(I["L1_ffn_w1"]).reshape(2 * D, W1OUT)).astype(f)
    hp["f_w2"] = np.ascontiguousarray(np.asarray(I["L1_ffn_w2"]).reshape(2 * HID, D)).astype(f)
    hp["f_qs"] = np.ascontiguousarray(np.tile(np.asarray(I["L1_ffn_q_scale"]), (1, 2)).T).astype(f)
    hp["f_ks"] = np.ascontiguousarray(np.tile(np.asarray(I["L1_ffn_k_scale"]), (1, 2)).T).astype(f)
    hp["f_lng"] = np.ascontiguousarray(np.concatenate(
        [np.asarray(I["L1_ffn_ln_g"])[l].reshape(NCH, 128).T for l in range(2)], axis=1)).astype(f)
    hp["f_lnb"] = np.ascontiguousarray(np.concatenate(
        [np.asarray(I["L1_ffn_ln_b"])[l].reshape(NCH, 128).T for l in range(2)], axis=1)).astype(f)
    b1 = np.zeros((128, 2 * 44), dtype=f)
    for l in range(2):
        ap_ = np.asarray(I["L1_ffn_b1"])[l][:HID]
        gp_ = np.asarray(I["L1_ffn_b1"])[l][HID:]
        for j in range(NJ):
            m = CHJ[j]
            b1[:m, l * 44 + j] = ap_[j * 128:j * 128 + m]
            b1[:m, l * 44 + 22 + j] = gp_[j * 128:j * 128 + m]
    hp["f_b1"] = np.ascontiguousarray(b1)
    hp["f_b2"] = np.ascontiguousarray(np.concatenate(
        [np.asarray(I["L1_ffn_b2"])[l].reshape(NCH, 128).T for l in range(2)], axis=1)).astype(f)
    hp["c_ident"] = np.eye(128, dtype=f)
    R = np.zeros((64, 64), dtype=f)
    R[0:32, 32:64] = -np.eye(32)
    R[32:64, 0:32] = np.eye(32)
    hp["c_rt"] = np.ascontiguousarray(np.kron(np.eye(2, dtype=f), R.T))
    h2 = np.zeros((128, 2), dtype=f)
    h2[0:64, 0] = 1.0
    h2[64:128, 1] = 1.0
    hp["c_h2"] = h2
    hp["c_b2t"] = np.ascontiguousarray(h2.T)
    g8 = np.zeros((8, 64), dtype=f)
    t8 = np.zeros((8, 64), dtype=f)
    for m in range(64):
        g8[m // 8, m] = 1.0
        t8[m % 8, m] = 1.0
    hp["c_g8"] = g8
    hp["c_t8"] = t8
    iq = np.arange(128)[:, None]
    jj = np.arange(128)[None, :]
    hp["c_tlo"] = np.where(jj < iq, NEG_BIG, 0.0).astype(f)
    hp["c_thi"] = np.where(jj > iq, NEG_BIG, 0.0).astype(f)
    hp["c_j32"] = np.ascontiguousarray((np.arange(128) % 32).reshape(128, 1)).astype(f)
    return hp


def build_nc(debug_want=(), build_until=None):
    global DEBUG_TAPS
    DEBUG_TAPS = []
    BU = build_until if build_until is not None else BUILD_UNTIL
    nc = bass.Bass("TRN2", target_bir_lowering=False)
    es = contextlib.ExitStack()
    hb = {}

    def din(name, shape, dt=F32):
        hb[name] = nc.dram_tensor(name, list(shape), dt, kind="ExternalInput")

    def dscr(name, shape, dt=F32):
        hb[name] = nc.dram_tensor(name, list(shape), dt)

    din("x_fm", (D, S))
    for nm, shp in [("l0_wkv", (64, 2048)), ("l0_wq", (64, 1024)), ("l0_wo", (D, D)),
                    ("l0_bwT", (D, D)), ("l0_bb", (128, 8)), ("l0_rwT", (D, 8)),
                    ("l0_rb", (8, 1)), ("l0_lA", (D, 8)), ("l0_lB", (64, D)),
                    ("l1_wkT", (D, D)), ("l1_wvT", (D, D)), ("l1_wqT", (D, D)),
                    ("l1_woutT", (D, D)), ("l1_w0T", (D, D)), ("l1_w1T", (D, D)),
                    ("l1_w1n", (D, D)), ("l1_wg", (D, 2)), ("l1_bg", (2, 1)),
                    ("f_wqkv", (2 * D, 3 * INNER)), ("f_wo", (2 * INNER, D)),
                    ("f_w1", (2 * D, W1OUT)), ("f_w2", (2 * HID, D)),
                    ("f_qs", (128, 2)), ("f_ks", (128, 2)),
                    ("f_lng", (128, 16)), ("f_lnb", (128, 16)),
                    ("f_b1", (128, 88)), ("f_b2", (128, 16)),
                    ("c_ident", (128, 128)), ("c_rt", (128, 128)),
                    ("c_h2", (128, 2)), ("c_b2t", (2, 128)),
                    ("c_g8", (8, 64)), ("c_t8", (8, 64)),
                    ("c_tlo", (128, 128)), ("c_thi", (128, 128)),
                    ("c_j32", (128, 1))]:
        din(nm, shp)
    out_fm = nc.dram_tensor("out_fm", [D, S], F32, kind="ExternalOutput")

    dscr("x1h", (D, S))
    dscr("xn1h", (D, S), BF16)
    dscr("kfmh", (D, S), BF16)
    dscr("vfmh", (D, S), BF16)
    dscr("a1tmh", (S, D), BF16)
    dscr("d2tmh", (S, D), BF16)
    dscr("d1tmh", (S, D), BF16)
    dscr("ktmh", (S, D), BF16)
    dscr("tabC", (128, S))
    dscr("tabS", (128, S))
    dscr("tabCi", (128, S))
    dscr("tabSi", (128, S))
    dscr("yAh", (D, S))
    dscr("yBh", (D, S))
    dscr("y2h", (D, S))
    dscr("acth", (NJ * 128, S), BF16)
    dscr("cc_in", (2050, D))
    dscr("cc_out", (2050, D))

    taps = {}
    def tap(name, rows, cols, dt=F32):
        if name in taps:
            return taps[name]
        t = nc.dram_tensor("tap_" + name, [rows, cols], dt, kind="ExternalOutput")
        taps[name] = t
        DEBUG_TAPS.append("tap_" + name)
        return t

    p = P(nc, es)

    def sbuf(name, part, free, dt):
        return es.enter_context(nc.sbuf_tensor(name, [part, free], dt))

    ones_b = sbuf("ones_b", 128, 1, BF16)
    ones_f = sbuf("ones_f", 128, 1, F32)
    ones1f = sbuf("ones1f", 1, 128, F32)
    ident = sbuf("ident", 128, 128, F32)
    identb = sbuf("identb", 128, 128, BF16)
    cpool = sbuf("cpool", 128, 8, F32)
    C_EPSLN, C_EPSL0, C_IVF, C_LNS, C_LNB, C_MEAN = 0, 1, 2, 3, 4, 5
    gacc = sbuf("gacc", 2, 1, F32)

    p.op("gp", "memset", ones_b[:, :], 1.0)
    p.op("gp", "memset", ones_f[:, :], 1.0)
    p.op("gp", "memset", ones1f[:, :], 1.0)
    p.op("gp", "memset", cpool[:, C_EPSLN:C_EPSLN + 1], 1e-5)
    p.op("gp", "memset", cpool[:, C_EPSL0:C_EPSL0 + 1], 1e-6)
    p.op("gp", "memset", cpool[:, C_IVF:C_IVF + 1], float(-math.log(10000.0) / 32.0))
    p.op("gp", "memset", cpool[:, C_LNS:C_LNS + 1], float(2.0 / 89.6))
    p.op("gp", "memset", cpool[:, C_LNB:C_LNB + 1], float(25.6 / 89.6))
    p.op("gp", "memset", cpool[:, C_MEAN:C_MEAN + 1], float(1.0 / 32768.0))
    p.op("gp", "memset", gacc[:, :], 0.0)
    p.dma("sy", ident[:, :], hb["c_ident"][:, :])
    p.dma("gp", identb[:, :], hb["c_ident"][:, :])
    p.barrier()

    def load_fm_tile(qeng, dst, dstcols, hbmT, t0, width):
        for c in range(NCH):
            p.dma(qeng, dst[:, dstcols(c):dstcols(c) + width],
                  hb[hbmT][c * 128:(c + 1) * 128, t0:t0 + width])
        return p.cnt["d_" + qeng]

    def store_fm_tile(qeng, hbmT, src, srccols, t0, width):
        for c in range(NCH):
            p.dma(qeng, hb[hbmT][c * 128:(c + 1) * 128, t0:t0 + width],
                  src[:, srccols(c):srccols(c) + width])
        return p.cnt["d_" + qeng]

    def load_w(qeng, dst, hbm_name, K, row0=0, nrows=D):
        for c in range(nrows // 128):
            p.dma(qeng, dst[:, c * K:(c + 1) * K],
                  hb[hbm_name][row0 + c * 128:row0 + (c + 1) * 128, 0:K])
        return p.cnt["d_" + qeng]

    def emit_ln(src_c, dst_c, W, sqb, xbb, tmpf, srow, psS, psQ, psA, psB,
                lg_col=None, lb_col=None):
        for c in range(NCH):
            p.op("se", "activation", sqb[:, c * W:(c + 1) * W], src_c(c), AF.Square)
            p.op("se", "activation", xbb[:, c * W:(c + 1) * W], src_c(c), AF.Copy)
        p.wait("te", "se", p.cnt["se"])
        p.wait("te", "ve", p.cnt["ve"])
        for c in range(NCH):
            p.op("te", "matmul", psS[0:1, 0:W], ones_b[:, :], xbb[:, c * W:(c + 1) * W],
                 start=(c == 0), stop=(c == NCH - 1), inc=(c == NCH - 1))
        for c in range(NCH):
            p.op("te", "matmul", psQ[0:1, 0:W], ones_b[:, :], sqb[:, c * W:(c + 1) * W],
                 start=(c == 0), stop=(c == NCH - 1), inc=(c == NCH - 1))
        p.wait("ve", "te", p.cnt["te"])
        m_, q2 = srow[0:1, 0:W], srow[0:1, W:2 * W]
        a_, b_ = srow[0:1, 2 * W:3 * W], srow[0:1, 3 * W:4 * W]
        p.op("ve", "tensor_scalar", m_, psS[0:1, 0:W], 1.0 / D, None, ALU.mult)
        p.op("ve", "tensor_scalar", q2, psQ[0:1, 0:W], 1.0 / D, None, ALU.mult)
        p.op("ve", "tensor_tensor", a_, m_, m_, ALU.mult)
        p.op("ve", "tensor_tensor", q2, q2, a_, ALU.subtract)
        p.wait("se", "ve", p.cnt["ve"])
        p.op("se", "activation", a_, q2, AF.Sqrt, bias=cpool[0:1, C_EPSLN:C_EPSLN + 1])
        p.wait("ve", "se", p.cnt["se"])
        p.op("ve", "reciprocal", a_, a_)
        p.op("ve", "scalar_tensor_tensor", b_, m_, -1.0, a_, ALU.mult, ALU.mult)
        p.wait("te", "ve", p.cnt["ve"])
        p.op("te", "matmul", psA[:, 0:W], ones1f[0:1, :], a_, start=True, stop=True)
        p.op("te", "matmul", psB[:, 0:W], ones1f[0:1, :], b_, start=True, stop=True)
        p.wait("ve", "te", p.cnt["te"])
        for c in range(NCH):
            p.op("ve", "tensor_tensor", tmpf[:, 0:W], src_c(c), psA[:, 0:W], ALU.mult)
            if lg_col is None:
                p.op("ve", "tensor_tensor", dst_c(c), tmpf[:, 0:W], psB[:, 0:W], ALU.add)
            else:
                p.op("ve", "tensor_tensor", tmpf[:, 0:W], tmpf[:, 0:W], psB[:, 0:W], ALU.add)
                p.op("ve", "tensor_scalar", dst_c(c), tmpf[:, 0:W],
                     lg_col[:, c:c + 1], lb_col[:, c:c + 1], ALU.mult, ALU.add)
        return p.cnt["ve"]

    if BU >= 1:
      with contextlib.ExitStack() as ph:
        tA = ph.enter_context(nc.sbuf_tensor("tA", [128, S], F32))
        tB = ph.enter_context(nc.sbuf_tensor("tB", [128, S], F32))
        tC = ph.enter_context(nc.sbuf_tensor("tC", [128, S], F32))
        tD = ph.enter_context(nc.sbuf_tensor("tD", [128, S], F32))
        tI = ph.enter_context(nc.sbuf_tensor("tI", [128, S], I32))
        col = ph.enter_context(nc.sbuf_tensor("colv", [128, 4], F32))
        inv2pi = float(1.0 / (2.0 * math.pi))
        twopi = float(2.0 * math.pi)
        p.op("gp", "iota", tD[:, :], [[1, S]], channel_multiplier=0,
             allow_small_or_imprecise_dtypes=True)
        ldj = p.dma("sy", col[:, 0:1], hb["c_j32"][:, :])
        p.wait("se", "d_sy", ldj)
        p.wait("se", "gp", p.cnt["gp"])
        p.wait("ve", "gp", p.cnt["gp"])
        p.wait("ve", "d_sy", ldj)
        p.op("se", "activation", col[:, 1:2], col[:, 0:1], AF.Exp,
             scale=cpool[:, C_IVF:C_IVF + 1])
        p.op("se", "activation", col[:, 2:3], col[:, 0:1], AF.Ln,
             scale=cpool[:, C_LNS:C_LNS + 1], bias=cpool[:, C_LNB:C_LNB + 1])
        p.wait("ve", "se", p.cnt["se"])
        p.op("ve", "tensor_scalar", col[:, 2:3], col[:, 2:3], 1.0 / 256.0, None, ALU.mult)
        p.op("ve", "tensor_scalar", tA[:, :], tD[:, :], 2048.0, None, ALU.subtract)
        p.op("ve", "tensor_scalar", tA[:, :], tA[:, :], col[:, 2:3], None, ALU.mult)
        p.wait("se", "ve", p.cnt["ve"])
        p.op("se", "activation", tA[:, :], tA[:, :], AF.Exp)
        p.op("se", "activation", tD[:, :], tD[:, :], AF.Copy, scale=col[:, 1:2])
        p.wait("ve", "se", p.cnt["se"])

        def reduced_sin(dst, shift):
            if shift:
                p.op("ve", "tensor_scalar", tB[:, :], tD[:, :], float(shift), inv2pi,
                     ALU.add, ALU.mult)
            else:
                p.op("ve", "tensor_scalar", tB[:, :], tD[:, :], inv2pi, None, ALU.mult)
            p.op("ve", "tensor_copy", tI[:, :], tB[:, :])
            p.op("ve", "tensor_copy", tB[:, :], tI[:, :])
            p.op("ve", "scalar_tensor_tensor", tB[:, :], tB[:, :], -twopi, tD[:, :],
                 ALU.mult, ALU.add)
            if shift:
                p.op("ve", "tensor_scalar", tB[:, :], tB[:, :], float(shift), None,
                     ALU.add)
            p.op("ve", "tensor_scalar", tB[:, :], tB[:, :], float(-np.pi), float(np.pi),
                 ALU.max, ALU.min)
            p.wait("se", "ve", p.cnt["ve"])
            p.op("se", "activation", dst, tB[:, :], AF.Sin)
            p.wait("ve", "se", p.cnt["se"])

        reduced_sin(tC[:, :], 0.0)
        p.op("ve", "tensor_tensor", tB[:, :], tC[:, :], tA[:, :], ALU.mult)
        p.wait("gp", "ve", p.cnt["ve"])
        st = p.dma("gp", hb["tabS"][:, :], tB[:, :])
        p.wait("ve", "d_gp", st)
        p.op("ve", "reciprocal", tB[:, :], tA[:, :])
        p.op("ve", "tensor_tensor", tC[:, :], tC[:, :], tB[:, :], ALU.mult)
        p.wait("gp", "ve", p.cnt["ve"])
        st = p.dma("gp", hb["tabSi"][:, :], tC[:, :])
        p.wait("ve", "d_gp", st)
        reduced_sin(tC[:, :], math.pi / 2.0)
        p.op("ve", "tensor_tensor", tB[:, :], tC[:, :], tA[:, :], ALU.mult)
        p.wait("gp", "ve", p.cnt["ve"])
        st = p.dma("gp", hb["tabC"][:, :], tB[:, :])
        p.wait("ve", "d_gp", st)
        p.op("ve", "reciprocal", tB[:, :], tA[:, :])
        p.op("ve", "tensor_tensor", tC[:, :], tC[:, :], tB[:, :], ALU.mult)
        p.wait("gp", "ve", p.cnt["ve"])
        p.dma("gp", hb["tabCi"][:, :], tC[:, :])
        p.barrier()

    if BU >= 2:
      with contextlib.ExitStack() as ph0:
        xn0 = ph0.enter_context(nc.sbuf_tensor("xn0", [128, NCH * S], BF16))
        kv_sb = ph0.enter_context(nc.sbuf_tensor("kv_sb", [64, H0 * 64], BF16))
        z_col = ph0.enter_context(nc.sbuf_tensor("z_col", [64, H0], BF16))

        with contextlib.ExitStack() as ph:
            xt = [ph.enter_context(nc.sbuf_tensor(f"p0x{i}", [128, NCH * TT], F32))
                  for i in range(2)]
            sqb = ph.enter_context(nc.sbuf_tensor("p0sq", [128, NCH * TT], BF16))
            xbb = ph.enter_context(nc.sbuf_tensor("p0xb", [128, NCH * TT], BF16))
            tmpf = ph.enter_context(nc.sbuf_tensor("p0tmp", [128, TT], F32))
            srow = ph.enter_context(nc.sbuf_tensor("p0srow", [1, 4 * TT], F32))
            psS = ph.enter_context(nc.psum_tensor("p0psS", [1, TT], F32))
            psQ = ph.enter_context(nc.psum_tensor("p0psQ", [1, TT], F32))
            psA = ph.enter_context(nc.psum_tensor("p0psA", [128, TT], F32))
            psB = ph.enter_context(nc.psum_tensor("p0psB", [128, TT], F32))
            lds = [load_fm_tile("sy", xt[0], lambda c: c * TT, "x_fm", 0, TT)]
            for t in range(NTILE):
                if t + 1 < NTILE:
                    p.wait_compute("sy")
                    lds.append(load_fm_tile("sy", xt[(t + 1) % 2], lambda c: c * TT,
                                            "x_fm", (t + 1) * TT, TT))
                x = xt[t % 2]
                p.wait("se", "d_sy", lds[t])
                p.wait("ve", "d_sy", lds[t])
                emit_ln(lambda c, x=x: x[:, c * TT:(c + 1) * TT],
                        lambda c, t=t: xn0[:, c * S + t * TT:c * S + t * TT + TT],
                        TT, sqb, xbb, tmpf, srow, psS, psQ, psA, psB)
            if "xn0" in debug_want:
                tt = tap("xn0", D, S, BF16)
                p.wait_compute("gp")
                for c in range(NCH):
                    p.dma("gp", tt[c * 128:(c + 1) * 128, :], xn0[:, c * S:(c + 1) * S])
            p.barrier()

        if BU >= 3:
          with contextlib.ExitStack() as ph:
            wkv = ph.enter_context(nc.sbuf_tensor("p1wkv", [128, H0 * 128], BF16))
            kf_fm = ph.enter_context(nc.sbuf_tensor("p1kf", [64, 2 * TT], BF16))
            v_fm = ph.enter_context(nc.sbuf_tensor("p1vf", [64, 2 * TT], BF16))
            kfp_tm = ph.enter_context(nc.sbuf_tensor("p1kfp", [128, 4 * 128], BF16))
            vp_tm = ph.enter_context(nc.sbuf_tensor("p1vp", [128, 4 * 128], BF16))
            tmf = ph.enter_context(nc.sbuf_tensor("p1tmf", [64, 2 * TT], F32))
            tme = ph.enter_context(nc.sbuf_tensor("p1tme", [64, 2 * TT], F32))
            zpart = ph.enter_context(nc.sbuf_tensor("p1zp", [64, 2], F32))
            zcf = ph.enter_context(nc.sbuf_tensor("p1zcf", [64, H0], F32))
            kv_acc = ph.enter_context(nc.sbuf_tensor("p1kvacc", [128, 1024], F32))
            kvb_st = ph.enter_context(nc.sbuf_tensor("p1kvst", [128, 1024], BF16))
            Pa = ph.enter_context(nc.psum_tensor("p1Pa", [64, TT], F32))
            Pb = ph.enter_context(nc.psum_tensor("p1Pb", [64, TT], F32))
            Pc = ph.enter_context(nc.psum_tensor("p1Pc", [64, TT], F32))
            Pd = ph.enter_context(nc.psum_tensor("p1Pd", [64, TT], F32))
            PT0 = ph.enter_context(nc.psum_tensor("p1PT0", [128, 128], BF16))
            PT1 = ph.enter_context(nc.psum_tensor("p1PT1", [128, 128], BF16))
            KVP = [ph.enter_context(nc.psum_tensor(f"p1KVP{i}", [128, 128], F32))
                   for i in range(2)]
            p.dma("gp", wkv[0:64, :], hb["l0_wkv"][:, :])
            ld = p.dma("gp", wkv[64:128, :], hb["l0_wkv"][:, :])
            p.wait("te", "d_gp", ld)
            p.op("gp", "memset", kv_acc[:, :], 0.0)
            p.op("gp", "memset", zcf[:, :], 0.0)
            p.wait("ve", "gp", p.cnt["gp"])
            for t in range(NTILE):
                tc0 = t * TT
                for hp_ in range(8):
                    h0, h1 = 2 * hp_, 2 * hp_ + 1
                    p.wait("te", "ve", p.cnt["ve"])
                    p.wait("te", "se", p.cnt["se"])
                    for (PP, h, kvoff) in ((Pa, h0, 0), (Pb, h0, 64),
                                           (Pc, h1, 0), (Pd, h1, 64)):
                        b = (h % 2) * 64
                        p.op("te", "matmul", PP[:, :],
                             wkv[b:b + 64, h * 128 + kvoff:h * 128 + kvoff + 64],
                             xn0[b:b + 64, hp_ * S + tc0:hp_ * S + tc0 + TT],
                             start=True, stop=True, inc=True)
                    p.wait("ve", "te", p.cnt["te"])
                    p.op("ve", "tensor_scalar", tmf[:, 0:TT], Pa[:, :], 0.0, None,
                         ALU.min)
                    p.op("ve", "tensor_scalar", tmf[:, TT:2 * TT], Pc[:, :], 0.0, None,
                         ALU.min)
                    p.op("ve", "tensor_copy", v_fm[:, 0:TT], Pb[:, :])
                    p.op("ve", "tensor_copy", v_fm[:, TT:2 * TT], Pd[:, :])
                    p.wait("se", "ve", p.cnt["ve"])
                    p.op("se", "activation", tme[:, :], tmf[:, :], AF.Exp)
                    p.wait("ve", "se", p.cnt["se"])
                    p.op("ve", "scalar_tensor_tensor", kf_fm[:, 0:TT], Pa[:, :], 0.0,
                         tme[:, 0:TT], ALU.max, ALU.add)
                    p.op("ve", "scalar_tensor_tensor", kf_fm[:, TT:2 * TT], Pc[:, :],
                         0.0, tme[:, TT:2 * TT], ALU.max, ALU.add)
                    p.op("ve", "reduce_sum", zpart[:, 0:1], kf_fm[:, 0:TT],
                         axis=mybir.AxisListType.X)
                    p.op("ve", "reduce_sum", zpart[:, 1:2], kf_fm[:, TT:2 * TT],
                         axis=mybir.AxisListType.X)
                    p.vsync()
                    p.op("ve", "tensor_tensor", zcf[:, h0:h0 + 1], zcf[:, h0:h0 + 1],
                         zpart[:, 0:1], ALU.add)
                    p.op("ve", "tensor_tensor", zcf[:, h1:h1 + 1], zcf[:, h1:h1 + 1],
                         zpart[:, 1:2], ALU.add)
                    evs = p.cnt["ve"]
                    for (src, dstt) in ((kf_fm, kfp_tm), (v_fm, vp_tm)):
                        for g in range(2):
                            for bb in range(4):
                                k_ = g * 4 + bb
                                PT = PT0 if k_ % 2 == 0 else PT1
                                eng = "ve" if k_ % 2 == 0 else "se"
                                p.wait("te", "ve", p.cnt["ve"])
                                p.wait("te", eng, p.cnt[eng])
                                p.op("te", "transpose", PT[:, 0:64],
                                     src[:, g * TT + bb * 128:g * TT + (bb + 1) * 128],
                                     identb[0:64, 0:64], inc=True)
                                p.wait(eng, "te", p.cnt["te"])
                                dst_ = dstt[:, bb * 128 + g * 64:bb * 128 + g * 64 + 64]
                                if eng == "ve":
                                    p.op("ve", "tensor_copy", dst_, PT[:, 0:64])
                                else:
                                    p.op("se", "activation", dst_, PT[:, 0:64], AF.Copy)
                    p.wait("te", "ve", p.cnt["ve"])
                    p.wait("te", "se", p.cnt["se"])
                    KV = KVP[hp_ % 2]
                    for bb in range(4):
                        p.op("te", "matmul", KV[:, :],
                             kfp_tm[:, bb * 128:(bb + 1) * 128],
                             vp_tm[:, bb * 128:(bb + 1) * 128],
                             start=(bb == 0), stop=(bb == 3), inc=(bb == 3))
                    p.wait("ve", "te", p.cnt["te"])
                    p.op("ve", "tensor_tensor", kv_acc[:, hp_ * 128:(hp_ + 1) * 128],
                         kv_acc[:, hp_ * 128:(hp_ + 1) * 128], KV[:, :], ALU.add)
            p.op("ve", "tensor_copy", kvb_st[:, :], kv_acc[:, :])
            p.op("ve", "tensor_copy", z_col[:, :], zcf[:, :])
            p.wait("gp", "ve", p.cnt["ve"])
            for hp_ in range(8):
                p.dma("gp", kv_sb[0:64, (2 * hp_) * 64:(2 * hp_ + 1) * 64],
                      kvb_st[0:64, hp_ * 128:hp_ * 128 + 64])
                p.dma("gp", kv_sb[0:64, (2 * hp_ + 1) * 64:(2 * hp_ + 2) * 64],
                      kvb_st[64:128, hp_ * 128 + 64:hp_ * 128 + 128])
            p.wait("te", "d_gp", p.cnt["d_gp"])
            p.wait("ve", "d_gp", p.cnt["d_gp"])
            if "kv" in debug_want:
                p.wait_compute("gp")
                p.dma("gp", tap("kv", 64, H0 * 64, BF16)[:, :], kv_sb[:, :])
                p.dma("gp", tap("z", 64, H0, BF16)[:, :], z_col[:, :])
            p.barrier()

        if BU >= 4:
          with contextlib.ExitStack() as ph:
            wq = ph.enter_context(nc.sbuf_tensor("p2wq", [128, 1024], BF16))
            wo = ph.enter_context(nc.sbuf_tensor("p2wo", [128, NCH * D], BF16))
            bwT = ph.enter_context(nc.sbuf_tensor("p2bw", [128, NCH * D], BF16))
            rwT = ph.enter_context(nc.sbuf_tensor("p2rw", [128, NCH * 8], BF16))
            rb = ph.enter_context(nc.sbuf_tensor("p2rb", [8, 1], F32))
            lA = ph.enter_context(nc.sbuf_tensor("p2lA", [128, NCH * 8], BF16))
            lB = ph.enter_context(nc.sbuf_tensor("p2lB", [64, D], BF16))
            bb = ph.enter_context(nc.sbuf_tensor("p2bb", [128, 8], F32))
            g8 = ph.enter_context(nc.sbuf_tensor("p2g8", [8, 64], BF16))
            t8 = ph.enter_context(nc.sbuf_tensor("p2t8", [8, 64], BF16))
            qf = ph.enter_context(nc.sbuf_tensor("p2qf", [64, 2 * TT], BF16))
            rden = ph.enter_context(nc.sbuf_tensor("p2rden", [1, TT], F32))
            ctx = ph.enter_context(nc.sbuf_tensor("p2ctx", [128, NCH * TT], BF16))
            s_sb = ph.enter_context(nc.sbuf_tensor("p2s", [128, NCH * TT], F32))
            g_sb = ph.enter_context(nc.sbuf_tensor("p2g", [128, NCH * TT], F32))
            lgt = ph.enter_context(nc.sbuf_tensor("p2lgt", [8, TT], F32))
            gt_tm = ph.enter_context(nc.sbuf_tensor("p2gtm", [128, 4 * 8], BF16))
            gate_fm = ph.enter_context(nc.sbuf_tensor("p2gfm", [8, TT], BF16))
            mid_sb = ph.enter_context(nc.sbuf_tensor("p2mid", [8, TT], BF16))
            gmid = ph.enter_context(nc.sbuf_tensor("p2gmid", [64, TT], BF16))
            xt = ph.enter_context(nc.sbuf_tensor("p2x", [128, NCH * TT], F32))
            x1t = xt
            xn1t = ph.enter_context(nc.sbuf_tensor("p2xn1", [128, NCH * TT], BF16))
            sqb = ph.enter_context(nc.sbuf_tensor("p2sq", [128, NCH * TT], BF16))
            xbb = ph.enter_context(nc.sbuf_tensor("p2xb", [128, NCH * TT], BF16))
            tmpf = ph.enter_context(nc.sbuf_tensor("p2tmp", [128, TT], F32))
            tmpf2 = ph.enter_context(nc.sbuf_tensor("p2tmp2", [64, 2 * TT], F32))
            srow = ph.enter_context(nc.sbuf_tensor("p2srow", [1, 4 * TT], F32))
            rowm = ph.enter_context(nc.sbuf_tensor("p2rowm", [128, 1], F32))
            PA = ph.enter_context(nc.psum_tensor("p2PA", [128, 1024], F32))
            PB = ph.enter_context(nc.psum_tensor("p2PB", [128, 512], F32))
            PC = ph.enter_context(nc.psum_tensor("p2PC", [128, 512], F32))
            PD = ph.enter_context(nc.psum_tensor("p2PD", [128, 512], F32))
            PE = ph.enter_context(nc.psum_tensor("p2PE", [1, 512], F32))
            PF = ph.enter_context(nc.psum_tensor("p2PF", [1, 512], F32))
            PG = ph.enter_context(nc.psum_tensor("p2PG", [8, 512], BF16))
            p.dma("gp", wq[0:64, :], hb["l0_wq"][:, :])
            p.dma("gp", wq[64:128, :], hb["l0_wq"][:, :])
            load_w("gp", wo, "l0_wo", D)
            load_w("gp", bwT, "l0_bwT", D)
            load_w("gp", rwT, "l0_rwT", 8)
            p.dma("sy", rb[:, :], hb["l0_rb"][:, :])
            load_w("gp", lA, "l0_lA", 8)
            p.dma("gp", lB[:, :], hb["l0_lB"][:, :])
            p.dma("sy", bb[:, :], hb["l0_bb"][:, :])
            p.dma("gp", g8[:, :], hb["c_g8"][:, :])
            ld = p.dma("gp", t8[:, :], hb["c_t8"][:, :])
            p.wait("te", "d_gp", ld)
            p.wait("ve", "d_gp", ld)
            p.wait("te", "d_sy", p.cnt["d_sy"])
            p.wait("ve", "d_sy", p.cnt["d_sy"])
            p.op("ve", "tensor_scalar", lB[:, :], lB[:, :], 2.0, None, ALU.mult)
            st_x1 = 0
            consS = {0: 0, 1: 0}
            for t in range(NTILE):
                tc0 = t * TT
                p.wait_compute("sy")
                p.wait("sy", "d_gp", st_x1)
                ldx = load_fm_tile("sy", xt, lambda c: c * TT, "x_fm", tc0, TT)
                for h in range(H0):
                    b = (h % 2) * 64
                    g = h % 2
                    if g == 0:
                        p.wait("te", "ve", p.cnt["ve"])
                        p.wait("te", "se", p.cnt["se"])
                        for g2 in range(2):
                            h2_ = h + g2
                            b2 = (h2_ % 2) * 64
                            p.op("te", "matmul", PA[0:64, g2 * TT:(g2 + 1) * TT],
                                 wq[b2:b2 + 64, h2_ * 64:(h2_ + 1) * 64],
                                 xn0[b2:b2 + 64, (h2_ // 2) * S + tc0:(h2_ // 2) * S + tc0 + TT],
                                 start=True, stop=True, inc=(g2 == 1))
                        p.wait("ve", "te", p.cnt["te"])
                        p.op("ve", "tensor_scalar", tmpf2[:, :], PA[0:64, 0:2 * TT],
                             0.0, None, ALU.min)
                        p.wait("se", "ve", p.cnt["ve"])
                        p.op("se", "activation", tmpf2[:, :], tmpf2[:, :], AF.Exp)
                        p.wait("ve", "se", p.cnt["se"])
                        p.op("ve", "scalar_tensor_tensor", qf[:, :],
                             PA[0:64, 0:2 * TT], 0.0, tmpf2[:, :], ALU.max, ALU.add)
                    p.wait("te", "ve", p.cnt["ve"])
                    p.op("te", "matmul", PC[0:64, 0:TT], kv_sb[:, h * 64:(h + 1) * 64],
                         qf[:, g * TT:(g + 1) * TT], start=True, stop=True)
                    p.op("te", "matmul", PE[0:1, 0:TT], z_col[:, h:h + 1],
                         qf[:, g * TT:(g + 1) * TT], start=True, stop=True, inc=True)
                    p.wait("ve", "te", p.cnt["te"])
                    p.op("ve", "tensor_scalar", rden[0:1, 0:TT], PE[0:1, 0:TT],
                         1e-6, None, ALU.add)
                    p.op("ve", "reciprocal", rden[0:1, 0:TT], rden[0:1, 0:TT])
                    p.wait("te", "ve", p.cnt["ve"])
                    p.op("te", "matmul", PD[0:64, 0:TT], ones1f[0:1, 0:64],
                         rden[0:1, 0:TT], start=True, stop=True, inc=True)
                    p.wait("ve", "te", p.cnt["te"])
                    p.op("ve", "tensor_copy", tmpf2[:, 0:TT], PC[0:64, 0:TT])
                    p.op("ve", "tensor_tensor",
                         ctx[b:b + 64, (h // 2) * TT:(h // 2) * TT + TT],
                         tmpf2[:, 0:TT], PD[0:64, 0:TT], ALU.mult)
                ctx_done = p.cnt["ve"]
                for oc in range(NCH):
                    par = oc % 2
                    PP = PB if par == 0 else PA
                    p.wait("te", "ve", ctx_done)
                    p.wait("te", "se", consS[par])
                    for ec in range(NCH):
                        p.op("te", "matmul", PP[:, 0:TT],
                             wo[:, ec * D + oc * 128:ec * D + oc * 128 + 128],
                             ctx[:, ec * TT:(ec + 1) * TT],
                             start=(ec == 0), stop=(ec == NCH - 1), inc=(ec == NCH - 1))
                    p.wait("se", "te", p.cnt["te"])
                    p.op("se", "activation", s_sb[:, oc * TT:(oc + 1) * TT], PP[:, 0:TT],
                         AF.Copy)
                    consS[par] = p.cnt["se"]
                p.wait("te", "se", p.cnt["se"])
                p.wait("te", "ve", p.cnt["ve"])
                for dc in range(NCH):
                    p.op("te", "matmul", PC[0:8, 0:TT], rwT[:, dc * 8:(dc + 1) * 8],
                         xn0[:, dc * S + tc0:dc * S + tc0 + TT],
                         start=(dc == 0), stop=(dc == NCH - 1), inc=(dc == NCH - 1))
                p.wait("ve", "te", p.cnt["te"])
                p.op("ve", "tensor_scalar", lgt[:, :], PC[0:8, 0:TT], rb[:, 0:1], None,
                     ALU.add)
                for j in range(4):
                    p.wait("te", "ve", p.cnt["ve"])
                    p.wait("te", "se", p.cnt["se"])
                    p.op("te", "transpose", PD[:, 0:8],
                         lgt[0:8, j * 128:(j + 1) * 128], ident[0:8, 0:8], inc=True)
                    p.wait("ve", "te", p.cnt["te"])
                    p.op("ve", "reduce_max", rowm[:, 0:1], PD[:, 0:8],
                         axis=mybir.AxisListType.X, negate=True)
                    p.wait("se", "ve", p.cnt["ve"])
                    p.op("se", "activation", tmpf[:, 0:8], PD[:, 0:8],
                         AF.Exp, bias=rowm[:, 0:1])
                    p.wait("ve", "se", p.cnt["se"])
                    p.op("ve", "reduce_sum", rowm[:, 0:1], tmpf[:, 0:8],
                         axis=mybir.AxisListType.X)
                    p.vsync()
                    p.op("ve", "reciprocal", rowm[:, 0:1], rowm[:, 0:1])
                    p.vsync()
                    p.op("ve", "tensor_scalar", gt_tm[:, j * 8:(j + 1) * 8],
                         tmpf[:, 0:8], rowm[:, 0:1], None, ALU.mult)
                for j in range(4):
                    p.wait("te", "ve", p.cnt["ve"])
                    p.op("te", "transpose", PG[0:8, 0:128],
                         gt_tm[:, j * 8:(j + 1) * 8], identb[:, 0:128], inc=True)
                    p.wait("ve", "te", p.cnt["te"])
                    p.op("ve", "tensor_copy", gate_fm[:, j * 128:(j + 1) * 128],
                         PG[0:8, 0:128])
                p.wait("te", "ve", p.cnt["ve"])
                for dc in range(NCH):
                    p.op("te", "matmul", PD[0:8, 0:TT], lA[:, dc * 8:(dc + 1) * 8],
                         xn0[:, dc * S + tc0:dc * S + tc0 + TT],
                         start=(dc == 0), stop=(dc == NCH - 1), inc=(dc == NCH - 1))
                p.wait("ve", "te", p.cnt["te"])
                p.op("ve", "tensor_copy", mid_sb[:, :], PD[0:8, 0:TT])
                p.wait("te", "ve", p.cnt["ve"])
                p.op("te", "matmul", PA[0:64, 0:TT], g8[:, :], gate_fm[:, :],
                     start=True, stop=True)
                p.op("te", "matmul", PA[0:64, TT:2 * TT], t8[:, :], mid_sb[:, :],
                     start=True, stop=True, inc=True)
                p.wait("ve", "te", p.cnt["te"])
                p.op("ve", "tensor_copy", tmpf2[:, 0:TT], PA[0:64, 0:TT])
                p.op("ve", "tensor_tensor", gmid[:, :], tmpf2[:, 0:TT],
                     PA[0:64, TT:2 * TT], ALU.mult)
                gmid_done = p.cnt["ve"]
                for oc in range(NCH):
                    par = oc % 2
                    PP = PB if par == 0 else PA
                    p.wait("te", "ve", gmid_done)
                    p.wait("te", "se", consS[par])
                    for dc in range(NCH):
                        p.op("te", "matmul", PP[:, 0:TT],
                             bwT[:, dc * D + oc * 128:dc * D + oc * 128 + 128],
                             xn0[:, dc * S + tc0:dc * S + tc0 + TT],
                             start=(dc == 0), stop=False)
                    p.op("te", "matmul", PP[:, 0:TT], lB[:, oc * 128:(oc + 1) * 128],
                         gmid[:, :], start=False, stop=True, inc=True)
                    p.wait("se", "te", p.cnt["te"])
                    p.op("se", "activation", g_sb[:, oc * TT:(oc + 1) * TT], PP[:, 0:TT],
                         AF.Silu, bias=bb[:, oc:oc + 1])
                    consS[par] = p.cnt["se"]
                p.wait("ve", "d_sy", ldx)
                p.wait("ve", "se", p.cnt["se"])
                p.wait("ve", "d_gp", st_x1)
                for c in range(NCH):
                    p.op("ve", "tensor_tensor", tmpf[:, 0:TT], s_sb[:, c * TT:(c + 1) * TT],
                         g_sb[:, c * TT:(c + 1) * TT], ALU.mult)
                    p.op("ve", "tensor_tensor", x1t[:, c * TT:(c + 1) * TT],
                         tmpf[:, 0:TT], xt[:, c * TT:(c + 1) * TT], ALU.add)
                emit_ln(lambda c: x1t[:, c * TT:(c + 1) * TT],
                        lambda c: xn1t[:, c * TT:(c + 1) * TT],
                        TT, sqb, xbb, tmpf, srow, PE, PF, PC, PD)
                p.wait("gp", "ve", p.cnt["ve"])
                store_fm_tile("gp", "x1h", x1t, lambda c: c * TT, tc0, TT)
                st_x1 = store_fm_tile("gp", "xn1h", xn1t, lambda c: c * TT, tc0, TT)
                if "gate_l0" in debug_want:
                    p.wait_compute("gp")
                    p.dma("gp", tap("gate_l0", 8, S, BF16)[:, tc0:tc0 + TT], gate_fm[:, :])
                    p.dma("gp", tap("mid_l0", 8, S, BF16)[:, tc0:tc0 + TT], mid_sb[:, :])
                    p.dma("gp", tap("gmid_l0", 64, S, BF16)[:, tc0:tc0 + TT], gmid[:, :])
                    p.dma("gp", tap("lgt_l0", 8, S)[:, tc0:tc0 + TT], lgt[:, :])
                if "s0" in debug_want:
                    for c in range(NCH):
                        p.dma("gp", tap("s0", D, S)[c * 128:(c + 1) * 128, tc0:tc0 + TT],
                              s_sb[:, c * TT:(c + 1) * TT])
                if "g0" in debug_want:
                    for c in range(NCH):
                        p.dma("gp", tap("g0", D, S)[c * 128:(c + 1) * 128, tc0:tc0 + TT],
                              g_sb[:, c * TT:(c + 1) * TT])
            p.barrier()

    if BU >= 5:
      with contextlib.ExitStack() as ph:
        wk = ph.enter_context(nc.sbuf_tensor("p3wk", [128, NCH * D], BF16))
        wv = ph.enter_context(nc.sbuf_tensor("p3wv", [128, NCH * D], BF16))
        wg = ph.enter_context(nc.sbuf_tensor("p3wg", [128, NCH * 2], BF16))
        xnt = [ph.enter_context(nc.sbuf_tensor(f"p3x{i}", [128, NCH * TT], BF16))
               for i in range(2)]
        kt = ph.enter_context(nc.sbuf_tensor("p3k", [128, NCH * TT], BF16))
        vt = ph.enter_context(nc.sbuf_tensor("p3v", [128, NCH * TT], BF16))
        grow = ph.enter_context(nc.sbuf_tensor("p3grow", [2, 1], F32))
        P3A = ph.enter_context(nc.psum_tensor("p3A", [128, 512], F32))
        P3B = ph.enter_context(nc.psum_tensor("p3B", [128, 512], F32))
        P3C = ph.enter_context(nc.psum_tensor("p3C", [2, 512], F32))
        load_w("gp", wk, "l1_wkT", D)
        load_w("gp", wv, "l1_wvT", D)
        ld = load_w("gp", wg, "l1_wg", 2)
        p.wait("te", "d_gp", ld)
        lds = [load_fm_tile("sy", xnt[0], lambda c: c * TT, "xn1h", 0, TT)]
        st_done = 0
        cons3 = {0: 0, 1: 0}
        for t in range(NTILE):
            if t + 1 < NTILE:
                p.wait_compute("sy")
                lds.append(load_fm_tile("sy", xnt[(t + 1) % 2], lambda c: c * TT,
                                        "xn1h", (t + 1) * TT, TT))
            xn = xnt[t % 2]
            p.wait("te", "d_sy", lds[t])
            p.wait("se", "d_gp", st_done)
            for wsb, dst in ((wk, kt), (wv, vt)):
                for oc in range(NCH):
                    par = oc % 2
                    PP = P3A if par == 0 else P3B
                    p.wait("te", "se", cons3[par])
                    for dc in range(NCH):
                        p.op("te", "matmul", PP[:, 0:TT],
                             wsb[:, dc * D + oc * 128:dc * D + oc * 128 + 128],
                             xn[:, dc * TT:(dc + 1) * TT],
                             start=(dc == 0), stop=(dc == NCH - 1), inc=(dc == NCH - 1))
                    p.wait("se", "te", p.cnt["te"])
                    p.op("se", "activation", dst[:, oc * TT:(oc + 1) * TT], PP[:, 0:TT],
                         AF.Copy)
                    cons3[par] = p.cnt["se"]
            p.wait("te", "ve", p.cnt["ve"])
            for dc in range(NCH):
                p.op("te", "matmul", P3C[0:2, 0:TT], wg[:, dc * 2:(dc + 1) * 2],
                     xn[:, dc * TT:(dc + 1) * TT],
                     start=(dc == 0), stop=(dc == NCH - 1), inc=(dc == NCH - 1))
            p.wait("ve", "te", p.cnt["te"])
            p.op("ve", "reduce_sum", grow[:, 0:1], P3C[0:2, 0:TT],
                 axis=mybir.AxisListType.X)
            p.vsync()
            p.op("ve", "tensor_tensor", gacc[:, :], gacc[:, :], grow[:, :], ALU.add)
            p.wait("gp", "se", p.cnt["se"])
            store_fm_tile("gp", "kfmh", kt, lambda c: c * TT, t * TT, TT)
            st_done = store_fm_tile("gp", "vfmh", vt, lambda c: c * TT, t * TT, TT)
        p.barrier()

    if BU >= 6:
      with contextlib.ExitStack() as ph:
        w0T = ph.enter_context(nc.sbuf_tensor("p4w0", [128, NCH * D], BF16))
        w1T = ph.enter_context(nc.sbuf_tensor("p4w1", [128, NCH * D], BF16))
        w1n = ph.enter_context(nc.sbuf_tensor("p4w1n", [128, NCH * D], BF16))
        ktf = [ph.enter_context(nc.sbuf_tensor(f"p4k{i}", [128, NCH * TT], BF16))
               for i in range(2)]
        vtf = [ph.enter_context(nc.sbuf_tensor(f"p4v{i}", [128, NCH * TT], BF16))
               for i in range(2)]
        a1t = ph.enter_context(nc.sbuf_tensor("p4a1", [128, NCH * TT], BF16))
        d2t = ph.enter_context(nc.sbuf_tensor("p4d2", [128, NCH * TT], BF16))
        d1t = ph.enter_context(nc.sbuf_tensor("p4d1", [128, NCH * TT], BF16))
        sdt = ph.enter_context(nc.sbuf_tensor("p4sd", [128, NCH * TT], F32))
        sgt = ph.enter_context(nc.sbuf_tensor("p4sg", [128, 2 * TT], F32))
        tmpa = ph.enter_context(nc.sbuf_tensor("p4tmpa", [128, 2 * TT], F32))
        tmtile = ph.enter_context(nc.sbuf_tensor("p4tm", [128, 4 * D], BF16))
        P4A = ph.enter_context(nc.psum_tensor("p4A", [128, 512], F32))
        P4B = ph.enter_context(nc.psum_tensor("p4B", [128, 512], F32))
        P4C = ph.enter_context(nc.psum_tensor("p4C", [128, 512], F32))
        P4T = ph.enter_context(nc.psum_tensor("p4T", [128, 128], BF16))
        P4T2 = ph.enter_context(nc.psum_tensor("p4T2", [128, 128], BF16))
        P4A2 = ph.enter_context(nc.psum_tensor("p4A2", [128, 512], F32))
        P4B2 = ph.enter_context(nc.psum_tensor("p4B2", [128, 512], F32))
        P4C2 = ph.enter_context(nc.psum_tensor("p4C2", [128, 512], F32))
        load_w("gp", w0T, "l1_w0T", D)
        load_w("gp", w1T, "l1_w1T", D)
        ld = load_w("gp", w1n, "l1_w1n", D)
        p.wait("te", "d_gp", ld)
        st_tm = [0]

        def transpose_out(src_b, hbm_name, t):
            for oc in range(NCH):
                p.wait("ve", "d_gp", st_tm[0])
                p.wait("se", "d_gp", st_tm[0])
                for s_ in range(4):
                    PT = P4T if s_ % 2 == 0 else P4T2
                    eng = "ve" if s_ % 2 == 0 else "se"
                    p.wait("te", eng, p.cnt[eng])
                    p.op("te", "transpose", PT[:, :],
                         src_b[:, oc * TT + s_ * 128:oc * TT + (s_ + 1) * 128],
                         identb[:, :], inc=True)
                    p.wait(eng, "te", p.cnt["te"])
                    dst_ = tmtile[:, s_ * D + oc * 128:s_ * D + oc * 128 + 128]
                    if eng == "ve":
                        p.op("ve", "tensor_copy", dst_, PT[:, :])
                    else:
                        p.op("se", "activation", dst_, PT[:, :], AF.Copy)
            p.wait("gp", "ve", p.cnt["ve"])
            p.wait("gp", "se", p.cnt["se"])
            for s_ in range(4):
                p.dma("gp", hb[hbm_name][t * TT + s_ * 128:t * TT + (s_ + 1) * 128, :],
                      tmtile[:, s_ * D:(s_ + 1) * D])
            st_tm[0] = p.cnt["d_gp"]

        lds = [(load_fm_tile("sy", ktf[0], lambda c: c * TT, "kfmh", 0, TT),
                load_fm_tile("sy", vtf[0], lambda c: c * TT, "vfmh", 0, TT))]
        consA = {0: (0, 0), 1: (0, 0)}
        consB = {0: 0, 1: 0}
        consC = {0: 0, 1: 0}
        for t in range(NTILE):
            if t + 1 < NTILE:
                p.wait_compute("sy")
                lds.append((load_fm_tile("sy", ktf[(t + 1) % 2], lambda c: c * TT,
                                         "kfmh", (t + 1) * TT, TT),
                            load_fm_tile("sy", vtf[(t + 1) % 2], lambda c: c * TT,
                                         "vfmh", (t + 1) * TT, TT)))
            kt, vt = ktf[t % 2], vtf[t % 2]
            p.wait("te", "d_sy", lds[t][1])
            for oc in range(NCH):
                par = oc % 2
                PP = P4A if par == 0 else P4A2
                sg_ = sgt[:, 0:TT] if par == 0 else sgt[:, TT:2 * TT]
                tm_ = tmpa[:, 0:TT] if par == 0 else tmpa[:, TT:2 * TT]
                p.wait("te", "se", consA[par][0])
                p.wait("te", "ve", consA[par][1])
                for dc in range(NCH):
                    p.op("te", "matmul", PP[:, 0:TT],
                         w0T[:, dc * D + oc * 128:dc * D + oc * 128 + 128],
                         kt[:, dc * TT:(dc + 1) * TT],
                         start=(dc == 0), stop=(dc == NCH - 1), inc=(dc == NCH - 1))
                p.wait("se", "te", p.cnt["te"])
                p.op("se", "activation", a1t[:, oc * TT:(oc + 1) * TT], PP[:, 0:TT],
                     AF.Silu)
                p.op("se", "activation", sg_, PP[:, 0:TT], AF.Sigmoid)
                p.wait("ve", "se", p.cnt["se"])
                p.op("ve", "tensor_tensor", tm_, sg_,
                     a1t[:, oc * TT:(oc + 1) * TT], ALU.mult)
                p.op("ve", "tensor_tensor", sdt[:, oc * TT:(oc + 1) * TT], sg_,
                     a1t[:, oc * TT:(oc + 1) * TT], ALU.add)
                p.op("ve", "tensor_tensor", sdt[:, oc * TT:(oc + 1) * TT],
                     sdt[:, oc * TT:(oc + 1) * TT], tm_, ALU.subtract)
                consA[par] = (p.cnt["se"], p.cnt["ve"])
            p.wait("te", "se", p.cnt["se"])
            for oc in range(NCH):
                par = oc % 2
                PP = P4B if par == 0 else P4B2
                tm_ = tmpa[:, 0:TT] if par == 0 else tmpa[:, TT:2 * TT]
                p.wait("te", "ve", consB[par])
                for ec in range(NCH):
                    p.op("te", "matmul", PP[:, 0:TT],
                         w1T[:, ec * D + oc * 128:ec * D + oc * 128 + 128],
                         a1t[:, ec * TT:(ec + 1) * TT],
                         start=(ec == 0), stop=(ec == NCH - 1), inc=(ec == NCH - 1))
                p.wait("ve", "te", p.cnt["te"])
                p.op("ve", "tensor_tensor", tm_, PP[:, 0:TT],
                     vt[:, oc * TT:(oc + 1) * TT], ALU.subtract)
                p.op("ve", "tensor_scalar", d2t[:, oc * TT:(oc + 1) * TT], tm_,
                     -5.0, 5.0, ALU.max, ALU.min)
                consB[par] = p.cnt["ve"]
            p.wait("te", "ve", p.cnt["ve"])
            for oc in range(NCH):
                par = oc % 2
                PP = P4C if par == 0 else P4C2
                p.wait("te", "ve", consC[par])
                for dc in range(NCH):
                    p.op("te", "matmul", PP[:, 0:TT],
                         w1n[:, dc * D + oc * 128:dc * D + oc * 128 + 128],
                         d2t[:, dc * TT:(dc + 1) * TT],
                         start=(dc == 0), stop=(dc == NCH - 1), inc=(dc == NCH - 1))
                p.wait("ve", "te", p.cnt["te"])
                p.op("ve", "scalar_tensor_tensor", d1t[:, oc * TT:(oc + 1) * TT],
                     PP[:, 0:TT], SCALE2S, sdt[:, oc * TT:(oc + 1) * TT],
                     ALU.mult, ALU.mult)
                consC[par] = p.cnt["ve"]
            transpose_out(a1t, "a1tmh", t)
            transpose_out(d2t, "d2tmh", t)
            transpose_out(d1t, "d1tmh", t)
            transpose_out(kt, "ktmh", t)
        p.barrier()

    if BU >= 7:
      with contextlib.ExitStack() as ph:
        G0T = ph.enter_context(nc.sbuf_tensor("G0T", [128, NCH * D], F32))
        G1T = ph.enter_context(nc.sbuf_tensor("G1T", [128, NCH * D], F32))
        rhsb = [ph.enter_context(nc.sbuf_tensor(f"p5r{i}", [128, D], BF16))
                for i in range(2)]
        lhsb = [ph.enter_context(nc.sbuf_tensor(f"p5l{i}", [128, 512], BF16))
                for i in range(2)]
        GP = [ph.enter_context(nc.psum_tensor(f"p5G{j}", [128, D], F32))
              for j in range(4)]
        NB = S // 128
        for (lhs_name, rhs_name, dst) in (("a1tmh", "d2tmh", G1T),
                                          ("ktmh", "d1tmh", G0T)):
            for half in range(2):
                lds = [(p.dma("sy", lhsb[0][:, :],
                              hb[lhs_name][0:128, half * 512:(half + 1) * 512]),
                        p.dma("sy", rhsb[0][:, :], hb[rhs_name][0:128, :]))]
                for tb in range(NB):
                    if tb + 1 < NB:
                        p.wait_compute("sy")
                        lds.append((
                            p.dma("sy", lhsb[(tb + 1) % 2][:, :],
                                  hb[lhs_name][(tb + 1) * 128:(tb + 2) * 128,
                                               half * 512:(half + 1) * 512]),
                            p.dma("sy", rhsb[(tb + 1) % 2][:, :],
                                  hb[rhs_name][(tb + 1) * 128:(tb + 2) * 128, :])))
                    p.wait("te", "d_sy", lds[tb][1])
                    for j in range(4):
                        for g in range(2):
                            p.op("te", "matmul", GP[j][:, g * 512:(g + 1) * 512],
                                 lhsb[tb % 2][:, j * 128:(j + 1) * 128],
                                 rhsb[tb % 2][:, g * 512:(g + 1) * 512],
                                 start=(tb == 0), stop=(tb == NB - 1),
                                 inc=(j == 3 and g == 1),
                                 skip_group_check=True)
                p.wait("ve", "te", p.cnt["te"])
                for j in range(4):
                    p.op("ve", "tensor_copy",
                         dst[:, (half * 4 + j) * D:(half * 4 + j + 1) * D], GP[j][:, :])
                p.wait("te", "ve", p.cnt["ve"])
        p.wait("gp", "ve", p.cnt["ve"])
        for c in range(NCH):
            p.dma("gp", hb["cc_in"][c * 128:(c + 1) * 128, :], G0T[:, c * D:(c + 1) * D])
            p.dma("gp", hb["cc_in"][1024 + c * 128:1024 + (c + 1) * 128, :],
                  G1T[:, c * D:(c + 1) * D])
        p.dma("gp", hb["cc_in"][2048:2050, 0:1], gacc[0:2, 0:1], allow_slow_non_contiguous=True)
        p.wait("gp", "d_gp", p.cnt["d_gp"])
        p.ccop("AllReduce", ALU.add, replica_groups=[list(range(NCORE))],
               ins=[hb["cc_in"].ap()], outs=[hb["cc_out"].ap()])
        p.barrier()

    def ffn_layer(l, in_name, out_name):
        sfx = str(l)
        subs = FFN_SUBS
        if "A" in subs:
          with contextlib.ExitStack() as ph:
            wqkv = ph.enter_context(nc.sbuf_tensor("pAwqkv" + sfx, [128, NCH * 3 * INNER], BF16))
            wo = ph.enter_context(nc.sbuf_tensor("pAwo" + sfx, [128, 4 * D], BF16))
            qsv = ph.enter_context(nc.sbuf_tensor("pAqs" + sfx, [128, 2], F32))
            ksv = ph.enter_context(nc.sbuf_tensor("pAks" + sfx, [128, 1], F32))
            h2b = ph.enter_context(nc.sbuf_tensor("pAh2" + sfx, [128, 2], BF16))
            b2t = ph.enter_context(nc.sbuf_tensor("pAb2t" + sfx, [2, 128], F32))
            rtp = ph.enter_context(nc.sbuf_tensor("pArt" + sfx, [128, 128], F32))
            tlo = ph.enter_context(nc.sbuf_tensor("pAtlo" + sfx, [128, 128], F32))
            thi = ph.enter_context(nc.sbuf_tensor("pAthi" + sfx, [128, 128], F32))
            ywm = ph.enter_context(nc.sbuf_tensor("pAywm" + sfx, [128, NCH * WIN], BF16))
            ywr = ph.enter_context(nc.sbuf_tensor("pAywr" + sfx, [128, NCH * WIN], F32))
            hbot = ph.enter_context(nc.sbuf_tensor("pAhbot" + sfx, [128, 4 * WIN], BF16))
            qf32 = ph.enter_context(nc.sbuf_tensor("pAqf" + sfx, [128, 4 * WIN], F32))
            kf32 = ph.enter_context(nc.sbuf_tensor("pAkf" + sfx, [128, 4 * WIN], F32))
            vbf = ph.enter_context(nc.sbuf_tensor("pAvbf" + sfx, [128, WIN], BF16))
            sqb = ph.enter_context(nc.sbuf_tensor("pAsq" + sfx, [128, 2 * WIN], BF16))
            nrow = ph.enter_context(nc.sbuf_tensor("pAnrow" + sfx, [2, 2 * WIN], F32))
            tm1 = ph.enter_context(nc.sbuf_tensor("pAtm1" + sfx, [128, 2 * WIN], F32))
            tm2 = ph.enter_context(nc.sbuf_tensor("pAtm2" + sfx, [128, 2 * WIN], F32))
            tabs = ph.enter_context(nc.sbuf_tensor("pAtabs" + sfx, [128, 4 * WIN], F32))
            qrot = ph.enter_context(nc.sbuf_tensor("pAqrot" + sfx, [128, 4 * WIN], BF16))
            krot = [ph.enter_context(nc.sbuf_tensor(f"pAkrot{i}" + sfx, [128, 4 * WIN], BF16))
                    for i in range(2)]
            vtm = [ph.enter_context(nc.sbuf_tensor(f"pAvtm{i}" + sfx, [128, 4 * WIN], BF16))
                   for i in range(2)]
            attn2_ = [ph.enter_context(nc.sbuf_tensor(f"pAattn{i}" + sfx, [128, 1024], BF16))
                      for i in range(2)]
            attnT2_ = [ph.enter_context(nc.sbuf_tensor(f"pAattnT{i}" + sfx, [128, 640], BF16))
                       for i in range(2)]
            rowm = ph.enter_context(nc.sbuf_tensor("pArowm" + sfx, [128, 4], F32))
            ofm = ph.enter_context(nc.sbuf_tensor("pAofm" + sfx, [128, 4 * WIN], BF16))
            yA = ph.enter_context(nc.sbuf_tensor("pAyA" + sfx, [128, NCH * WIN], F32))
            psA = ph.enter_context(nc.psum_tensor("pApsA" + sfx, [128, 512], F32))
            pscp2_ = [ph.enter_context(nc.psum_tensor(f"pAscp{i}" + sfx, [128, 1024], F32))
                      for i in range(2)]
            atpA = ph.enter_context(nc.psum_tensor("pAatpA" + sfx, [128, 128], BF16))
            atpB = ph.enter_context(nc.psum_tensor("pAatpB" + sfx, [128, 128], BF16))
            avp = ph.enter_context(nc.psum_tensor("pAavp" + sfx, [64, 128], F32))
            load_w("gp", wqkv, "f_wqkv", 3 * INNER, row0=l * D)
            load_w("gp", wo, "f_wo", D, row0=l * INNER, nrows=INNER)
            p.dma("sy", qsv[:, 0:1], hb["f_qs"][:, l:l + 1], allow_slow_non_contiguous=True)
            p.dma("sy", ksv[:, :], hb["f_ks"][:, l:l + 1], allow_slow_non_contiguous=True)
            p.dma("gp", h2b[:, :], hb["c_h2"][:, :])
            p.dma("sy", b2t[:, :], hb["c_b2t"][:, :])
            p.dma("sy", rtp[:, :], hb["c_rt"][:, :])
            p.dma("sy", tlo[:, :], hb["c_tlo"][:, :])
            ldw = p.dma("sy", thi[:, :], hb["c_thi"][:, :])
            p.wait("ve", "d_sy", ldw)
            p.wait("ve", "d_gp", p.cnt["d_gp"])
            p.wait("te", "d_gp", p.cnt["d_gp"])
            p.op("ve", "tensor_scalar", qsv[:, 1:2], qsv[:, 0:1], 8.0, None, ALU.mult)
            st_yA = 0
            for w in range(NCH):
                t0 = w * WIN
                p.wait_compute("sy")
                p.wait_compute("gp")
                load_fm_tile("gp", ywm, lambda c: c * WIN, in_name, t0, WIN)
                load_fm_tile("gp", ywr, lambda c: c * WIN, in_name, t0, WIN)
                if w == 0:
                    for c in range(4):
                        p.op("gp", "memset", hbot[:, c * WIN:c * WIN + 1], 0.0)
                    for c in range(4):
                        p.dma("gp", hbot[:, c * WIN + 1:(c + 1) * WIN],
                              hb[in_name][(4 + c) * 128:(5 + c) * 128, 0:WIN - 1])
                else:
                    for c in range(4):
                        p.dma("gp", hbot[:, c * WIN:(c + 1) * WIN],
                              hb[in_name][(4 + c) * 128:(5 + c) * 128, t0 - 1:t0 + WIN - 1])
                for ti, tn in enumerate(("tabC", "tabS", "tabCi", "tabSi")):
                    p.dma("sy", tabs[:, ti * WIN:(ti + 1) * WIN], hb[tn][:, t0:t0 + WIN])
                ld2 = p.cnt["d_sy"]
                ld2g = p.cnt["d_gp"]
                consQK = {0: 0, 1: 0}
                for oc in range(12):
                    par = oc % 2
                    PQK = psA if (par == 0 or oc >= 8) else pscp2_[0]
                    p.wait("te", "d_sy", ld2)
                    p.wait("te", "d_gp", ld2g)
                    if oc == 0:
                        p.wait("te", "ve", p.cnt["ve"])
                        p.wait("te", "se", p.cnt["se"])
                    if oc >= 8:
                        p.wait("te", "se", p.cnt["se"])
                        p.wait("te", "ve", p.cnt["ve"])
                    else:
                        p.wait("te", "se", consQK[par])
                    for dc in range(NCH):
                        rhs = (ywm[:, dc * WIN:(dc + 1) * WIN] if dc < 4 else
                               hbot[:, (dc - 4) * WIN:(dc - 3) * WIN])
                        p.op("te", "matmul", PQK[:, 0:WIN],
                             wqkv[:, dc * 3 * INNER + oc * 128:dc * 3 * INNER + oc * 128 + 128],
                             rhs, start=(dc == 0), stop=(dc == NCH - 1),
                             inc=(dc == NCH - 1))
                    p.wait("se", "te", p.cnt["te"])
                    if oc < 4:
                        p.op("se", "activation", qf32[:, oc * WIN:(oc + 1) * WIN],
                             PQK[:, 0:WIN], AF.Copy)
                        consQK[par] = p.cnt["se"]
                    elif oc < 8:
                        p.op("se", "activation", kf32[:, (oc - 4) * WIN:(oc - 3) * WIN],
                             PQK[:, 0:WIN], AF.Copy)
                        consQK[par] = p.cnt["se"]
                    else:
                        p.op("se", "activation", vbf[:, :], PQK[:, 0:WIN], AF.Copy)
                        for s_ in range(4):
                            PT = atpA if s_ % 2 == 0 else atpB
                            eng = "ve" if s_ % 2 == 0 else "se"
                            p.wait("te", "se", p.cnt["se"])
                            p.wait("te", eng, p.cnt[eng])
                            p.op("te", "transpose", PT[:, :],
                                 vbf[:, s_ * 128:(s_ + 1) * 128], identb[:, :],
                                 inc=True)
                            p.wait(eng, "te", p.cnt["te"])
                            dst_ = vtm[w % 2][:, s_ * WIN + (oc - 8) * 128:
                                              s_ * WIN + (oc - 8) * 128 + 128]
                            if eng == "ve":
                                p.op("ve", "tensor_copy", dst_, PT[:, :])
                            else:
                                p.op("se", "activation", dst_, PT[:, :], AF.Copy)
                consR = {0: (0, 0), 1: (0, 0)}
                for idx in (range(8) if FFN_ASEC in ("rot", "attn", "all") else ()):
                    kq, c = idx // 4, idx % 4
                    par = idx % 2
                    PR = psA if par == 0 else pscp2_[1]
                    sq_ = sqb[:, par * WIN:(par + 1) * WIN]
                    nr_ = nrow[:, par * WIN:(par + 1) * WIN]
                    t1_ = tm1[:, par * WIN:(par + 1) * WIN]
                    t2_ = tm2[:, par * WIN:(par + 1) * WIN]
                    src = qf32 if kq == 0 else kf32
                    dst = qrot if kq == 0 else krot[w % 2]
                    ct0 = 0 if kq == 0 else 2 * WIN
                    scl = qsv[:, 1:2] if kq == 0 else ksv[:, 0:1]
                    if idx == 0:
                        p.wait("te", "ve", p.cnt["ve"])
                        p.wait("te", "se", p.cnt["se"])
                    p.wait("se", "ve", p.cnt["ve"])
                    p.op("se", "activation", sq_, src[:, c * WIN:(c + 1) * WIN],
                         AF.Square)
                    p.wait("te", "se", p.cnt["se"])
                    p.wait("te", "ve", consR[par][1])
                    p.op("te", "matmul", PR[0:2, 0:WIN], h2b[:, :], sq_,
                         start=True, stop=True, inc=True)
                    p.wait("se", "te", p.cnt["te"])
                    p.op("se", "activation", nr_, PR[0:2, 0:WIN], AF.Sqrt)
                    p.wait("ve", "se", p.cnt["se"])
                    p.op("ve", "tensor_scalar", nr_, nr_, 1e-12, None, ALU.max)
                    p.op("ve", "reciprocal", nr_, nr_)
                    p.wait("te", "ve", p.cnt["ve"])
                    p.op("te", "matmul", PR[:, 0:WIN], b2t[:, :], nr_[0:2, :],
                         start=True, stop=True, inc=True)
                    p.wait("ve", "te", p.cnt["te"])
                    p.op("ve", "tensor_tensor", t1_,
                         src[:, c * WIN:(c + 1) * WIN], PR[:, 0:WIN], ALU.mult)
                    p.op("ve", "tensor_scalar", t1_, t1_, scl, None, ALU.mult)
                    p.wait("te", "ve", p.cnt["ve"])
                    p.op("te", "matmul", PR[:, 0:WIN], rtp[:, :], t1_,
                         start=True, stop=True, inc=True)
                    p.wait("ve", "te", p.cnt["te"])
                    p.op("ve", "tensor_tensor", t2_, t1_,
                         tabs[:, ct0:ct0 + WIN], ALU.mult)
                    p.op("ve", "tensor_tensor", t1_, PR[:, 0:WIN],
                         tabs[:, ct0 + WIN:ct0 + 2 * WIN], ALU.mult)
                    p.op("ve", "tensor_tensor", dst[:, c * WIN:(c + 1) * WIN],
                         t2_, t1_, ALU.add)
                    consR[par] = (p.cnt["se"], p.cnt["ve"])
                its = ([(h, qc) for h in range(FH) for qc in range(4)]
                       if FFN_ASEC in ("attn", "all") else [])
                s1_te = {}
                pscp_free = {0: (0, 0), 1: (0, 0)}
                avp_free = [0]

                def emit_S1(i):
                    h, qc = its[i]
                    hbase = (h % 2) * 64
                    hc = h // 2
                    lo = qc * 128
                    bhi = lo + 640 if w > 0 else lo + 128
                    par = i % 2
                    PSC = pscp2_[par]
                    p.wait("te", "ve", pscp_free[par][0])
                    p.wait("te", "se", pscp_free[par][1])
                    if i == 0:
                        p.wait("te", "ve", p.cnt["ve"])
                        p.wait("te", "se", p.cnt["se"])
                    if w > 0:
                        p.op("te", "matmul", PSC[:, lo:512],
                             qrot[hbase:hbase + 64, hc * WIN + lo:hc * WIN + lo + 128],
                             krot[(w + 1) % 2][hbase:hbase + 64, hc * WIN + lo:hc * WIN + 512],
                             start=True, stop=True)
                        p.op("te", "matmul", PSC[:, 512:bhi],
                             qrot[hbase:hbase + 64, hc * WIN + lo:hc * WIN + lo + 128],
                             krot[w % 2][hbase:hbase + 64, hc * WIN:hc * WIN + bhi - 512],
                             start=True, stop=True, inc=True)
                    else:
                        p.op("te", "matmul", PSC[:, 0:bhi],
                             qrot[hbase:hbase + 64, hc * WIN + lo:hc * WIN + lo + 128],
                             krot[0][hbase:hbase + 64, hc * WIN:hc * WIN + bhi],
                             start=True, stop=True, inc=True)
                    s1_te[i] = p.cnt["te"]

                def emit_S2(i):
                    h, qc = its[i]
                    hbase = (h % 2) * 64
                    hc = h // 2
                    lo = qc * 128
                    if w > 0:
                        blo, bhi = lo, lo + 640
                    else:
                        blo, bhi = 0, lo + 128
                    par = i % 2
                    PSC = pscp2_[par]
                    attn = attn2_[par]
                    attnT = attnT2_[par]
                    rm0 = rowm[:, 2 * par:2 * par + 1]
                    rm1 = rowm[:, 2 * par + 1:2 * par + 2]
                    p.wait("ve", "te", s1_te[i])
                    if w > 0:
                        p.op("ve", "tensor_tensor", PSC[:, lo:lo + 128],
                             PSC[:, lo:lo + 128], tlo[:, :], ALU.add)
                    p.op("ve", "tensor_tensor", PSC[:, bhi - 128:bhi],
                         PSC[:, bhi - 128:bhi], thi[:, :], ALU.add)
                    p.op("ve", "reduce_max", rm0, PSC[:, blo:bhi],
                         axis=mybir.AxisListType.X, negate=True)
                    ve_masks = p.cnt["ve"]
                    p.wait("se", "ve", ve_masks)
                    p.op("se", "activation", attn[:, blo:bhi], PSC[:, blo:bhi],
                         AF.Exp, bias=rm0, accum_out=rm1)
                    se_exp = p.cnt["se"]
                    pscp_free[par] = (ve_masks, se_exp)
                    p.wait("ve", "se", se_exp)
                    p.op("ve", "reciprocal", rm1, rm1)
                    p.vsync()
                    p.op("ve", "tensor_scalar", attn[:, blo:bhi], attn[:, blo:bhi],
                         rm1, None, ALU.mult)
                    nj = (bhi - blo) // 128
                    for jb in range(nj):
                        PT = atpA if jb % 2 == 0 else atpB
                        eng = "ve" if jb % 2 == 0 else "se"
                        p.wait("te", eng, p.cnt[eng])
                        p.op("te", "transpose", PT[:, :],
                             attn[:, blo + jb * 128:blo + (jb + 1) * 128],
                             identb[:, :], inc=True)
                        p.wait(eng, "te", p.cnt["te"])
                        if eng == "ve":
                            p.op("ve", "tensor_copy",
                                 attnT[:, jb * 128:(jb + 1) * 128], PT[:, :])
                        else:
                            p.op("se", "activation",
                                 attnT[:, jb * 128:(jb + 1) * 128], PT[:, :], AF.Copy)
                    p.wait("te", "ve", p.cnt["ve"])
                    p.wait("te", "se", p.cnt["se"])
                    for jb in range(nj):
                        jl = blo + jb * 128
                        if w > 0:
                            if jl < 512:
                                vsl = vtm[(w + 1) % 2][:, (jl // 128) * WIN + h * 64:
                                                       (jl // 128) * WIN + h * 64 + 64]
                            else:
                                vsl = vtm[w % 2][:, ((jl - 512) // 128) * WIN + h * 64:
                                                 ((jl - 512) // 128) * WIN + h * 64 + 64]
                        else:
                            vsl = vtm[0][:, (jl // 128) * WIN + h * 64:
                                         (jl // 128) * WIN + h * 64 + 64]
                        p.op("te", "matmul", avp[:, 0:128], vsl,
                             attnT[:, jb * 128:(jb + 1) * 128],
                             start=(jb == 0), stop=(jb == nj - 1), inc=(jb == nj - 1))
                    p.wait("se", "te", p.cnt["te"])
                    p.op("se", "activation",
                         ofm[hbase:hbase + 64, hc * WIN + lo:hc * WIN + lo + 128],
                         avp[:, 0:128], AF.Copy)
                    avp_free[0] = p.cnt["se"]

                if its:
                    emit_S1(0)
                    for i in range(len(its)):
                        if i + 1 < len(its):
                            emit_S1(i + 1)
                        emit_S2(i)
                ofm_done = p.cnt["ve"]
                consWO = {0: 0, 1: 0}
                first_wo = [True]
                for oc in (range(NCH) if FFN_ASEC == "all" else ()):
                    par = oc % 2
                    PWO = psA if par == 0 else pscp2_[0]
                    if first_wo[0]:
                        p.wait("te", "ve", p.cnt["ve"])
                        p.wait("te", "se", p.cnt["se"])
                        first_wo[0] = False
                    p.wait("te", "ve", consWO[par])
                    for kc in range(4):
                        p.op("te", "matmul", PWO[:, 0:WIN],
                             wo[:, kc * D + oc * 128:kc * D + oc * 128 + 128],
                             ofm[:, kc * WIN:(kc + 1) * WIN],
                             start=(kc == 0), stop=(kc == 3), inc=(kc == 3))
                    p.wait("ve", "te", p.cnt["te"])
                    p.wait("ve", "d_sy", ld2)
                    p.wait("ve", "d_gp", ld2g)
                    p.op("ve", "tensor_tensor", yA[:, oc * WIN:(oc + 1) * WIN],
                         PWO[:, 0:WIN], ywr[:, oc * WIN:(oc + 1) * WIN], ALU.add)
                    consWO[par] = p.cnt["ve"]
                if FFN_ASEC == "all":
                    p.wait("gp", "ve", p.cnt["ve"])
                    st_yA = store_fm_tile("gp", "yAh", yA, lambda c: c * WIN, t0, WIN)
            p.barrier()
        if "1" in subs:
          with contextlib.ExitStack() as ph:
            w1 = ph.enter_context(nc.sbuf_tensor("pB1w1" + sfx, [128, NCH * W1OUT], BF16))
            lg = ph.enter_context(nc.sbuf_tensor("pB1lg" + sfx, [128, NCH], F32))
            lb = ph.enter_context(nc.sbuf_tensor("pB1lb" + sfx, [128, NCH], F32))
            b1 = ph.enter_context(nc.sbuf_tensor("pB1b1" + sfx, [128, 44], F32))
            h2 = ph.enter_context(nc.sbuf_tensor("pB1h2" + sfx, [128, NCH * WIN], F32))
            h2n = ph.enter_context(nc.sbuf_tensor("pB1h2n" + sfx, [128, NCH * WIN], BF16))
            sqb = ph.enter_context(nc.sbuf_tensor("pB1sq" + sfx, [128, NCH * WIN], BF16))
            xbb = ph.enter_context(nc.sbuf_tensor("pB1xb" + sfx, [128, NCH * WIN], BF16))
            tmpf = ph.enter_context(nc.sbuf_tensor("pB1tmp" + sfx, [128, WIN], F32))
            gel = ph.enter_context(nc.sbuf_tensor("pB1gel" + sfx, [128, 2 * WIN], F32))
            srow = ph.enter_context(nc.sbuf_tensor("pB1srow" + sfx, [1, 4 * WIN], F32))
            act = ph.enter_context(nc.sbuf_tensor("pB1act" + sfx, [128, NJ * WIN], BF16))
            psS = ph.enter_context(nc.psum_tensor("pB1psS" + sfx, [1, WIN], F32))
            psQ = ph.enter_context(nc.psum_tensor("pB1psQ" + sfx, [1, WIN], F32))
            psA = ph.enter_context(nc.psum_tensor("pB1psA" + sfx, [128, WIN], F32))
            psB = ph.enter_context(nc.psum_tensor("pB1psB" + sfx, [128, WIN], F32))
            psa = [ph.enter_context(nc.psum_tensor(f"pB1psa{i}" + sfx, [128, WIN], F32))
                   for i in range(2)]
            psg = [ph.enter_context(nc.psum_tensor(f"pB1psg{i}" + sfx, [128, WIN], F32))
                   for i in range(2)]
            load_w("gp", w1, "f_w1", W1OUT, row0=l * D)
            p.wait("te", "d_gp", p.cnt["d_gp"])
            p.dma("sy", lg[:, :], hb["f_lng"][:, l * 8:(l + 1) * 8])
            p.dma("sy", lb[:, :], hb["f_lnb"][:, l * 8:(l + 1) * 8])
            p.dma("sy", b1[:, :], hb["f_b1"][:, l * 44:(l + 1) * 44])
            st_act = 0
            consW1 = {0: (0, 0), 1: (0, 0)}
            for w in range(NCH):
                t0 = w * WIN
                p.wait_compute("sy")
                for c in range(4):
                    p.dma("sy", h2[:, c * WIN:(c + 1) * WIN],
                          hb["yAh"][c * 128:(c + 1) * 128, t0:t0 + WIN])
                if w == 0:
                    for c in range(4):
                        p.op("gp", "memset", h2[:, (4 + c) * WIN:(4 + c) * WIN + 1], 0.0)
                    p.wait("sy", "gp", p.cnt["gp"])
                    for c in range(4):
                        p.dma("sy", h2[:, (4 + c) * WIN + 1:(5 + c) * WIN],
                              hb["yAh"][(4 + c) * 128:(5 + c) * 128, 0:WIN - 1])
                else:
                    for c in range(4):
                        p.dma("sy", h2[:, (4 + c) * WIN:(5 + c) * WIN],
                              hb["yAh"][(4 + c) * 128:(5 + c) * 128, t0 - 1:t0 + WIN - 1])
                ldh = p.cnt["d_sy"]
                p.wait("se", "d_sy", ldh)
                p.wait("ve", "d_sy", ldh)
                emit_ln(lambda c: h2[:, c * WIN:(c + 1) * WIN],
                        lambda c: h2n[:, c * WIN:(c + 1) * WIN],
                        WIN, sqb, xbb, tmpf, srow, psS, psQ, psA, psB,
                        lg_col=lg, lb_col=lb)
                p.wait("te", "ve", p.cnt["ve"])
                for j in range(NJ):
                    m = CHJ[j]
                    par = j % 2
                    PSA, PSG = psa[par], psg[par]
                    gl_ = gel[:, 0:WIN] if par == 0 else gel[:, WIN:2 * WIN]
                    p.wait("te", "ve", consW1[par][1])
                    p.wait("te", "se", consW1[par][0])
                    for dc in range(NCH):
                        p.op("te", "matmul", PSA[0:m, 0:WIN],
                             w1[:, dc * W1OUT + j * 128:dc * W1OUT + j * 128 + m],
                             h2n[:, dc * WIN:(dc + 1) * WIN],
                             start=(dc == 0), stop=(dc == NCH - 1), inc=(dc == NCH - 1))
                    for dc in range(NCH):
                        p.op("te", "matmul", PSG[0:m, 0:WIN],
                             w1[:, dc * W1OUT + HID + j * 128:dc * W1OUT + HID + j * 128 + m],
                             h2n[:, dc * WIN:(dc + 1) * WIN],
                             start=(dc == 0), stop=(dc == NCH - 1), inc=(dc == NCH - 1))
                    p.wait("se", "te", p.cnt["te"])
                    p.op("se", "activation", gl_[0:m, 0:WIN], PSG[0:m, 0:WIN], AF.Gelu,
                         bias=b1[0:m, 22 + j:23 + j])
                    p.wait("ve", "se", p.cnt["se"])
                    p.wait("ve", "d_gp", st_act)
                    p.op("ve", "scalar_tensor_tensor", act[0:m, j * WIN:(j + 1) * WIN],
                         PSA[0:m, 0:WIN], b1[0:m, j:j + 1], gl_[0:m, 0:WIN],
                         ALU.add, ALU.mult)
                    consW1[par] = (p.cnt["se"], p.cnt["ve"])
                p.wait("gp", "ve", p.cnt["ve"])
                for j in range(NJ):
                    p.dma("gp", hb["acth"][j * 128:j * 128 + CHJ[j], t0:t0 + WIN],
                          act[0:CHJ[j], j * WIN:(j + 1) * WIN])
                st_act = p.cnt["d_gp"]
            p.barrier()
        if "2" in subs:
          with contextlib.ExitStack() as ph:
            w2 = ph.enter_context(nc.sbuf_tensor("pB2w2" + sfx, [128, NJ * D], BF16))
            b2 = ph.enter_context(nc.sbuf_tensor("pB2b2" + sfx, [128, NCH], F32))
            actt = [ph.enter_context(nc.sbuf_tensor(f"pB2a{i}" + sfx, [128, NJ * WIN], BF16))
                    for i in range(2)]
            ywr = [ph.enter_context(nc.sbuf_tensor(f"pB2y{i}" + sfx, [128, NCH * WIN], F32))
                   for i in range(2)]
            yo = ph.enter_context(nc.sbuf_tensor("pB2yo" + sfx, [128, NCH * WIN], F32))
            ps2 = [ph.enter_context(nc.psum_tensor(f"pB2ps{i}" + sfx, [128, WIN], F32))
                   for i in range(2)]
            for j in range(NJ):
                p.dma("gp", w2[0:CHJ[j], j * D:j * D + D],
                      hb["f_w2"][l * HID + j * 128:l * HID + j * 128 + CHJ[j], 0:D])
            p.dma("sy", b2[:, :], hb["f_b2"][:, l * 8:(l + 1) * 8])
            p.wait("te", "d_gp", p.cnt["d_gp"])

            def ldact(w, buf):
                for j in range(NJ):
                    p.dma("sy", actt[buf][0:CHJ[j], j * WIN:(j + 1) * WIN],
                          hb["acth"][j * 128:j * 128 + CHJ[j], w * WIN:(w + 1) * WIN])
                for c in range(NCH):
                    p.dma("sy", ywr[buf][:, c * WIN:(c + 1) * WIN],
                          hb["yAh"][c * 128:(c + 1) * 128, w * WIN:(w + 1) * WIN])
                return p.cnt["d_sy"]

            lds = [ldact(0, 0)]
            st_yo = 0
            consW2 = {0: 0, 1: 0}
            for w in range(NCH):
                if w + 1 < NCH:
                    p.wait_compute("sy")
                    lds.append(ldact(w + 1, (w + 1) % 2))
                p.wait("te", "d_sy", lds[w])
                for oc in range(NCH):
                    par = oc % 2
                    PP = ps2[par]
                    p.wait("te", "ve", consW2[par])
                    for j in range(NJ):
                        p.op("te", "matmul", PP[:, 0:WIN],
                             w2[0:CHJ[j], j * D + oc * 128:j * D + oc * 128 + 128],
                             actt[w % 2][0:CHJ[j], j * WIN:(j + 1) * WIN],
                             start=(j == 0), stop=(j == NJ - 1), inc=(j == NJ - 1))
                    p.wait("ve", "te", p.cnt["te"])
                    p.wait("ve", "d_gp", st_yo)
                    p.op("ve", "tensor_scalar", yo[:, oc * WIN:(oc + 1) * WIN],
                         PP[:, 0:WIN], b2[:, oc:oc + 1], None, ALU.add)
                    p.op("ve", "tensor_tensor", yo[:, oc * WIN:(oc + 1) * WIN],
                         yo[:, oc * WIN:(oc + 1) * WIN],
                         ywr[w % 2][:, oc * WIN:(oc + 1) * WIN], ALU.add)
                    consW2[par] = p.cnt["ve"]
                p.wait("gp", "ve", p.cnt["ve"])
                st_yo = store_fm_tile("gp", out_name, yo, lambda c: c * WIN, w * WIN, WIN)
            p.barrier()

    if BU >= 8:
        ffn_layer(0, "xn1h", "yBh")
    if BU >= 9:
        ffn_layer(1, "yBh", "y2h")

    w0nT = sbuf("w0nT", 128, NCH * D, BF16)
    w1nT = sbuf("w1nT", 128, NCH * D, BF16)
    if BU >= 10:
      with contextlib.ExitStack() as ph:
        wtf = ph.enter_context(nc.sbuf_tensor("p9wt", [128, D], F32))
        gtf = ph.enter_context(nc.sbuf_tensor("p9gt", [128, D], F32))
        grow = ph.enter_context(nc.sbuf_tensor("p9grow", [2, 1], F32))
        grow2 = ph.enter_context(nc.sbuf_tensor("p9grow2", [1, 2], F32))
        bgs = ph.enter_context(nc.sbuf_tensor("p9bgs", [2, 1], F32))
        amul = ph.enter_context(nc.sbuf_tensor("p9amul", [128, 2], F32))
        ps9 = ph.enter_context(nc.psum_tensor("p9ps", [128, 2], F32))
        p.wait("gp", "cc", 1)
        p.wait("sy", "cc", 1)
        p.dma("gp", grow[:, :], hb["cc_out"][2048:2050, 0:1], allow_slow_non_contiguous=True)
        p.dma("gp", bgs[:, :], hb["l1_bg"][:, :])
        p.wait("se", "d_gp", p.cnt["d_gp"])
        p.op("se", "activation", grow[:, :], grow[:, :], AF.Copy,
             scale=cpool[0:2, C_MEAN:C_MEAN + 1])
        p.wait("ve", "se", p.cnt["se"])
        p.op("ve", "tensor_tensor", grow[:, :], grow[:, :], bgs[:, :], ALU.add)
        p.wait("se", "ve", p.cnt["ve"])
        p.op("se", "activation", grow[:, :], grow[:, :], AF.Sigmoid)
        p.wait("te", "se", p.cnt["se"])
        p.op("te", "transpose", ps9[0:1, 0:2], grow[0:2, 0:1], ident[0:2, 0:2], inc=True)
        p.wait("ve", "te", p.cnt["te"])
        p.op("ve", "tensor_copy", grow2[:, :], ps9[0:1, 0:2])
        p.wait("te", "ve", p.cnt["ve"])
        p.op("te", "matmul", ps9[:, :], ones1f[0:1, :], grow2[0:1, :],
             start=True, stop=True, inc=True)
        p.wait("ve", "te", p.cnt["te"])
        p.op("ve", "tensor_scalar", amul[:, 0:1], ps9[:, 0:1], -1.0, 1.0,
             ALU.mult, ALU.add)
        p.op("ve", "tensor_copy", amul[:, 1:2], ps9[:, 1:2])
        p.vsync()
        for (g_row0, wt_name, dst, sc) in ((0, "l1_w0T", w0nT, None),
                                           (1024, "l1_w1T", w1nT, SCALE2S)):
            for c in range(NCH):
                p.dma("sy", wtf[:, :], hb[wt_name][c * 128:(c + 1) * 128, :])
                ldg = p.dma("sy", gtf[:, :],
                            hb["cc_out"][g_row0 + c * 128:g_row0 + (c + 1) * 128, :])
                p.wait("ve", "d_sy", ldg)
                if sc is not None:
                    p.op("ve", "tensor_scalar", gtf[:, :], gtf[:, :], sc, None, ALU.mult)
                p.op("ve", "tensor_scalar", gtf[:, :], gtf[:, :], -1.0, 1.0,
                     ALU.max, ALU.min)
                p.op("ve", "tensor_scalar", gtf[:, :], gtf[:, :], amul[:, 1:2], None,
                     ALU.mult)
                p.op("ve", "tensor_scalar", wtf[:, :], wtf[:, :], amul[:, 0:1], None,
                     ALU.mult)
                p.op("ve", "tensor_tensor", dst[:, c * D:(c + 1) * D], wtf[:, :],
                     gtf[:, :], ALU.subtract)
                p.wait("sy", "ve", p.cnt["ve"])
        if "w0n" in debug_want:
            p.wait_compute("gp")
            for c in range(NCH):
                p.dma("gp", tap("w0n", D, D, BF16)[c * 128:(c + 1) * 128, :],
                      w0nT[:, c * D:(c + 1) * D])
                p.dma("gp", tap("w1n", D, D, BF16)[c * 128:(c + 1) * 128, :],
                      w1nT[:, c * D:(c + 1) * D])
            p.dma("gp", tap("amul", 128, 2)[:, :], amul[:, :])
        p.barrier()

    if BU >= 11:
      with contextlib.ExitStack() as ph:
        wqT = ph.enter_context(nc.sbuf_tensor("pTwq", [128, NCH * D], BF16))
        woT = ph.enter_context(nc.sbuf_tensor("pTwo", [128, NCH * D], BF16))
        xnt = [ph.enter_context(nc.sbuf_tensor(f"pTx{i}", [128, NCH * TT], BF16))
               for i in range(2)]
        y2t = [ph.enter_context(nc.sbuf_tensor(f"pTy{i}", [128, NCH * TT], F32))
               for i in range(2)]
        x1t = [ph.enter_context(nc.sbuf_tensor(f"pTx1{i}", [128, NCH * TT], F32))
               for i in range(2)]
        qt = ph.enter_context(nc.sbuf_tensor("pTq", [128, NCH * TT], BF16))
        t1t = ph.enter_context(nc.sbuf_tensor("pTt1", [128, NCH * TT], BF16))
        t2t = ph.enter_context(nc.sbuf_tensor("pTt2", [128, NCH * TT], BF16))
        gt = ph.enter_context(nc.sbuf_tensor("pTg", [128, TT], F32))
        ot = ph.enter_context(nc.sbuf_tensor("pTo", [128, NCH * TT], F32))
        psX = [ph.enter_context(nc.psum_tensor(f"pTps{i}", [128, TT], F32))
               for i in range(2)]
        load_w("gp", wqT, "l1_wqT", D)
        load_w("gp", woT, "l1_woutT", D)
        p.wait("te", "d_gp", p.cnt["d_gp"])

        def ldtile(t, buf):
            load_fm_tile("sy", xnt[buf], lambda c: c * TT, "xn1h", t * TT, TT)
            load_fm_tile("sy", y2t[buf], lambda c: c * TT, "y2h", t * TT, TT)
            load_fm_tile("sy", x1t[buf], lambda c: c * TT, "x1h", t * TT, TT)
            return p.cnt["d_sy"]

        lds = [ldtile(0, 0)]
        st_o = 0
        consX = [(0, 0), (0, 0)]
        for t in range(NTILE):
            if t + 1 < NTILE:
                p.wait_compute("sy")
                lds.append(ldtile(t + 1, (t + 1) % 2))
            xn, y2, x1 = xnt[t % 2], y2t[t % 2], x1t[t % 2]
            p.wait("te", "d_sy", lds[t])

            def proj(wsb, rhs, dst, fn):
                cons = {0: consX[0], 1: consX[1]}
                p.wait("te", "se", p.cnt["se"])
                for oc in range(NCH):
                    par = oc % 2
                    PP = psX[par]
                    p.wait("te", "se", cons[par][0])
                    p.wait("te", "ve", cons[par][1])
                    for dc in range(NCH):
                        p.op("te", "matmul", PP[:, 0:TT],
                             wsb[:, dc * D + oc * 128:dc * D + oc * 128 + 128],
                             rhs[:, dc * TT:(dc + 1) * TT],
                             start=(dc == 0), stop=(dc == NCH - 1), inc=(dc == NCH - 1))
                    p.wait("se", "te", p.cnt["te"])
                    p.op("se", "activation", dst[:, oc * TT:(oc + 1) * TT], PP[:, 0:TT],
                         fn)
                    cons[par] = (p.cnt["se"], cons[par][1])
                consX[0], consX[1] = cons[0], cons[1]
            proj(wqT, xn, qt, AF.Copy)
            proj(w0nT, qt, t1t, AF.Silu)
            proj(w1nT, t1t, t2t, AF.Copy)
            consF = dict(enumerate([consX[0], consX[1]]))
            p.wait("te", "se", p.cnt["se"])
            for oc in range(NCH):
                par = oc % 2
                PP = psX[par]
                p.wait("te", "se", consF[par][0])
                p.wait("te", "ve", consF[par][1])
                for dc in range(NCH):
                    p.op("te", "matmul", PP[:, 0:TT],
                         woT[:, dc * D + oc * 128:dc * D + oc * 128 + 128],
                         t2t[:, dc * TT:(dc + 1) * TT],
                         start=(dc == 0), stop=(dc == NCH - 1), inc=(dc == NCH - 1))
                p.wait("se", "d_sy", lds[t])
                p.wait("se", "ve", p.cnt["ve"])
                p.op("se", "activation", gt[:, 0:TT], y2[:, oc * TT:(oc + 1) * TT],
                     AF.Silu)
                p.wait("ve", "te", p.cnt["te"])
                p.wait("ve", "se", p.cnt["se"])
                p.wait("ve", "d_gp", st_o)
                p.op("ve", "tensor_tensor", ot[:, oc * TT:(oc + 1) * TT], PP[:, 0:TT],
                     gt[:, 0:TT], ALU.mult)
                p.op("ve", "tensor_tensor", ot[:, oc * TT:(oc + 1) * TT],
                     ot[:, oc * TT:(oc + 1) * TT], x1[:, oc * TT:(oc + 1) * TT], ALU.add)
                consF[par] = (p.cnt["se"], p.cnt["ve"])
            consX[0], consX[1] = consF[0], consF[1]
            p.wait("gp", "ve", p.cnt["ve"])
            for c in range(NCH):
                p.dma("gp", out_fm[c * 128:(c + 1) * 128, t * TT:(t + 1) * TT],
                      ot[:, c * TT:(c + 1) * TT])
            if "qt" in debug_want:
                p.wait_compute("gp")
                for c in range(NCH):
                    p.dma("gp", tap("qt", D, S, BF16)[c * 128:(c + 1) * 128, t * TT:(t + 1) * TT],
                          qt[:, c * TT:(c + 1) * TT])
                    p.dma("gp", tap("t1t", D, S, BF16)[c * 128:(c + 1) * 128, t * TT:(t + 1) * TT],
                          t1t[:, c * TT:(c + 1) * TT])
                    p.dma("gp", tap("t2t", D, S, BF16)[c * 128:(c + 1) * 128, t * TT:(t + 1) * TT],
                          t2t[:, c * TT:(c + 1) * TT])
            st_o = p.cnt["d_gp"]
        p.wait("gp", "d_gp", p.cnt["d_gp"])
        p.wait("sy", "d_gp", p.cnt["d_gp"])

    scr_shapes = {"x1h": (D, S, F32), "xn1h": (D, S, BF16), "kfmh": (D, S, BF16),
                  "vfmh": (D, S, BF16), "a1tmh": (S, D, BF16), "d2tmh": (S, D, BF16),
                  "d1tmh": (S, D, BF16), "ktmh": (S, D, BF16), "yAh": (D, S, F32),
                  "yBh": (D, S, F32), "y2h": (D, S, F32), "cc_out": (2050, D, F32),
                  "tabC": (128, S, F32), "tabS": (128, S, F32),
                  "tabCi": (128, S, F32), "tabSi": (128, S, F32),
                  "acth": (NJ * 128, S, BF16)}
    p.barrier()
    if "cc_out" in debug_want:
        p.wait("gp", "cc", p.cnt["cc"])
    for nm in debug_want:
        if nm in scr_shapes:
            r_, c_, dt_ = scr_shapes[nm]
            t = tap(nm, r_, c_, dt_)
            for rr in range(0, r_, 1024):
                p.dma("gp", t[rr:min(rr + 1024, r_), :],
                      hb[nm][rr:min(rr + 1024, r_), :])
    p.wait("gp", "d_gp", p.cnt["d_gp"])

    with nc.Block() as blk:
        p.finalize(blk)
    es.close()
    return nc


def kernel(**inputs):
    hp = _host_prep(inputs)
    x = np.asarray(inputs["L0_attn_query"], dtype=np.float32)
    nc = build_nc(tuple(DEBUG_WANT))
    in_maps = []
    for b in range(NCORE):
        m = {"x_fm": np.ascontiguousarray(x[b].T)}
        m.update(hp)
        in_maps.append(m)
    res = run_bass_kernel_spmd(nc, in_maps, core_ids=list(range(NCORE)))
    out = np.stack([np.asarray(r["out_fm"]).T for r in res.results], axis=0)
    if DEBUG_WANT:
        kernel.last_taps = [{k: np.asarray(v) for k, v in r.items() if k.startswith("tap_")}
                            for r in res.results]
    return out.astype(np.float32)
